# version 04
# baseline (speedup 1.0000x reference)
import contextlib
import numpy as np
import ml_dtypes
import concourse.bass as bass
import concourse.mybir as mybir
from concourse.alu_op_type import AluOpType as Op
from concourse.bass_utils import run_bass_kernel_spmd
from concourse import library_config

F32 = mybir.dt.float32
BF16 = mybir.dt.bfloat16
FP8 = mybir.dt.float8e4
AF = mybir.ActivationFunctionType
AX = mybir.AxisListType

NEG = np.float32(-3.0e38)
EPS = 1e-5


def default_cfg():
    return dict(B=4, T=4096, D=2048, H=16, I=8192, n_cores=8)


def derived(cfg):
    d = dict(cfg)
    T, D, I = cfg["T"], cfg["D"], cfg["I"]
    d["DC"] = D // 128
    d["IC"] = I // 128
    d["OT"] = T // 2
    d["NT"] = d["OT"] // 128
    d["TT"] = T // 128
    d["NG"] = max(d["NT"] // 2, 1)
    d["KB"] = min(512, T)
    d["WBLK"] = min(512, D)
    d["DBLK"] = D // d["WBLK"]
    d["WBI"] = min(512, I)
    d["OBI"] = I // d["WBI"]
    d["AIT"] = min(1024, T)
    d["AIT2"] = min(1024, d["OT"])
    d["FMT"] = 128
    d["NM"] = d["OT"] // d["FMT"]
    d["CPB"] = cfg["n_cores"] // cfg["B"]
    return d


def kb_of_group(g):
    return g + 1



def _hadamard(n):
    h = np.array([[1.0]], dtype=np.float64)
    while h.shape[0] < n:
        h = np.block([[h, h], [h, -h]])
    return h


def _rope_tables(positions, scale):
    DH = 128
    j = np.arange(DH // 2, dtype=np.float64)
    theta = 10000.0 ** (-2.0 * j / DH)
    ang = positions.astype(np.float64)[:, None] * theta[None, :]
    out = []
    for t in (np.cos(ang) * scale, np.sin(ang) * scale):
        hi = t.astype(np.float32)
        lo = (t - hi).astype(np.float32)
        out += [hi, lo]
    return out


def _tile_major(a):
    T = a.shape[0]
    return np.ascontiguousarray(
        a.reshape(T // 128, 128, 64).transpose(1, 0, 2).reshape(128, -1))


def _deint_perm(D):
    DH = 128
    perm = []
    for h0 in range(0, D, DH):
        perm += [h0 + 2 * j for j in range(DH // 2)]
        perm += [h0 + 2 * j + 1 for j in range(DH // 2)]
    return np.array(perm)


def host_core_inputs(cfg, inputs, core, names=None):
    cd = derived(cfg)
    T, D = cfg["T"], cfg["D"]
    b, p = core // cd["CPB"], core % cd["CPB"]
    nm = names or {}
    X = np.asarray(inputs[nm.get("X", "L0_attn_X")])
    Wq = np.asarray(inputs[nm.get("Wq", "L0_attn_Wq")])
    Wk = np.asarray(inputs[nm.get("Wk", "L0_attn_Wk")])
    Wv = np.asarray(inputs[nm.get("Wv", "L0_attn_Wv")])
    Wo = np.asarray(inputs[nm.get("Wo", "L0_attn_Wo")])
    Wg = np.asarray(inputs[nm.get("Wg", "L0_ffn_w_gate")])
    Wu = np.asarray(inputs[nm.get("Wu", "L0_ffn_w_up")])
    Wd = np.asarray(inputs[nm.get("Wd", "L0_ffn_w_down")])
    own_rows = np.concatenate(
        [np.arange((2 * jt + p) * 128, (2 * jt + p + 1) * 128)
         for jt in range(cd["NT"])])
    perm = _deint_perm(D)

    m = {}
    m["xf"] = np.ascontiguousarray(X[b], dtype=np.float32)
    m["xo"] = np.ascontiguousarray(X[b][own_rows], dtype=np.float32)
    m["wqT"] = np.ascontiguousarray(Wq[perm].T, dtype=np.float32)
    m["wkT"] = np.ascontiguousarray(Wk[perm].T, dtype=np.float32)
    m["wvT"] = np.ascontiguousarray(Wv.T, dtype=np.float32)
    m["woT"] = np.ascontiguousarray(Wo.T, dtype=np.float32)
    m["wgT"] = np.ascontiguousarray(Wg.T, dtype=np.float32)
    m["wuT"] = np.ascontiguousarray(Wu.T, dtype=np.float32)
    m["wdT"] = np.ascontiguousarray(Wd.T, dtype=np.float32)

    kt = _rope_tables(np.arange(T), 1.0)
    m["tabk"] = np.stack([_tile_major(x) for x in kt], 0)
    qt = _rope_tables(own_rows, 128.0 ** -0.5)
    m["tabq"] = np.stack([_tile_major(x) for x in qt], 0)

    masks = np.zeros((cd["NT"], 128, cd["KB"]), dtype=np.float32)
    for jt in range(cd["NT"]):
        gq, g = 2 * jt + p, jt // 2
        k0 = (kb_of_group(g) - 1) * cd["KB"]
        i = np.arange(128)[:, None]
        jj = np.arange(cd["KB"])[None, :]
        masks[jt] = np.where(k0 + jj > gq * 128 + i, NEG, 0.0)
    m["masks"] = masks.astype(ml_dtypes.bfloat16)

    m["hmat"] = _hadamard(128).astype(ml_dtypes.bfloat16)
    m["id32"] = np.eye(128, dtype=np.float32)
    m["idbf"] = np.eye(128).astype(ml_dtypes.bfloat16)
    return m


def host_input_specs(cfg):
    cd = derived(cfg)
    T, D, I = cfg["T"], cfg["D"], cfg["I"]
    OT, NT, TT, KB = cd["OT"], cd["NT"], cd["TT"], cd["KB"]
    return [
        ("xf", [T, D], F32), ("xo", [OT, D], F32),
        ("wqT", [D, D], F32), ("wkT", [D, D], F32),
        ("wvT", [D, D], F32), ("woT", [D, D], F32),
        ("wgT", [D, I], F32), ("wuT", [D, I], F32), ("wdT", [I, D], F32),
        ("tabk", [4, 128, TT * 64], F32), ("tabq", [4, 128, NT * 64], F32),
        ("masks", [NT, 128, KB], BF16),
        ("hmat", [128, 128], BF16), ("id32", [128, 128], F32),
        ("idbf", [128, 128], BF16),
    ]



class Sm:
    __slots__ = ("h", "c")

    def __init__(self, h):
        self.h = h
        self.c = 0


ENGS = ("sync", "tensor", "vector", "scalar", "gpsimd")


class Ctx:
    def balloc(self, name, shape, dt):
        if name not in self.stage_cache:
            self.stage_cache[name] = self.stage_stack.enter_context(
                self.nc.sbuf_tensor("s%d_%s" % (self.stage_id, name), shape, dt))
        return self.stage_cache[name]

    def balloc2(self, name, shape, dt):
        if name not in self.stage_cache:
            self.stage_cache[name] = DBuf(self.stage_stack.enter_context(
                self.nc.sbuf_tensor("s%d_%s" % (self.stage_id, name), shape, dt)))
        return self.stage_cache[name]


def run_stage(ctx, fn):
    snap = [(s, s.c) for s in ctx.all_sems]
    ctx.stage_cache = {}
    ctx.stage_id = getattr(ctx, "stage_id", -1) + 1
    ctx.stage_stack = contextlib.ExitStack()
    with ctx.nc.Block() as blk:
        for eng in ENGS:
            for s, v in snap:
                s.c = v
            getattr(blk, eng)(_section(fn, ctx, eng))
    ctx.stage_stack.close()
    ctx.stage_cache = None


def _section(fn, ctx, eng):
    def sec(e):
        fn(ctx, e, eng)
    return sec


def mk_EW(e, eng, sems_all=None, own=None):
    last = [0]

    def E(tgt, make, *incs):
        if eng == tgt:
            if own is not None and own.c > last[0]:
                e.wait_ge(own.h, own.c)
                last[0] = own.c
            r = make()
            for (s, n) in incs:
                r = r.then_inc(s.h, n)
        for (s, n) in incs:
            s.c += n

    def W(tgt, s, v=None):
        if eng == tgt:
            e.wait_ge(s.h, s.c if v is None else v)

    def B():
        for s in (sems_all or ()):
            if s.c:
                e.wait_ge(s.h, s.c)
    return E, W, B



class DBuf:

    def __init__(self, t):
        self.t = t

    def __getitem__(self, sl):
        return _Slot(self.t, sl)


class _Slot:
    def __init__(self, t, sl):
        self.t = t
        self.sl = sl

    def __getitem__(self, idx):
        p, f = idx
        return self.t[p, self.sl, f]


def _bsel(apat, stp, which):
    nch = apat.shape[1]
    ng = nch // (2 * stp)
    r = apat.rearrange("p (g t c) j -> p g t c j", g=ng, t=2, c=stp)
    return r[:, :, which, :, :]



def build_nc(cfg, stages=('0', 'A1', 'A2', 'B', 'C', 'D')):
    cd = derived(cfg)
    T, D, H, I = cfg["T"], cfg["D"], cfg["H"], cfg["I"]
    DC, IC, OT, NT, TT, NG, KB = (cd["DC"], cd["IC"], cd["OT"], cd["NT"],
                                  cd["TT"], cd["NG"], cd["KB"])
    WBLK, DBLK, WBI, OBI = cd["WBLK"], cd["DBLK"], cd["WBI"], cd["OBI"]
    AIT, AIT2, FMT, NM = cd["AIT"], cd["AIT2"], cd["FMT"], cd["NM"]
    HPB = WBLK // 128
    C1 = float(1.0 / np.sqrt(D))
    C2 = float(1.0 / np.sqrt(I))
    L1 = int(np.log2(DC))
    L2 = int(np.log2(IC))

    nc = bass.Bass("TRN2", target_bir_lowering=False)
    ctx = Ctx()
    ctx.nc = nc
    stk = contextlib.ExitStack()

    ap = {}
    for name, shape, dt in host_input_specs(cfg):
        ap[name] = nc.dram_tensor(name, shape, dt, kind="ExternalInput").ap()
    ap["out"] = nc.dram_tensor("out", [OT, D], F32, kind="ExternalOutput").ap()
    for name, shape, dt in [
            ("kscr_h", [H, 128, T], BF16), ("kscr_l", [H, 128, T], BF16),
            ("vscr", [T, D], F32),
            ("qscr_h", [H, 128, OT], BF16), ("qscr_l", [H, 128, OT], BF16),
            ("xnscr_h", [DC, 128, OT], BF16), ("xnscr_l", [DC, 128, OT], BF16),
            ("wt_h", [3, DC, 128, D], BF16), ("wt_l", [3, DC, 128, D], BF16),
            ("wot_h", [DC, 128, D], BF16), ("wot_l", [DC, 128, D], BF16),
            ("wscr_g", [DC, 128, I], FP8), ("wscr_u", [DC, 128, I], FP8),
            ("wscr_d", [IC, 128, D], FP8),
            ("hscr_h", [H, 128, OT], BF16), ("hscr_l", [H, 128, OT], BF16),
            ("oscr", [OT, D], F32)]:
        ap[name] = nc.dram_tensor(name, shape, dt, kind="Internal").ap()

    sems = {}
    for name in ("sT", "sV", "sA", "sP",
                 "dC", "dT", "dT2", "dW0", "dW1", "dX0", "dX1",
                 "dK0", "dK1", "dQ", "dD",
                 "gW0", "gW1", "gK0", "gK1", "gV0", "gV1", "gX",
                 "gH0", "gH1", "gO0", "gO1", "gU0", "gU1"):
        sems[name] = Sm(stk.enter_context(nc.semaphore(name)))
    ctx.all_sems = list(sems.values())
    ctx.own_map = {"tensor": sems["sT"], "vector": sems["sV"],
                   "scalar": sems["sA"], "gpsimd": sems["sP"], "sync": None}
    sT, sV, sA, sP = (sems[k] for k in ("sT", "sV", "sA", "sP"))
    dC, dT, dT2, dQ, dD, gX = (sems[k] for k in ("dC", "dT", "dT2", "dQ", "dD", "gX"))
    dW = [sems["dW0"], sems["dW1"]]
    dX = [sems["dX0"], sems["dX1"]]
    dK = [sems["dK0"], sems["dK1"]]
    gW = [sems["gW0"], sems["gW1"]]
    gK = [sems["gK0"], sems["gK1"]]
    gV = [sems["gV0"], sems["gV1"]]
    gH = [sems["gH0"], sems["gH1"]]
    gO = [sems["gO0"], sems["gO1"]]
    gU = [sems["gU0"], sems["gU1"]]

    g_ = {}
    for name, shape, dt in [
            ("hmat", [128, 128], BF16), ("id32", [128, 128], F32),
            ("idbf", [128, 128], BF16), ("cbias", [128, NT * H], F32),
            ("linv", [128, NT * H], F32), ("qn2", [128, NT * H], F32),
            ("kn2", [128, H * TT], F32), ("kmax", [128, H], F32),
            ("wsg", [128, 1], F32), ("wsu", [128, 1], F32),
            ("wsd", [128, 1], F32), ("kc0", [128, 2], F32), ("ones1", [1, 128], F32)]:
        g_[name] = stk.enter_context(nc.sbuf_tensor("g_" + name, shape, dt))

    ps = stk.enter_context(nc.psum_tensor("ps", [128, 7, 512], F32))
    psb = stk.enter_context(nc.psum_tensor("psb", [128, 1024], BF16))

    def s0(ctx, e, eng):
        E, W, B = mk_EW(e, eng, ctx.all_sems, ctx.own_map[eng])
        B()
        for n in ("hmat", "id32", "idbf"):
            E("sync", lambda n=n: e.dma_start(out=g_[n][:, :], in_=ap[n][:, :]),
              (dC, 16))
        E("vector", lambda: e.memset(g_["ones1"][0:1, :], 1.0), (sV, 1))
        E("vector", lambda: e.memset(g_["kc0"][:, 0:1], 0.0), (sV, 1))
        E("vector", lambda: e.memset(g_["kc0"][:, 1:2], float(EPS)), (sV, 1))
        if True:
            win = ctx.balloc2("w0_in", [128, 2, max(I, D)], F32)
            whi = ctx.balloc2("w0_hi", [128, 2, D], BF16)
            wlo = ctx.balloc2("w0_lo", [128, 2, D], BF16)
            wt8 = ctx.balloc2("w0_t8", [128, 2, max(I, D)], FP8)
            wle = ctx.balloc("w0_le", [128, max(I, D)], F32)
            wms = ctx.balloc("w0_ms", [128, max(DC, IC)], F32)
            wsc = ctx.balloc("w0_sc", [128, 8], F32)
            whn = ctx.balloc("w0_hn", [128, 2], F32)
            srcs = [("wqT", "wt_h", "wt_l", 0), ("wkT", "wt_h", "wt_l", 1),
                    ("wvT", "wt_h", "wt_l", 2), ("woT", "wot_h", "wot_l", None)]
            winck = {0: 0, 1: 0}
            for wi, (src, dh_, dl_, idx) in enumerate(srcs):
                for rc in range(DC):
                    it = wi * DC + rc
                    sl = it % 2
                    W("sync", sV, winck[sl])
                    E("sync", lambda src=src, rc=rc, sl=sl: e.dma_start(
                        out=win[sl][:, 0:D],
                        in_=ap[src][rc * 128:(rc + 1) * 128, :]), (dW[sl], 16))
                    W("vector", dW[sl])
                    W("vector", gW[sl])
                    E("vector", lambda sl=sl: e.tensor_copy(
                        whi[sl][:, :], win[sl][:, 0:D]), (sV, 1))
                    W("vector", sV)
                    E("vector", lambda sl=sl: e.tensor_sub(
                        wlo[sl][:, :], win[sl][:, 0:D], whi[sl][:, :]), (sV, 1))
                    winck[sl] = sV.c
                    W("gpsimd", sV)
                    if idx is None:
                        dsth, dstl = ap[dh_][rc], ap[dl_][rc]
                    else:
                        dsth, dstl = ap[dh_][idx, rc], ap[dl_][idx, rc]
                    E("gpsimd", lambda d=dsth, sl=sl: e.dma_start(
                        out=d[:, :], in_=whi[sl][:, :]), (gW[sl], 16))
                    E("gpsimd", lambda d=dstl, sl=sl: e.dma_start(
                        out=d[:, :], in_=wlo[sl][:, :]), (gW[sl], 16))

            for (src, dst, wsn, NR, NW) in [
                    ("wgT", "wscr_g", "wsg", DC, I),
                    ("wuT", "wscr_u", "wsu", DC, I),
                    ("wdT", "wscr_d", "wsd", IC, D)]:
                for rc in range(NR):
                    sl = rc % 2
                    W("sync", sV, winck[sl])
                    E("sync", lambda src=src, rc=rc, sl=sl, NW=NW: e.dma_start(
                        out=win[sl][:, 0:NW],
                        in_=ap[src][rc * 128:(rc + 1) * 128, :]), (dW[sl], 16))
                    W("vector", dW[sl])
                    E("vector", lambda rc=rc, sl=sl, NW=NW: e.tensor_reduce(
                        wms[:, rc:rc + 1], win[sl][:, 0:NW], axis=AX.X,
                        op=Op.add, apply_absolute_value=True), (sV, 1))
                    winck[sl] = sV.c
                W("vector", sV)
                E("vector", lambda NR=NR: e.tensor_reduce(
                    wsc[:, 0:1], wms[:, 0:NR], axis=AX.X, op=Op.add), (sV, 1))
                W("tensor", sV)
                W("tensor", dC)
                E("tensor", lambda: e.transpose(
                    ps[0:1, 6, 0:128], wsc[:, 0:1], g_["id32"][:, :]), (sT, 1))
                W("vector", sT)
                E("vector", lambda: e.tensor_reduce(
                    wsc[0:1, 1:2], ps[0:1, 6, 0:128], axis=AX.X, op=Op.add), (sV, 1))
                W("vector", sV)
                E("vector", lambda NR=NR, NW=NW: e.tensor_scalar(
                    wsc[0:1, 2:3], wsc[0:1, 1:2], 1.0 / (NR * 128 * NW), EPS,
                    op0=Op.mult, op1=Op.max), (sV, 1))
                W("tensor", sV)
                E("tensor", lambda: e.matmul(
                    ps[:, 6, 0:1], g_["ones1"][0:1, :], wsc[0:1, 2:3],
                    start=True, stop=True), (sT, 1))
                W("scalar", sT)
                E("scalar", lambda wsn=wsn: e.activation(
                    g_[wsn][:, 0:1], ps[:, 6, 0:1], AF.Copy, bias=0.0,
                    scale=1.0), (sA, 1))
                W("vector", sA)
                E("vector", lambda wsn=wsn: e.tensor_scalar(
                    whn[:, 0:1], g_[wsn][:, 0:1], 0.5, None, op0=Op.mult), (sV, 1))
                W("vector", sV)
                E("vector", lambda: e.tensor_scalar(
                    whn[:, 1:2], whn[:, 0:1], -1.0, None, op0=Op.mult), (sV, 1))
                for rc in range(NR):
                    sl = rc % 2
                    W("sync", sV, winck[sl])
                    E("sync", lambda src=src, rc=rc, sl=sl, NW=NW: e.dma_start(
                        out=win[sl][:, 0:NW],
                        in_=ap[src][rc * 128:(rc + 1) * 128, :]), (dW[sl], 16))
                    W("vector", dW[sl])
                    W("vector", sV)
                    E("vector", lambda sl=sl, NW=NW: e.tensor_scalar(
                        wle[:, 0:NW], win[sl][:, 0:NW], whn[:, 1:2], None,
                        op0=Op.is_le), (sV, 1))
                    W("vector", sV)
                    W("vector", gW[sl])
                    E("vector", lambda sl=sl, NW=NW: e.scalar_tensor_tensor(
                        wt8[sl][:, 0:NW], win[sl][:, 0:NW], whn[:, 0:1],
                        wle[:, 0:NW], op0=Op.is_ge, op1=Op.subtract), (sV, 1))
                    winck[sl] = sV.c
                    W("gpsimd", sV)
                    E("gpsimd", lambda dst=dst, rc=rc, sl=sl, NW=NW: e.dma_start(
                        out=ap[dst][rc], in_=wt8[sl][:, 0:NW]), (gW[sl], 16))

    if '0' in stages:
        run_stage(ctx, s0)

    def make_sA(own):
        xsrc = "xo" if own else "xf"
        tab = "tabq" if own else "tabk"
        ait = AIT2 if own else AIT
        niter = (OT if own else T) // ait
        tpi = ait // 128

        def sA_(ctx, e, eng):
            E, W, B = mk_EW(e, eng, ctx.all_sems, ctx.own_map[eng])
            B()
            W("tensor", dC)
            if True:
                ax = ctx.balloc2("a_x", [128, 2, D], F32)
                axn = ctx.balloc("a_xn", [128, D], F32)
                ast = ctx.balloc("a_st", [128, 8], F32)
                axh = ctx.balloc("a_xh", [128, DC * ait], BF16)
                axl = ctx.balloc("a_xl", [128, DC * ait], BF16)
                awh = ctx.balloc2("a_wh", [128, 2, DC * WBLK], BF16)
                awl = ctx.balloc2("a_wl", [128, 2, DC * WBLK], BF16)
                act_ = ctx.balloc("a_ct", [128, tpi * 64], F32)
                astb = ctx.balloc("a_stb", [128, tpi * 64], F32)
                ajnk = ctx.balloc("a_jnk", [128, D], F32)
                aro = ctx.balloc("a_ro", [128, 128], F32)
                arh = ctx.balloc("a_rh", [128, 128], BF16)
                arl = ctx.balloc("a_rl", [128, 128], BF16)
                ake = ctx.balloc2("a_ke", [128, 2, 256], BF16)
                ave = ctx.balloc2("a_ve", [128, 2, WBLK], F32)
                anrm = ctx.balloc("a_nrm", [128, tpi * H], F32)
                wtick = {0: 0, 1: 0}
                ketick = {0: 0, 1: 0}
                vetick = {0: 0, 1: 0}
                abct = {"ke": 0, "ve": 0}

                def rope_head(bank, tc, gt, hglob):
                    x0 = ps[:, bank, (hglob % HPB) * 128:(hglob % HPB) * 128 + 64]
                    x1 = ps[:, bank, (hglob % HPB) * 128 + 64:(hglob % HPB) * 128 + 128]
                    ct = act_[:, tc * 64:(tc + 1) * 64]
                    st_ = astb[:, tc * 64:(tc + 1) * 64]
                    W("vector", sT)
                    E("vector", lambda: e.tensor_mul(ajnk[:, 0:64], x0, ct), (sV, 1))
                    E("vector", lambda: e.tensor_mul(ajnk[:, 64:128], x1, st_), (sV, 1))
                    E("vector", lambda: e.tensor_mul(ajnk[:, 128:192], x0, st_), (sV, 1))
                    E("vector", lambda: e.tensor_mul(ajnk[:, 192:256], x1, ct), (sV, 1))
                    W("vector", sV)
                    E("vector", lambda: e.tensor_sub(
                        aro[:, 0:64], ajnk[:, 0:64], ajnk[:, 64:128]), (sV, 1))
                    E("vector", lambda: e.tensor_add(
                        aro[:, 64:128], ajnk[:, 128:192], ajnk[:, 192:256]), (sV, 1))
                    W("vector", sV)
                    if own:
                        ncol = g_["qn2"][:, gt * H + hglob: gt * H + hglob + 1]
                    else:
                        ncol = g_["kn2"][:, hglob * TT + gt: hglob * TT + gt + 1]
                    E("vector", lambda ncol=ncol: e.scalar_tensor_tensor(
                        ajnk[:, 0:128], aro[:, :], 1.0, aro[:, :],
                        op0=Op.mult, op1=Op.mult, accum_out=ncol), (sV, 1))
                    E("vector", lambda: e.tensor_copy(arh[:, :], aro[:, :]), (sV, 1))
                    W("vector", sV)
                    E("vector", lambda: e.tensor_sub(arl[:, :], aro[:, :], arh[:, :]),
                      (sV, 1))
                    W("tensor", sV)
                    W("tensor", sA)
                    E("tensor", lambda: e.transpose(
                        psb[:, 0:128], arh[:, :], g_["idbf"][:, :]))
                    E("tensor", lambda: e.transpose(
                        psb[:, 128:256], arl[:, :], g_["idbf"][:, :]), (sT, 1))
                    kesl = abct["ke"] % 2
                    abct["ke"] += 1
                    W("scalar", sT)
                    W("scalar", gK[kesl])
                    E("scalar", lambda: e.activation(
                        ake[kesl][:, :], psb[:, 0:256], AF.Copy, bias=0.0,
                        scale=1.0), (sA, 1))
                    W("gpsimd", sA)
                    kdst = "qscr" if own else "kscr"
                    E("gpsimd", lambda: e.dma_start(
                        out=ap[kdst + "_h"][hglob][:, gt * 128:(gt + 1) * 128],
                        in_=ake[kesl][:, 0:128]), (gK[kesl], 16))
                    E("gpsimd", lambda: e.dma_start(
                        out=ap[kdst + "_l"][hglob][:, gt * 128:(gt + 1) * 128],
                        in_=ake[kesl][:, 128:256]), (gK[kesl], 16))

                def proj_tile(it, tc, blk, sl, isrope, xtick):
                    bank = 4 + (tc % 2)
                    gt = it * tpi + tc
                    W("tensor", dW[sl])
                    W("tensor", sV, xtick[tc])
                    W("tensor", sV)
                    W("tensor", sA)
                    k, n3 = 0, 3 * DC
                    for dc in range(DC):
                        c0 = dc * ait + tc * 128
                        xh_t = axh[:, c0:c0 + 128]
                        xl_t = axl[:, c0:c0 + 128]
                        wh_t = awh[sl][:, dc * WBLK:(dc + 1) * WBLK]
                        wl_t = awl[sl][:, dc * WBLK:(dc + 1) * WBLK]
                        for (sx, mv) in ((xh_t, wh_t), (xh_t, wl_t), (xl_t, wh_t)):
                            k += 1
                            E("tensor", lambda sx=sx, mv=mv, bank=bank, k=k, n3=n3:
                              e.matmul(ps[:, bank, 0:WBLK], sx, mv, start=(k == 1),
                                       stop=(k == n3)),
                                *([(sT, 1)] if k == n3 else []))
                    if isrope:
                        for hh in range(HPB):
                            rope_head(bank, tc, gt, blk * HPB + hh)
                    else:
                        vsl = abct["ve"] % 2
                        abct["ve"] += 1
                        W("scalar", sT)
                        W("scalar", gV[vsl])
                        E("scalar", lambda: e.activation(
                            ave[vsl][:, :], ps[:, bank, 0:WBLK], AF.Copy, bias=0.0,
                            scale=1.0), (sA, 1))
                        W("gpsimd", sA)
                        E("gpsimd", lambda: e.dma_start(
                            out=ap["vscr"][gt * 128:(gt + 1) * 128,
            blk * WBLK:(blk + 1) * WBLK],
                            in_=ave[vsl][:, :]), (gV[vsl], 16))

                for it in range(niter):
                    W("vector", sT)
                    W("vector", gX)
                    W("scalar", sT)
                    W("scalar", gX)
                    W("sync", sV)
                    E("sync", lambda it=it: e.dma_start(
                        out=act_[:, :],
                        in_=ap[tab][0][:, it * tpi * 64:(it + 1) * tpi * 64]),
                        (dT, 16))
                    E("sync", lambda it=it: e.dma_start(
                        out=ajnk[:, 0:tpi * 64],
                        in_=ap[tab][1][:, it * tpi * 64:(it + 1) * tpi * 64]),
                        (dT, 16))
                    W("vector", dT)
                    E("vector", lambda: e.tensor_add(
                        act_[:, :], act_[:, :], ajnk[:, 0:tpi * 64]), (sV, 1))
                    E("sync", lambda it=it: e.dma_start(
                        out=astb[:, :],
                        in_=ap[tab][2][:, it * tpi * 64:(it + 1) * tpi * 64]),
                        (dT2, 16))
                    W("sync", sV)
                    E("sync", lambda it=it: e.dma_start(
                        out=ajnk[:, 0:tpi * 64],
                        in_=ap[tab][3][:, it * tpi * 64:(it + 1) * tpi * 64]),
                        (dT2, 16))
                    W("vector", dT2)
                    E("vector", lambda: e.tensor_add(
                        astb[:, :], astb[:, :], ajnk[:, 0:tpi * 64]), (sV, 1))

                    import os as _os
                    _tpr = int(_os.environ.get("KLN", tpi) if _os.environ.get("KLN", "") != "off" else 0)
                    xtick = []
                    for tc in range(_tpr):
                        sl = tc % 2
                        if tc >= 2:
                            W("sync", sV, xtick[tc - 2])
                        E("sync", lambda it=it, tc=tc, sl=sl: e.dma_start(
                            out=ax[sl][:, :],
                            in_=ap[xsrc][(it * tpi + tc) * 128:
                                         (it * tpi + tc + 1) * 128, :]),
                            (dX[sl], 16))
                        _kcut = int(_os.environ.get("KCUT", 5))
                        if _kcut < 2:
                            xtick.append(sV.c)
                            continue
                        W("vector", dX[sl])
                        E("vector", lambda sl=sl: e.tensor_scalar(
                            ajnk[:, 0:D], ax[sl][:, :], 1.0, 0.0, op0=Op.mult,
                            op1=Op.add, accum_out=ast[:, 0:1]), (sV, 1))
                        W("vector", sV)
                        E("vector", lambda sl=sl: e.scalar_tensor_tensor(
                            ajnk[:, 0:D], ax[sl][:, :], 1.0, ax[sl][:, :],
                            op0=Op.mult, op1=Op.mult,
                            accum_out=ast[:, 1:2]), (sV, 1))
                        if _kcut < 3:
                            xtick.append(sV.c)
                            continue
                        W("vector", sV)
                        E("vector", lambda: e.tensor_scalar(
                            ast[:, 2:3], ast[:, 0:1], 1.0 / D, None, op0=Op.mult),
                            (sV, 1))
                        W("vector", sV)
                        E("vector", lambda: e.tensor_scalar(
                            ast[:, 3:4], ast[:, 1:2], 1.0 / D, None, op0=Op.mult),
                            (sV, 1))
                        W("vector", sV)
                        E("vector", lambda: e.scalar_tensor_tensor(
                            ast[:, 4:5], ast[:, 2:3], ast[:, 2:3], ast[:, 3:4],
                            op0=Op.mult, op1=Op.subtract), (sV, 1))
                        W("scalar", sV)
                        E("scalar", lambda: e.activation(
                            ast[:, 5:6], ast[:, 4:5], AF.Sqrt,
                            bias=g_["kc0"][:, 1:2], scale=-1.0), (sA, 1))
                        W("vector", sA)
                        E("vector", lambda: e.reciprocal(ast[:, 6:7], ast[:, 5:6]),
                          (sV, 1))
                        W("vector", sV)
                        E("vector", lambda sl=sl: e.tensor_scalar(
                            axn[:, :], ax[sl][:, :], ast[:, 2:3], ast[:, 6:7],
                            op0=Op.subtract, op1=Op.mult), (sV, 1))
                        if _kcut < 4:
                            xtick.append(sV.c)
                            continue
                        nb = (DC + 3) // 4
                        for bk in range(nb):
                            cnt = min(4, DC - bk * 4)
                            W("tensor", sV)
                            W("tensor", sA)
                            for q in range(cnt):
                                dc = bk * 4 + q
                                E("tensor", lambda dc=dc, bk=bk, q=q: e.transpose(
                                    ps[:, bk % 4, q * 128:(q + 1) * 128],
                                    axn[:, dc * 128:(dc + 1) * 128],
                                    g_["id32"][:, :]),
                                    *([(sT, 1)] if q == cnt - 1 else []))
                            for q in range(cnt):
                                if _kcut < 5:
                                    continue
                                dc = bk * 4 + q
                                col = dc * ait + tc * 128
                                if _os.environ.get("KEVAC", "dve") == "dve":
                                    W("vector", sT)
                                    E("vector", lambda bk=bk, q=q, col=col:
                                      e.tensor_copy(
                                          axh[:, col:col + 128],
                                          ps[:, bk % 4, q * 128:(q + 1) * 128]),
                                        (sA, 1))
                                else:
                                    W("scalar", sT)
                                    E("scalar", lambda bk=bk, q=q, col=col: e.activation(
                                        axh[:, col:col + 128],
                                        ps[:, bk % 4, q * 128:(q + 1) * 128],
                                        AF.Copy, bias=0.0, scale=1.0), (sA, 1))
                                W("vector", sA)
                                E("vector", lambda bk=bk, q=q, col=col:
                                  e.scalar_tensor_tensor(
                                      axl[:, col:col + 128],
                                      ps[:, bk % 4, q * 128:(q + 1) * 128], 0.0,
                                      axh[:, col:col + 128], op0=Op.add,
                                      op1=Op.subtract), (sV, 1))
                        xtick.append(sV.c)
                    if own:
                        W("gpsimd", sV)
                        for dc in range(DC):
                            E("gpsimd", lambda dc=dc, it=it: e.dma_start(
                                out=ap["xnscr_h"][dc][:, it * ait:(it + 1) * ait],
                                in_=axh[:, dc * ait:(dc + 1) * ait]), (gX, 16))
                            E("gpsimd", lambda dc=dc, it=it: e.dma_start(
                                out=ap["xnscr_l"][dc][:, it * ait:(it + 1) * ait],
                                in_=axl[:, dc * ait:(dc + 1) * ait]), (gX, 16))

                    projs = [(0, True)] if own else [(1, True), (2, False)]
                    import os
                    _pj = os.environ.get("KPROJ", "")
                    if _pj == "none":
                        projs = []
                    elif _pj == "k" and not own:
                        projs = [(1, True)]
                    elif _pj == "v" and not own:
                        projs = [(2, False)]
                    for (widx, isrope) in projs:
                        for blk in range(DBLK):
                            gwi = (it * 3 + widx) * DBLK + blk
                            sl = gwi % 2
                            W("sync", sT, wtick[sl])
                            for part, buf in (("wt_h", awh), ("wt_l", awl)):
                                E("sync", lambda part=part, widx=widx, blk=blk,
                                  buf=buf, sl=sl: e.dma_start(
                                      out=buf[sl][:, :].rearrange(
            "p (c j) -> p c j", c=DC),
                                      in_=ap[part][widx][:, :,
            blk * WBLK:(blk + 1) * WBLK]
                                      .rearrange("c p j -> p c j")), (dW[sl], 16))
                            for tc in range(tpi):
                                proj_tile(it, tc, blk, sl, isrope, xtick)
                            wtick[sl] = sT.c
        return sA_

    if 'A1' in stages:
        run_stage(ctx, make_sA(False))
    if 'A2' in stages:
        run_stage(ctx, make_sA(True))

    def sB(ctx, e, eng):
        E, W, B = mk_EW(e, eng, ctx.all_sems, ctx.own_map[eng])
        B()
        W("tensor", dC)
        NGKB = NG * KB
        if True:
            bmk = ctx.balloc("b_mk", [128, NT * KB], BF16)
            bqh = ctx.balloc("b_qh", [128, H * 256], BF16)
            bql = ctx.balloc("b_ql", [128, H * 256], BF16)
            bkh = ctx.balloc2("b_kh", [128, 2, NGKB], BF16)
            bkl = ctx.balloc2("b_kl", [128, 2, NGKB], BF16)
            bv = ctx.balloc2("b_v", [128, 2, NGKB], F32)
            bp = ctx.balloc("b_p", [128, 2 * KB], F32)
            bpt = ctx.balloc("b_pt", [128, 2 * KB], F32)
            blt = ctx.balloc("b_lt", [128, 2 * (NG + 2)], F32)
            bsc = ctx.balloc("b_sc", [128, 4], F32)
            blb = ctx.balloc("b_lb", [128, 256], F32)
            bhb = ctx.balloc2("b_hb", [128, 2, 256], F32)
            bhe = ctx.balloc2("b_he", [128, 2, 256], BF16)
            for jt in range(NT):
                E("sync", lambda jt=jt: e.dma_start(
                    out=bmk[:, jt * KB:(jt + 1) * KB], in_=ap["masks"][jt]),
                    (dC, 16))
            W("vector", dC)
            W("sync", gK[0])
            W("sync", gK[1])
            W("sync", gV[0])
            W("sync", gV[1])
            W("sync", gX)
            for h in range(H):
                E("vector", lambda h=h: e.tensor_reduce(
                    g_["kmax"][:, h:h + 1],
                    g_["kn2"][:, h * TT:(h + 1) * TT], axis=AX.X, op=Op.max),
                    (sV, 1))
            W("tensor", sV)
            E("tensor", lambda: e.transpose(
                ps[0:H, 6, 0:128], g_["kmax"][:, 0:H], g_["id32"][:, :]), (sT, 1))
            W("vector", sT)
            E("vector", lambda: e.tensor_reduce(
                bsc[0:H, 0:1], ps[0:H, 6, 0:128], axis=AX.X, op=Op.max), (sV, 1))
            W("scalar", sV)
            E("scalar", lambda: e.activation(
                bsc[0:H, 1:2], bsc[0:H, 0:1], AF.Sqrt, bias=g_["kc0"][0:H, 0:1],
                scale=1.0), (sA, 1))
            W("vector", sA)
            E("vector", lambda: e.tensor_scalar(
                bsc[0:H, 2:3], bsc[0:H, 1:2], -1.02, None, op0=Op.mult), (sV, 1))
            W("tensor", sV)
            E("tensor", lambda: e.transpose(
                ps[0:1, 6, 0:H], bsc[0:H, 2:3], g_["id32"][0:H, 0:H]), (sT, 1))
            W("scalar", sT)
            E("scalar", lambda: e.activation(
                bsc[0:1, 0:H] if H <= 4 else blb[0:1, 0:H],
                ps[0:1, 6, 0:H], AF.Copy, bias=0.0, scale=1.0), (sA, 1))
            W("tensor", sA)
            E("tensor", lambda: e.matmul(
                ps[:, 6, 0:H], g_["ones1"][0:1, :],
                bsc[0:1, 0:H] if H <= 4 else blb[0:1, 0:H],
                start=True, stop=True), (sT, 1))
            W("scalar", sT)
            E("scalar", lambda: e.activation(
                g_["kmax"][:, 0:H], ps[:, 6, 0:H], AF.Copy, bias=0.0,
                scale=1.0), (sA, 1))
            E("scalar", lambda: e.activation(
                g_["cbias"][:, 0:NT * H], g_["qn2"][:, 0:NT * H], AF.Sqrt,
                bias=g_["kc0"][:, 0:1], scale=1.0), (sA, 1))
            W("vector", sA)
            for jt in range(NT):
                E("vector", lambda jt=jt: e.tensor_mul(
                    g_["cbias"][:, jt * H:(jt + 1) * H],
                    g_["cbias"][:, jt * H:(jt + 1) * H], g_["kmax"][:, 0:H]),
                    (sV, 1))

            qtick = 0
            kvtick = {0: 0, 1: 0}
            avfree = {0: 0, 1: 0}
            hetick = {0: 0, 1: 0}
            hect = 0
            for g in range(NG):
                kb = kb_of_group(g)
                kbK = kb * KB
                W("sync", sT, qtick)
                for h in range(H):
                    E("sync", lambda g=g, h=h: e.dma_start(
                        out=bqh[:, h * 256:(h + 1) * 256],
                        in_=ap["qscr_h"][h][:, g * 256:(g + 1) * 256]), (dQ, 16))
                    E("sync", lambda g=g, h=h: e.dma_start(
                        out=bql[:, h * 256:(h + 1) * 256],
                        in_=ap["qscr_l"][h][:, g * 256:(g + 1) * 256]), (dQ, 16))
                for h in range(H):
                    sl = h % 2
                    if g * H + h >= 2:
                        W("sync", sT, kvtick[sl])
                    E("sync", lambda h=h, kbK=kbK, sl=sl: e.dma_start(
                        out=bkh[sl][:, 0:kbK], in_=ap["kscr_h"][h][:, 0:kbK]),
                        (dK[sl], 16))
                    E("sync", lambda h=h, kbK=kbK, sl=sl: e.dma_start(
                        out=bkl[sl][:, 0:kbK], in_=ap["kscr_l"][h][:, 0:kbK]),
                        (dK[sl], 16))
                    nvt = kbK // 128
                    for v0 in range(0, nvt, 8):
                        nv = min(8, nvt - v0)
                        E("sync", lambda h=h, v0=v0, nv=nv, sl=sl: e.dma_start(
                            out=bv[sl][:, v0 * 128:(v0 + nv) * 128].rearrange(
                                "p (n j) -> p n j", n=nv),
                            in_=ap["vscr"][v0 * 128:(v0 + nv) * 128,
            h * 128:(h + 1) * 128].rearrange(
                                "(n p) j -> p n j", p=128)), (dK[sl], 16))
                    avbank = 4 + sl
                    first_av = True
                    for kbi in range(kb):
                        for qt in range(2):
                            bnk = qt * 2 + (kbi % 2)
                            jt = 2 * g + qt
                            W("tensor", dQ)
                            W("tensor", dK[sl])
                            W("tensor", sA)
                            W("tensor", sV)
                            if first_av:
                                W("tensor", sV, avfree[sl])
                            qh_t = bqh[:, h * 256 + qt * 128: h * 256 + qt * 128 + 128]
                            ql_t = bql[:, h * 256 + qt * 128: h * 256 + qt * 128 + 128]
                            kh_t = bkh[sl][:, kbi * KB:(kbi + 1) * KB]
                            kl_t = bkl[sl][:, kbi * KB:(kbi + 1) * KB]
                            k = 0
                            for (sx, mv) in ((qh_t, kh_t), (qh_t, kl_t), (ql_t, kh_t)):
                                k += 1
                                E("tensor", lambda sx=sx, mv=mv, bnk=bnk, k=k:
                                  e.matmul(ps[:, bnk, 0:KB], sx, mv,
            start=(k == 1), stop=(k == 3)),
                                    *([(sT, 1)] if k == 3 else []))
                            if kbi == kb - 1:
                                W("vector", sT)
                                E("vector", lambda bnk=bnk, jt=jt: e.tensor_add(
                                    ps[:, bnk, 0:KB], ps[:, bnk, 0:KB],
                                    bmk[:, jt * KB:(jt + 1) * KB]), (sV, 1))
                            W("scalar", sT)
                            W("scalar", sV)
                            E("scalar", lambda bnk=bnk, qt=qt, jt=jt, kbi=kbi:
                              e.activation(
                                  bp[:, qt * KB:(qt + 1) * KB], ps[:, bnk, 0:KB],
                                  AF.Exp, bias=g_["cbias"][:, jt * H + h:jt * H + h + 1],
                                  scale=1.0,
                                  accum_out=blt[:, qt * (NG + 2) + kbi + 1:
            qt * (NG + 2) + kbi + 2]), (sA, 1))
                        tbank = 6
                        nkc = KB // 128
                        for kc in range(nkc):
                            if kc % 2 == 0:
                                W("tensor", sA)
                            for qt in range(2):
                                E("tensor", lambda kc=kc, qt=qt, tbank=tbank:
                                  e.transpose(
                                      ps[:, tbank, (kc % 2) * 256 + qt * 128:
            (kc % 2) * 256 + qt * 128 + 128],
                                      bp[:, qt * KB + kc * 128:
            qt * KB + (kc + 1) * 128],
                                      g_["id32"][:, :]),
                                    *([(sT, 1)] if qt == 1 else []))
                            if kc % 2 == 1 or kc == nkc - 1:
                                W("scalar", sT)
                                E("scalar", lambda kc=kc, tbank=tbank: e.activation(
                                    bpt[:, (kc - kc % 2) * 256:(kc + 1) * 256],
                                    ps[:, tbank, 0:(kc % 2 + 1) * 256],
                                    AF.Copy, bias=0.0, scale=1.0), (sA, 1))
                        W("tensor", sA)
                        for kc in range(nkc):
                            kt_glob = kbi * nkc + kc
                            vt = bv[sl][:, kt_glob * 128:(kt_glob + 1) * 128]
                            lastmm = (kbi == kb - 1 and kc == nkc - 1)
                            E("tensor", lambda vt=vt, kc=kc, avbank=avbank,
                              first_av=first_av, lastmm=lastmm: e.matmul(
                                  ps[:, avbank, 0:256], vt,
                                  bpt[:, kc * 256:(kc + 1) * 256],
                                  start=first_av, stop=lastmm),
                                *([(sT, 1)] if lastmm else []))
                            first_av = False
                    kvtick[sl] = sT.c
                    W("vector", sA)
                    for qt in range(2):
                        jt = 2 * g + qt
                        E("vector", lambda qt=qt, kb=kb: e.tensor_reduce(
                            blt[:, qt * (NG + 2):qt * (NG + 2) + 1],
                            blt[:, qt * (NG + 2) + 1:qt * (NG + 2) + 1 + kb],
                            axis=AX.X, op=Op.add), (sV, 1))
                        W("vector", sV)
                        E("vector", lambda qt=qt, jt=jt, h=h: e.reciprocal(
                            g_["linv"][:, jt * H + h:jt * H + h + 1],
                            blt[:, qt * (NG + 2):qt * (NG + 2) + 1]), (sV, 1))
                    W("tensor", sV)
                    W("tensor", sA)
                    for qt in range(2):
                        jt = 2 * g + qt
                        E("tensor", lambda jt=jt, h=h, qt=qt: e.transpose(
                            ps[0:1, 3, qt * 128:qt * 128 + 128],
                            g_["linv"][:, jt * H + h:jt * H + h + 1],
                            g_["id32"][:, :]), *([(sT, 1)] if qt == 1 else []))
                    W("scalar", sT)
                    E("scalar", lambda: e.activation(
                        blb[0:1, 0:256], ps[0:1, 3, 0:256], AF.Copy, bias=0.0,
                        scale=1.0), (sA, 1))
                    W("tensor", sA)
                    E("tensor", lambda: e.matmul(
                        ps[:, 3, 0:256], g_["ones1"][0:1, :], blb[0:1, 0:256],
                        start=True, stop=True), (sT, 1))
                    W("scalar", sT)
                    E("scalar", lambda: e.activation(
                        blb[:, 0:256], ps[:, 3, 0:256], AF.Copy, bias=0.0,
                        scale=1.0), (sA, 1))
                    hesl = hect % 2
                    hect += 1
                    W("vector", sA)
                    W("vector", sT)
                    W("vector", gH[hesl])
                    E("vector", lambda avbank=avbank, hesl=hesl: e.tensor_mul(
                        bhb[hesl][:, :], ps[:, avbank, 0:256], blb[:, 0:256]),
                        (sV, 1))
                    avfree[sl] = sV.c
                    W("vector", sV)
                    E("vector", lambda hesl=hesl: e.tensor_copy(
                        bhe[hesl][:, :], bhb[hesl][:, :]), (sV, 1))
                    W("vector", sV)
                    E("vector", lambda hesl=hesl: e.tensor_sub(
                        bhb[hesl][:, :], bhb[hesl][:, :], bhe[hesl][:, :]), (sV, 1))
                    W("gpsimd", sV)
                    E("gpsimd", lambda h=h, g=g, hesl=hesl: e.dma_start(
                        out=ap["hscr_h"][h][:, g * 256:(g + 1) * 256],
                        in_=bhe[hesl][:, :]), (gH[hesl], 16))
                    E("gpsimd", lambda h=h, g=g, hesl=hesl: e.dma_start(
                        out=ap["hscr_l"][h][:, g * 256:(g + 1) * 256],
                        in_=bhb[hesl][:, :]), (gH[hesl], 16))
                qtick = sT.c

    if 'B' in stages:
        run_stage(ctx, sB)

    def sC(ctx, e, eng):
        E, W, B = mk_EW(e, eng, ctx.all_sems, ctx.own_map[eng])
        B()
        W("tensor", dC)
        TCB = min(512, OT)
        NCB = OT // TCB
        TPC = TCB // 128
        if True:
            chh = ctx.balloc("c_hh", [128, H * TCB], BF16)
            chl = ctx.balloc("c_hl", [128, H * TCB], BF16)
            cwh = ctx.balloc2("c_wh", [128, 2, H * WBLK], BF16)
            cwl = ctx.balloc2("c_wl", [128, 2, H * WBLK], BF16)
            cx = ctx.balloc("c_x", [128, TPC * D], F32)
            co = ctx.balloc2("c_o", [128, 2, WBLK], F32)
            wtick = {0: 0, 1: 0}
            oct_ = 0
            W("sync", gH[0])
            W("sync", gH[1])
            for tcb in range(NCB):
                W("sync", sT)
                W("sync", sV)
                for h in range(H):
                    E("sync", lambda h=h, tcb=tcb: e.dma_start(
                        out=chh[:, h * TCB:(h + 1) * TCB],
                        in_=ap["hscr_h"][h][:, tcb * TCB:(tcb + 1) * TCB]),
                        (dX[0], 16))
                    E("sync", lambda h=h, tcb=tcb: e.dma_start(
                        out=chl[:, h * TCB:(h + 1) * TCB],
                        in_=ap["hscr_l"][h][:, tcb * TCB:(tcb + 1) * TCB]),
                        (dX[0], 16))
                for tc in range(TPC):
                    E("sync", lambda tcb=tcb, tc=tc: e.dma_start(
                        out=cx[:, tc * D:(tc + 1) * D],
                        in_=ap["xo"][(tcb * TPC + tc) * 128:
                                     (tcb * TPC + tc + 1) * 128, :]), (dX[0], 16))
                for dob in range(DBLK):
                    sl = (tcb * DBLK + dob) % 2
                    if tcb * DBLK + dob >= 2:
                        W("sync", sT, wtick[sl])
                    for part, buf in (("wot_h", cwh), ("wot_l", cwl)):
                        E("sync", lambda part=part, dob=dob, buf=buf, sl=sl:
                          e.dma_start(
                              out=buf[sl][:, :].rearrange("p (c j) -> p c j", c=H),
                              in_=ap[part][:, :, dob * WBLK:(dob + 1) * WBLK]
                              .rearrange("c p j -> p c j")), (dW[sl], 16))
                    for tc in range(TPC):
                        bank = tc % 2
                        osl = oct_ % 2
                        oct_ += 1
                        W("tensor", dW[sl])
                        W("tensor", dX[0])
                        W("tensor", sV)
                        k, n3 = 0, 3 * H
                        for dc in range(H):
                            hh_t = chh[:, dc * TCB + tc * 128: dc * TCB + tc * 128 + 128]
                            hl_t = chl[:, dc * TCB + tc * 128: dc * TCB + tc * 128 + 128]
                            wh_t = cwh[sl][:, dc * WBLK:(dc + 1) * WBLK]
                            wl_t = cwl[sl][:, dc * WBLK:(dc + 1) * WBLK]
                            for (sx, mv) in ((hh_t, wh_t), (hh_t, wl_t), (hl_t, wh_t)):
                                k += 1
                                E("tensor", lambda sx=sx, mv=mv, bank=bank, k=k,
                                  n3=n3: e.matmul(
                                      ps[:, bank, 0:WBLK], sx, mv, start=(k == 1),
                                      stop=(k == n3)),
                                    *([(sT, 1)] if k == n3 else []))
                        W("vector", sT)
                        W("vector", dX[0])
                        W("vector", gO[osl])
                        E("vector", lambda bank=bank, tc=tc, dob=dob, osl=osl:
                          e.tensor_add(
                              co[osl][:, :], ps[:, bank, 0:WBLK],
                              cx[:, tc * D + dob * WBLK: tc * D + (dob + 1) * WBLK]),
                            (sV, 1))
                        W("gpsimd", sV)
                        E("gpsimd", lambda tcb=tcb, tc=tc, dob=dob, osl=osl:
                          e.dma_start(
                              out=ap["oscr"][(tcb * TPC + tc) * 128:
            (tcb * TPC + tc + 1) * 128,
            dob * WBLK:(dob + 1) * WBLK],
                              in_=co[osl][:, :]), (gO[osl], 16))
                    wtick[sl] = sT.c

    if 'C' in stages:
        run_stage(ctx, sC)

    def sD(ctx, e, eng):
        E, W, B = mk_EW(e, eng, ctx.all_sems, ctx.own_map[eng])
        B()
        W("tensor", dC)
        OC = IC
        HIq = I // 2
        if True:
            dxh = ctx.balloc("d_xh", [128, DC * FMT], BF16)
            dxl = ctx.balloc("d_xl", [128, DC * FMT], BF16)
            df1 = ctx.balloc("d_f1", [128, DC * FMT], F32)
            dtm = ctx.balloc("d_tm", [128, D], F32)
            dq1f = ctx.balloc("d_q1f", [128, DC * FMT], BF16)
            dsc = ctx.balloc("d_sc", [128, 16], F32)
            dwg = ctx.balloc2("d_wg", [128, 1, DC * WBI], FP8)
            dwu = ctx.balloc2("d_wu", [128, 1, DC * WBI], FP8)
            dt1 = ctx.balloc("d_t1", [128, WBI], F32)
            dt2 = ctx.balloc("d_t2", [128, WBI], F32)
            dytm = ctx.balloc("d_ytm", [128, I], F32)
            dyfm = ctx.balloc("d_yfm", [128, IC * FMT], F32)
            dsc2 = ctx.balloc("d_sc2", [128, (IC // 2) * FMT], F32)
            dysh = ctx.balloc("d_ysh", [128, 2 * FMT], BF16)
            dysl = ctx.balloc("d_ysl", [128, 2 * FMT], BF16)
            dq2t = ctx.balloc("d_q2t", [128, I], BF16)
            dq2f = ctx.balloc("d_q2f", [128, IC * FMT], BF16)
            dwd = ctx.balloc("d_wd", [128, IC * WBLK], FP8)
            dos = ctx.balloc("d_os", [128, 512], F32)
            dou = ctx.balloc2("d_ou", [128, 1, WBLK], F32)
            wgtick = {0: 0, 1: 0}
            ouct = 0
            W("sync", gX)
            W("sync", gO[0])
            W("sync", gO[1])
            for mi in range(NM):
                t0 = mi * FMT
                W("sync", sT)
                W("sync", sV)
                for dc in range(DC):
                    E("sync", lambda dc=dc, t0=t0: e.dma_start(
                        out=dxh[:, dc * FMT:(dc + 1) * FMT],
                        in_=ap["xnscr_h"][dc][:, t0:t0 + FMT]), (dX[0], 16))
                    E("sync", lambda dc=dc, t0=t0: e.dma_start(
                        out=dxl[:, dc * FMT:(dc + 1) * FMT],
                        in_=ap["xnscr_l"][dc][:, t0:t0 + FMT]), (dX[0], 16))

                for dc in range(DC):
                    bank = dc % 2
                    W("tensor", dX[0])
                    W("tensor", sA)
                    E("tensor", lambda dc=dc, bank=bank: e.matmul(
                        ps[:, bank, 0:FMT], g_["hmat"][:, :],
                        dxh[:, dc * FMT:(dc + 1) * FMT], start=True, stop=False))
                    E("tensor", lambda dc=dc, bank=bank: e.matmul(
                        ps[:, bank, 0:FMT], g_["hmat"][:, :],
                        dxl[:, dc * FMT:(dc + 1) * FMT], start=False, stop=True),
                        (sT, 1))
                    W("scalar", sT)
                    E("scalar", lambda dc=dc, bank=bank: e.activation(
                        df1[:, dc * FMT:(dc + 1) * FMT], ps[:, bank, 0:FMT],
                        AF.Copy, bias=0.0, scale=1.0), (sA, 1))
                ap1 = df1[:, :].rearrange("p (c j) -> p c j", c=DC)
                for lv in range(L1):
                    stp = 1 << lv
                    np_ = DC // 2
                    W("vector", sA if lv == 0 else sV)
                    E("vector", lambda stp=stp: e.tensor_copy(
                        dsc2[:, 0:np_ * FMT].rearrange("p (c j) -> p c j", c=np_),
                        _bsel(ap1, stp, 1)), (sV, 1))
                    W("vector", sV)
                    E("vector", lambda stp=stp: e.tensor_sub(
                        _bsel(ap1, stp, 1), _bsel(ap1, stp, 0), _bsel(ap1, stp, 1)),
                        (sV, 1))
                    W("vector", sV)
                    E("vector", lambda stp=stp: e.tensor_add(
                        _bsel(ap1, stp, 0), _bsel(ap1, stp, 0),
                        dsc2[:, 0:np_ * FMT].rearrange("p (c j) -> p c j", c=np_)),
                        (sV, 1))
                for dc in range(DC):
                    bank = 4 + ((dc // 4) % 2)
                    q = dc % 4
                    if q == 0:
                        W("tensor", sV)
                        W("tensor", sA)
                    E("tensor", lambda dc=dc, bank=bank, q=q: e.transpose(
                        ps[:, bank, q * 128:(q + 1) * 128],
                        df1[:, dc * FMT:(dc + 1) * FMT], g_["id32"][:, :]),
                        *([(sT, 1)] if (q == 3 or dc == DC - 1) else []))
                    if q == 3 or dc == DC - 1:
                        W("scalar", sT)
                        E("scalar", lambda dc=dc, bank=bank, q=q: e.activation(
                            dtm[:, (dc - q) * 128:(dc + 1) * 128],
                            ps[:, bank, 0:(q + 1) * 128], AF.Copy, bias=0.0,
                            scale=1.0), (sA, 1))
                W("vector", sA)
                E("vector", lambda: e.tensor_reduce(
                    dsc[:, 0:1], dtm[:, :], axis=AX.X, op=Op.max,
                    apply_absolute_value=True), (sV, 1))
                W("vector", sV)
                E("vector", lambda: e.tensor_scalar(
                    dsc[:, 1:2], dsc[:, 0:1], C1, EPS, op0=Op.mult, op1=Op.max),
                    (sV, 1))
                W("vector", sV)
                E("vector", lambda: e.tensor_scalar(
                    dsc[:, 1:2], dsc[:, 1:2], 1.0 / 127.0, None, op0=Op.mult),
                    (sV, 1))
                W("vector", sV)
                E("vector", lambda: e.reciprocal(dsc[:, 2:3], dsc[:, 1:2]), (sV, 1))
                W("vector", sV)
                E("vector", lambda: e.tensor_scalar(
                    dsc[:, 2:3], dsc[:, 2:3], C1, None, op0=Op.mult), (sV, 1))
                W("vector", sV)
                E("vector", lambda: e.tensor_scalar(
                    dsc[:, 3:4], dsc[:, 1:2], g_["wsg"][:, 0:1], None, op0=Op.mult),
                    (sV, 1))
                E("vector", lambda: e.tensor_scalar(
                    dsc[:, 4:5], dsc[:, 1:2], g_["wsu"][:, 0:1], None, op0=Op.mult),
                    (sV, 1))
                E("vector", lambda: e.tensor_scalar(
                    dtm[:, :], dtm[:, :], dsc[:, 2:3], -128.0, op0=Op.mult,
                    op1=Op.max), (sV, 1))
                E("vector", lambda: e.tensor_scalar(
                    dtm[:, :], dtm[:, :], 128.0, 128.5, op0=Op.min, op1=Op.add),
                    (sV, 1))
                E("vector", lambda: e.tensor_scalar(
                    df1[:, 0:D], dtm[:, :], 1.0, None, op0=Op.mult), (sV, 1))
                for kk in range(8, -1, -1):
                    tt = float(1 << kk)
                    E("vector", lambda tt=tt: e.tensor_scalar(
                        dq2t[:, 0:D], dtm[:, :], tt, None, op0=Op.is_ge), (sV, 1))
                    E("vector", lambda tt=tt: e.scalar_tensor_tensor(
                        dtm[:, :], dq2t[:, 0:D], -tt, dtm[:, :], op0=Op.mult,
                        op1=Op.add), (sV, 1))
                E("vector", lambda: e.tensor_sub(
                    dtm[:, :], df1[:, 0:D], dtm[:, :]), (sV, 1))
                E("vector", lambda: e.tensor_scalar(
                    dq2t[:, 0:D], dtm[:, :], -128.0, 127.0, op0=Op.add, op1=Op.min),
                    (sV, 1))
                for dc in range(DC):
                    half = (dc // 4) % 2
                    q = dc % 4
                    if q == 0:
                        W("tensor", sV)
                        W("tensor", sA)
                    E("tensor", lambda dc=dc, half=half, q=q: e.transpose(
                        psb[:, half * 512 + q * 128: half * 512 + (q + 1) * 128],
                        dq2t[:, dc * 128:(dc + 1) * 128], g_["idbf"][:, :]),
                        *([(sT, 1)] if (q == 3 or dc == DC - 1) else []))
                    if q == 3 or dc == DC - 1:
                        W("scalar", sT)
                        for qq in range(q + 1):
                            dcq = dc - q + qq
                            E("scalar", lambda dcq=dcq, half=half, qq=qq:
                              e.activation(
                                  dq1f[:, dcq * FMT:(dcq + 1) * FMT],
                                  psb[:, half * 512 + qq * 128:
                                      half * 512 + (qq + 1) * 128], AF.Copy,
                                  bias=0.0, scale=1.0),
                                *([(sA, 1)] if qq == q else []))
                for ob in range(OBI):
                    gsl = 0
                    if mi * OBI + ob >= 1:
                        W("sync", sT, wgtick[gsl])
                    E("sync", lambda ob=ob, gsl=gsl: e.dma_start(
                        out=dwg[gsl][:, :].rearrange("p (c j) -> p c j", c=DC),
                        in_=ap["wscr_g"][:, :, ob * WBI:(ob + 1) * WBI]
                        .rearrange("c p j -> p c j")), (dW[gsl], 16))
                    E("sync", lambda ob=ob, gsl=gsl: e.dma_start(
                        out=dwu[gsl][:, :].rearrange("p (c j) -> p c j", c=DC),
                        in_=ap["wscr_u"][:, :, ob * WBI:(ob + 1) * WBI]
                        .rearrange("c p j -> p c j")), (dW[gsl], 16))
                    for (wbuf, bank) in ((dwg, 2), (dwu, 3)):
                        W("tensor", dW[gsl])
                        W("tensor", sA)
                        k = 0
                        for dc in range(DC):
                            k += 1
                            E("tensor", lambda wbuf=wbuf, dc=dc, bank=bank, k=k,
                              gsl=gsl: e.matmul(
                                  ps[:, bank, 0:WBI],
                                  dq1f[:, dc * FMT:(dc + 1) * FMT],
                                  wbuf[gsl][:, dc * WBI:(dc + 1) * WBI],
                                  start=(k == 1), stop=(k == DC)),
                                *([(sT, 1)] if k == DC else []))
                    wgtick[gsl] = sT.c
                    W("scalar", sT)
                    W("scalar", sV)
                    E("scalar", lambda: e.activation(
                        dt1[:, :], ps[:, 2, 0:WBI], AF.Silu,
                        bias=g_["kc0"][:, 0:1], scale=dsc[:, 3:4]), (sA, 1))
                    E("scalar", lambda: e.activation(
                        dt2[:, :], ps[:, 3, 0:WBI], AF.Identity,
                        bias=g_["kc0"][:, 0:1], scale=dsc[:, 4:5]), (sA, 1))
                    W("vector", sA)
                    E("vector", lambda ob=ob: e.tensor_mul(
                        dytm[:, ob * WBI:(ob + 1) * WBI], dt1[:, :], dt2[:, :]),
                        (sV, 1))
                for ic in range(IC):
                    bank = 4 + ((ic // 4) % 2)
                    q = ic % 4
                    if q == 0:
                        W("tensor", sV)
                        W("tensor", sA)
                    E("tensor", lambda ic=ic, bank=bank, q=q: e.transpose(
                        ps[:, bank, q * 128:(q + 1) * 128],
                        dytm[:, ic * 128:(ic + 1) * 128], g_["id32"][:, :]),
                        *([(sT, 1)] if (q == 3 or ic == IC - 1) else []))
                    if q == 3 or ic == IC - 1:
                        W("scalar", sT)
                        for qq in range(q + 1):
                            icq = ic - q + qq
                            E("scalar", lambda icq=icq, bank=bank, qq=qq:
                              e.activation(
                                  dyfm[:, icq * FMT:(icq + 1) * FMT],
                                  ps[:, bank, qq * 128:(qq + 1) * 128], AF.Copy,
                                  bias=0.0, scale=1.0),
                                *([(sA, 1)] if qq == q else []))
                for c0 in range(0, IC, 2):
                    ng = min(2, IC - c0)
                    W("vector", sA)
                    W("vector", sV)
                    E("vector", lambda c0=c0, ng=ng: e.tensor_copy(
                        dysh[:, 0:ng * FMT], dyfm[:, c0 * FMT:(c0 + ng) * FMT]),
                        (sV, 1))
                    W("vector", sV)
                    E("vector", lambda c0=c0, ng=ng: e.tensor_sub(
                        dysl[:, 0:ng * FMT], dyfm[:, c0 * FMT:(c0 + ng) * FMT],
                        dysh[:, 0:ng * FMT]), (sV, 1))
                    for ci in range(ng):
                        ic = c0 + ci
                        bank = ic % 2
                        W("tensor", sV)
                        W("tensor", sA)
                        E("tensor", lambda ci=ci, bank=bank: e.matmul(
                            ps[:, bank, 0:FMT], g_["hmat"][:, :],
                            dysh[:, ci * FMT:(ci + 1) * FMT], start=True,
                            stop=False))
                        E("tensor", lambda ci=ci, bank=bank: e.matmul(
                            ps[:, bank, 0:FMT], g_["hmat"][:, :],
                            dysl[:, ci * FMT:(ci + 1) * FMT], start=False,
                            stop=True), (sT, 1))
                        W("scalar", sT)
                        E("scalar", lambda ic=ic, bank=bank: e.activation(
                            dyfm[:, ic * FMT:(ic + 1) * FMT], ps[:, bank, 0:FMT],
                            AF.Copy, bias=0.0, scale=1.0), (sA, 1))
                ap2 = dyfm[:, :].rearrange("p (c j) -> p c j", c=IC)
                for lv in range(L2):
                    stp = 1 << lv
                    np_ = IC // 2
                    W("vector", sA if lv == 0 else sV)
                    E("vector", lambda stp=stp: e.tensor_copy(
                        dsc2[:, 0:np_ * FMT].rearrange("p (c j) -> p c j", c=np_),
                        _bsel(ap2, stp, 1)), (sV, 1))
                    W("vector", sV)
                    E("vector", lambda stp=stp: e.tensor_sub(
                        _bsel(ap2, stp, 1), _bsel(ap2, stp, 0), _bsel(ap2, stp, 1)),
                        (sV, 1))
                    W("vector", sV)
                    E("vector", lambda stp=stp: e.tensor_add(
                        _bsel(ap2, stp, 0), _bsel(ap2, stp, 0),
                        dsc2[:, 0:np_ * FMT].rearrange("p (c j) -> p c j", c=np_)),
                        (sV, 1))
                for ic in range(IC):
                    bank = 4 + ((ic // 4) % 2)
                    q = ic % 4
                    if q == 0:
                        W("tensor", sV)
                        W("tensor", sA)
                    E("tensor", lambda ic=ic, bank=bank, q=q: e.transpose(
                        ps[:, bank, q * 128:(q + 1) * 128],
                        dyfm[:, ic * FMT:(ic + 1) * FMT], g_["id32"][:, :]),
                        *([(sT, 1)] if (q == 3 or ic == IC - 1) else []))
                    if q == 3 or ic == IC - 1:
                        W("scalar", sT)
                        E("scalar", lambda ic=ic, bank=bank, q=q: e.activation(
                            dytm[:, (ic - q) * 128:(ic + 1) * 128],
                            ps[:, bank, 0:(q + 1) * 128], AF.Copy, bias=0.0,
                            scale=1.0), (sA, 1))
                W("vector", sA)
                E("vector", lambda: e.tensor_reduce(
                    dsc[:, 8:9], dytm[:, :], axis=AX.X, op=Op.max,
                    apply_absolute_value=True), (sV, 1))
                W("vector", sV)
                E("vector", lambda: e.tensor_scalar(
                    dsc[:, 9:10], dsc[:, 8:9], C2, EPS, op0=Op.mult, op1=Op.max),
                    (sV, 1))
                W("vector", sV)
                E("vector", lambda: e.tensor_scalar(
                    dsc[:, 9:10], dsc[:, 9:10], 1.0 / 127.0, None, op0=Op.mult),
                    (sV, 1))
                W("vector", sV)
                E("vector", lambda: e.reciprocal(dsc[:, 10:11], dsc[:, 9:10]),
                  (sV, 1))
                W("vector", sV)
                E("vector", lambda: e.tensor_scalar(
                    dsc[:, 10:11], dsc[:, 10:11], C2, None, op0=Op.mult), (sV, 1))
                E("vector", lambda: e.tensor_scalar(
                    dsc[:, 11:12], dsc[:, 9:10], g_["wsd"][:, 0:1], None,
                    op0=Op.mult), (sV, 1))
                for hf in range(2):
                    dyh = dytm[:, hf * HIq:(hf + 1) * HIq]
                    dqh = dq2t[:, hf * HIq:(hf + 1) * HIq]
                    E("vector", lambda dyh=dyh: e.tensor_scalar(
                        dyh, dyh, dsc[:, 10:11], -128.0, op0=Op.mult, op1=Op.max),
                        (sV, 1))
                    E("vector", lambda dyh=dyh: e.tensor_scalar(
                        dyh, dyh, 128.0, 128.5, op0=Op.min, op1=Op.add), (sV, 1))
                    E("vector", lambda dyh=dyh: e.tensor_scalar(
                        dsc2[:, 0:HIq], dyh, 1.0, None, op0=Op.mult), (sV, 1))
                    for kk in range(8, -1, -1):
                        tt = float(1 << kk)
                        E("vector", lambda dyh=dyh, dqh=dqh, tt=tt: e.tensor_scalar(
                            dqh, dyh, tt, None, op0=Op.is_ge), (sV, 1))
                        E("vector", lambda dyh=dyh, dqh=dqh, tt=tt:
                          e.scalar_tensor_tensor(
                              dyh, dqh, -tt, dyh, op0=Op.mult, op1=Op.add), (sV, 1))
                    E("vector", lambda dyh=dyh: e.tensor_sub(
                        dyh, dsc2[:, 0:HIq], dyh), (sV, 1))
                    E("vector", lambda dyh=dyh, dqh=dqh: e.tensor_scalar(
                        dqh, dyh, -128.0, 127.0, op0=Op.add, op1=Op.min), (sV, 1))
                for ic in range(IC):
                    half = (ic // 4) % 2
                    q = ic % 4
                    if q == 0:
                        W("tensor", sV)
                        W("tensor", sA)
                    E("tensor", lambda ic=ic, half=half, q=q: e.transpose(
                        psb[:, half * 512 + q * 128: half * 512 + (q + 1) * 128],
                        dq2t[:, ic * 128:(ic + 1) * 128], g_["idbf"][:, :]),
                        *([(sT, 1)] if (q == 3 or ic == IC - 1) else []))
                    if q == 3 or ic == IC - 1:
                        W("scalar", sT)
                        for qq in range(q + 1):
                            icq = ic - q + qq
                            E("scalar", lambda icq=icq, half=half, qq=qq:
                              e.activation(
                                  dq2f[:, icq * FMT:(icq + 1) * FMT],
                                  psb[:, half * 512 + qq * 128:
                                      half * 512 + (qq + 1) * 128], AF.Copy,
                                  bias=0.0, scale=1.0),
                                *([(sA, 1)] if qq == q else []))
                for dob in range(DBLK):
                    W("sync", sT)
                    W("sync", sV)
                    E("sync", lambda mi=mi, dob=dob: e.dma_start(
                        out=dos[:, 0:WBLK],
                        in_=ap["oscr"][mi * 128:(mi + 1) * 128,
                                       dob * WBLK:(dob + 1) * WBLK]), (dX[0], 16))
                    E("sync", lambda dob=dob: e.dma_start(
                        out=dwd[:, :].rearrange("p (c j) -> p c j", c=IC),
                        in_=ap["wscr_d"][:, :, dob * WBLK:(dob + 1) * WBLK]
                        .rearrange("c p j -> p c j")), (dD, 16))
                    bank = dob % 2
                    osl = 0
                    ouct += 1
                    W("tensor", dD)
                    W("tensor", sV)
                    W("tensor", sA)
                    k = 0
                    for ic in range(IC):
                        k += 1
                        E("tensor", lambda ic=ic, bank=bank, k=k: e.matmul(
                            ps[:, bank, 0:WBLK], dq2f[:, ic * FMT:(ic + 1) * FMT],
                            dwd[:, ic * WBLK:(ic + 1) * WBLK], start=(k == 1),
                            stop=(k == IC)), *([(sT, 1)] if k == IC else []))
                    W("vector", sT)
                    W("vector", dX[0])
                    W("vector", gU[osl])
                    E("vector", lambda bank=bank, dob=dob, osl=osl:
                      e.scalar_tensor_tensor(
                          dou[osl][:, :], ps[:, bank, 0:WBLK], dsc[:, 11:12],
                          dos[:, 0:WBLK], op0=Op.mult,
                          op1=Op.add), (sV, 1))
                    W("gpsimd", sV)
                    E("gpsimd", lambda mi=mi, dob=dob, osl=osl: e.dma_start(
                        out=ap["out"][mi * 128:(mi + 1) * 128,
                                      dob * WBLK:(dob + 1) * WBLK],
                        in_=dou[osl][:, :]), (gU[osl], 16))
            W("gpsimd", gU[0])
            W("gpsimd", gU[1])

    if 'D' in stages:
        run_stage(ctx, sD)

    stk.close()
    return nc



_NC_CACHE = {}


def kernel(**inputs):
    cfg = default_cfg()
    cd = derived(cfg)
    if "nc" not in _NC_CACHE:
        _NC_CACHE["nc"] = build_nc(cfg)
    nc = _NC_CACHE["nc"]
    in_maps = [host_core_inputs(cfg, inputs, c) for c in range(cfg["n_cores"])]
    res = run_bass_kernel_spmd(nc, in_maps, core_ids=list(range(cfg["n_cores"])))
    B, T, D = cfg["B"], cfg["T"], cfg["D"]
    out = np.empty((B, T, D), dtype=np.float32)
    for c in range(cfg["n_cores"]):
        b, p = c // cd["CPB"], c % cd["CPB"]
        o = np.asarray(res.results[c]["out"])
        for jt in range(cd["NT"]):
            out[b, (2 * jt + p) * 128:(2 * jt + p + 1) * 128, :] = \
                o[jt * 128:(jt + 1) * 128, :]
    return out


# version 06
# speedup vs baseline: 1.1227x; 1.1227x over previous
import contextlib
import numpy as np
import ml_dtypes
import concourse.bass as bass
import concourse.mybir as mybir
from concourse.alu_op_type import AluOpType as Op
from concourse.bass_utils import run_bass_kernel_spmd
from concourse import library_config

F32 = mybir.dt.float32
BF16 = mybir.dt.bfloat16
FP8 = mybir.dt.float8e4
AF = mybir.ActivationFunctionType
AX = mybir.AxisListType

NEG = np.float32(-3.0e38)
EPS = 1e-5


def default_cfg():
    return dict(B=4, T=4096, D=2048, H=16, I=8192, n_cores=8)


def derived(cfg):
    d = dict(cfg)
    T, D, I = cfg["T"], cfg["D"], cfg["I"]
    d["DC"] = D // 128
    d["IC"] = I // 128
    d["OT"] = T // 2
    d["NT"] = d["OT"] // 128
    d["TT"] = T // 128
    d["NG"] = max(d["NT"] // 2, 1)
    d["KB"] = min(512, T)
    d["WBLK"] = min(512, D)
    d["DBLK"] = D // d["WBLK"]
    d["WBI"] = min(512, I)
    d["OBI"] = I // d["WBI"]
    d["AIT"] = min(1024, T)
    d["AIT2"] = min(1024, d["OT"])
    d["FMT"] = 128
    d["NM"] = d["OT"] // d["FMT"]
    d["CPB"] = cfg["n_cores"] // cfg["B"]
    return d


def kb_of_group(g):
    return g + 1



def _hadamard(n):
    h = np.array([[1.0]], dtype=np.float64)
    while h.shape[0] < n:
        h = np.block([[h, h], [h, -h]])
    return h


def _rope_tables(positions, scale):
    DH = 128
    j = np.arange(DH // 2, dtype=np.float64)
    theta = 10000.0 ** (-2.0 * j / DH)
    ang = positions.astype(np.float64)[:, None] * theta[None, :]
    out = []
    for t in (np.cos(ang) * scale, np.sin(ang) * scale):
        hi = t.astype(np.float32)
        lo = (t - hi).astype(np.float32)
        out += [hi, lo]
    return out


def _tile_major(a):
    T = a.shape[0]
    return np.ascontiguousarray(
        a.reshape(T // 128, 128, 64).transpose(1, 0, 2).reshape(128, -1))


def _deint_perm(D):
    DH = 128
    perm = []
    for h0 in range(0, D, DH):
        perm += [h0 + 2 * j for j in range(DH // 2)]
        perm += [h0 + 2 * j + 1 for j in range(DH // 2)]
    return np.array(perm)


def host_core_inputs(cfg, inputs, core, names=None):
    cd = derived(cfg)
    T, D = cfg["T"], cfg["D"]
    b, p = core // cd["CPB"], core % cd["CPB"]
    nm = names or {}
    X = np.asarray(inputs[nm.get("X", "L0_attn_X")])
    Wq = np.asarray(inputs[nm.get("Wq", "L0_attn_Wq")])
    Wk = np.asarray(inputs[nm.get("Wk", "L0_attn_Wk")])
    Wv = np.asarray(inputs[nm.get("Wv", "L0_attn_Wv")])
    Wo = np.asarray(inputs[nm.get("Wo", "L0_attn_Wo")])
    Wg = np.asarray(inputs[nm.get("Wg", "L0_ffn_w_gate")])
    Wu = np.asarray(inputs[nm.get("Wu", "L0_ffn_w_up")])
    Wd = np.asarray(inputs[nm.get("Wd", "L0_ffn_w_down")])
    own_rows = np.concatenate(
        [np.arange((2 * jt + p) * 128, (2 * jt + p + 1) * 128)
         for jt in range(cd["NT"])])
    perm = _deint_perm(D)

    m = {}
    m["xf"] = np.ascontiguousarray(X[b], dtype=np.float32)
    m["xo"] = np.ascontiguousarray(X[b][own_rows], dtype=np.float32)
    m["wqT"] = np.ascontiguousarray(Wq[perm].T, dtype=np.float32)
    m["wkT"] = np.ascontiguousarray(Wk[perm].T, dtype=np.float32)
    m["wvT"] = np.ascontiguousarray(Wv.T, dtype=np.float32)
    m["woT"] = np.ascontiguousarray(Wo.T, dtype=np.float32)
    m["wgT"] = np.ascontiguousarray(Wg.T, dtype=np.float32)
    m["wuT"] = np.ascontiguousarray(Wu.T, dtype=np.float32)
    m["wdT"] = np.ascontiguousarray(Wd.T, dtype=np.float32)

    kt = _rope_tables(np.arange(T), 1.0)
    m["tabk"] = np.stack([_tile_major(x) for x in kt], 0)
    qt = _rope_tables(own_rows, 128.0 ** -0.5)
    m["tabq"] = np.stack([_tile_major(x) for x in qt], 0)

    masks = np.zeros((cd["NT"], 128, cd["KB"]), dtype=np.float32)
    for jt in range(cd["NT"]):
        gq, g = 2 * jt + p, jt // 2
        k0 = (kb_of_group(g) - 1) * cd["KB"]
        i = np.arange(128)[:, None]
        jj = np.arange(cd["KB"])[None, :]
        masks[jt] = np.where(k0 + jj > gq * 128 + i, NEG, 0.0)
    m["masks"] = masks.astype(ml_dtypes.bfloat16)

    m["hmat"] = _hadamard(128).astype(ml_dtypes.bfloat16)
    m["id32"] = np.eye(128, dtype=np.float32)
    m["idbf"] = np.eye(128).astype(ml_dtypes.bfloat16)
    return m


def host_input_specs(cfg):
    cd = derived(cfg)
    T, D, I = cfg["T"], cfg["D"], cfg["I"]
    OT, NT, TT, KB = cd["OT"], cd["NT"], cd["TT"], cd["KB"]
    return [
        ("xf", [T, D], F32), ("xo", [OT, D], F32),
        ("wqT", [D, D], F32), ("wkT", [D, D], F32),
        ("wvT", [D, D], F32), ("woT", [D, D], F32),
        ("wgT", [D, I], F32), ("wuT", [D, I], F32), ("wdT", [I, D], F32),
        ("tabk", [4, 128, TT * 64], F32), ("tabq", [4, 128, NT * 64], F32),
        ("masks", [NT, 128, KB], BF16),
        ("hmat", [128, 128], BF16), ("id32", [128, 128], F32),
        ("idbf", [128, 128], BF16),
    ]



class Sm:
    __slots__ = ("h", "c")

    def __init__(self, h):
        self.h = h
        self.c = 0


ENGS = ("sync", "tensor", "vector", "scalar", "gpsimd")


class Ctx:
    def balloc(self, name, shape, dt):
        if name not in self.stage_cache:
            self.stage_cache[name] = self.stage_stack.enter_context(
                self.nc.sbuf_tensor("s%d_%s" % (self.stage_id, name), shape, dt))
        return self.stage_cache[name]

    def balloc2(self, name, shape, dt):
        if name not in self.stage_cache:
            self.stage_cache[name] = DBuf(self.stage_stack.enter_context(
                self.nc.sbuf_tensor("s%d_%s" % (self.stage_id, name), shape, dt)))
        return self.stage_cache[name]


def run_stage(ctx, fn):
    snap = [(s, s.c) for s in ctx.all_sems]
    ctx.stage_cache = {}
    ctx.stage_id = getattr(ctx, "stage_id", -1) + 1
    ctx.stage_stack = contextlib.ExitStack()
    with ctx.nc.Block() as blk:
        for eng in ENGS:
            for s, v in snap:
                s.c = v
            getattr(blk, eng)(_section(fn, ctx, eng))
    ctx.stage_stack.close()
    ctx.stage_cache = None


def _section(fn, ctx, eng):
    def sec(e):
        fn(ctx, e, eng)
    return sec


def mk_EW(e, eng, sems_all=None, own=None):
    last = [0]

    def E(tgt, make, *incs):
        if eng == tgt:
            if own is not None and own.c > last[0]:
                e.wait_ge(own.h, own.c)
                last[0] = own.c
            r = make()
            for (s, n) in incs:
                r = r.then_inc(s.h, n)
        for (s, n) in incs:
            s.c += n

    def W(tgt, s, v=None):
        if eng == tgt:
            e.wait_ge(s.h, s.c if v is None else v)

    def B():
        for s in (sems_all or ()):
            if s.c:
                e.wait_ge(s.h, s.c)
    return E, W, B



class DBuf:

    def __init__(self, t):
        self.t = t

    def __getitem__(self, sl):
        return _Slot(self.t, sl)


class _Slot:
    def __init__(self, t, sl):
        self.t = t
        self.sl = sl

    def __getitem__(self, idx):
        p, f = idx
        return self.t[p, self.sl, f]


def _bsel2(apat, stp, which, hb, nh):
    nch = apat.shape[1]
    ng = nch // (2 * stp)
    r = apat.rearrange("p (g t c) j -> p g t c j", g=ng, t=2, c=stp)
    sel = r[:, :, which, :, :]
    if ng >= nh:
        gp = ng // nh
        return sel[:, hb * gp:(hb + 1) * gp, :, :]
    cp = stp // nh
    return sel[:, :, hb * cp:(hb + 1) * cp, :]


def _bsel(apat, stp, which):
    nch = apat.shape[1]
    ng = nch // (2 * stp)
    r = apat.rearrange("p (g t c) j -> p g t c j", g=ng, t=2, c=stp)
    return r[:, :, which, :, :]



def build_nc(cfg, stages=('0', 'A1', 'A2', 'B', 'C', 'D')):
    cd = derived(cfg)
    T, D, H, I = cfg["T"], cfg["D"], cfg["H"], cfg["I"]
    DC, IC, OT, NT, TT, NG, KB = (cd["DC"], cd["IC"], cd["OT"], cd["NT"],
                                  cd["TT"], cd["NG"], cd["KB"])
    WBLK, DBLK, WBI, OBI = cd["WBLK"], cd["DBLK"], cd["WBI"], cd["OBI"]
    AIT, AIT2, FMT, NM = cd["AIT"], cd["AIT2"], cd["FMT"], cd["NM"]
    HPB = WBLK // 128
    C1 = float(1.0 / np.sqrt(D))
    C2 = float(1.0 / np.sqrt(I))
    L1 = int(np.log2(DC))
    L2 = int(np.log2(IC))

    nc = bass.Bass("TRN2", target_bir_lowering=False)
    ctx = Ctx()
    ctx.nc = nc
    stk = contextlib.ExitStack()

    ap = {}
    for name, shape, dt in host_input_specs(cfg):
        ap[name] = nc.dram_tensor(name, shape, dt, kind="ExternalInput").ap()
    ap["out"] = nc.dram_tensor("out", [OT, D], F32, kind="ExternalOutput").ap()
    for name, shape, dt in [
            ("kscr_h", [H, 128, T], BF16), ("kscr_l", [H, 128, T], BF16),
            ("vscr", [T, D], F32),
            ("qscr_h", [H, 128, OT], BF16), ("qscr_l", [H, 128, OT], BF16),
            ("xnscr_h", [DC, 128, OT], BF16), ("xnscr_l", [DC, 128, OT], BF16),
            ("wt_h", [3, DC, 128, D], BF16), ("wt_l", [3, DC, 128, D], BF16),
            ("wot_h", [DC, 128, D], BF16), ("wot_l", [DC, 128, D], BF16),
            ("wscr_g", [DC, 128, I], FP8), ("wscr_u", [DC, 128, I], FP8),
            ("wscr_d", [IC, 128, D], FP8),
            ("hscr_h", [H, 128, OT], BF16), ("hscr_l", [H, 128, OT], BF16),
            ("oscr", [OT, D], F32)]:
        ap[name] = nc.dram_tensor(name, shape, dt, kind="Internal").ap()

    sems = {}
    for name in ("sT", "sV", "sA", "sP",
                 "dC", "dT", "dT2", "dW0", "dW1", "dX0", "dX1",
                 "dK0", "dK1", "dQ", "dD",
                 "gW0", "gW1", "gK0", "gK1", "gV0", "gV1", "gX",
                 "gH0", "gH1", "gO0", "gO1", "gU0", "gU1"):
        sems[name] = Sm(stk.enter_context(nc.semaphore(name)))
    ctx.all_sems = list(sems.values())
    ctx.own_map = {"tensor": sems["sT"], "vector": sems["sV"],
                   "scalar": sems["sA"], "gpsimd": sems["sP"], "sync": None}
    sT, sV, sA, sP = (sems[k] for k in ("sT", "sV", "sA", "sP"))
    dC, dT, dT2, dQ, dD, gX = (sems[k] for k in ("dC", "dT", "dT2", "dQ", "dD", "gX"))
    dW = [sems["dW0"], sems["dW1"]]
    dX = [sems["dX0"], sems["dX1"]]
    dK = [sems["dK0"], sems["dK1"]]
    gW = [sems["gW0"], sems["gW1"]]
    gK = [sems["gK0"], sems["gK1"]]
    gV = [sems["gV0"], sems["gV1"]]
    gH = [sems["gH0"], sems["gH1"]]
    gO = [sems["gO0"], sems["gO1"]]
    gU = [sems["gU0"], sems["gU1"]]

    g_ = {}
    for name, shape, dt in [
            ("hmat", [128, 128], BF16), ("id32", [128, 128], F32),
            ("idbf", [128, 128], BF16), ("cbias", [128, NT * H], F32),
            ("linv", [128, NT * H], F32), ("qn2", [128, NT * H], F32),
            ("kn2", [128, H * TT], F32), ("kmax", [128, H], F32),
            ("wsg", [128, 1], F32), ("wsu", [128, 1], F32),
            ("wsd", [128, 1], F32), ("kc0", [128, 2], F32), ("ones1", [1, 128], F32)]:
        g_[name] = stk.enter_context(nc.sbuf_tensor("g_" + name, shape, dt))

    ps = stk.enter_context(nc.psum_tensor("ps", [128, 7, 512], F32))
    psb = stk.enter_context(nc.psum_tensor("psb", [128, 1024], BF16))

    def s0(ctx, e, eng):
        E, W, B = mk_EW(e, eng, ctx.all_sems, ctx.own_map[eng])
        B()
        for n in ("hmat", "id32", "idbf"):
            E("sync", lambda n=n: e.dma_start(out=g_[n][:, :], in_=ap[n][:, :]),
              (dC, 16))
        E("vector", lambda: e.memset(g_["ones1"][0:1, :], 1.0), (sV, 1))
        E("vector", lambda: e.memset(g_["kc0"][:, 0:1], 0.0), (sV, 1))
        E("vector", lambda: e.memset(g_["kc0"][:, 1:2], float(EPS)), (sV, 1))
        if True:
            win = ctx.balloc2("w0_in", [128, 2, max(I, D)], F32)
            whi = ctx.balloc2("w0_hi", [128, 2, D], BF16)
            wlo = ctx.balloc2("w0_lo", [128, 2, D], BF16)
            wt8 = ctx.balloc2("w0_t8", [128, 2, max(I, D)], FP8)
            wle = ctx.balloc("w0_le", [128, max(I, D)], F32)
            wms = ctx.balloc("w0_ms", [128, max(DC, IC)], F32)
            wsc = ctx.balloc("w0_sc", [128, 8], F32)
            whn = ctx.balloc("w0_hn", [128, 2], F32)
            srcs = [("wqT", "wt_h", "wt_l", 0), ("wkT", "wt_h", "wt_l", 1),
                    ("wvT", "wt_h", "wt_l", 2), ("woT", "wot_h", "wot_l", None)]
            winck = {0: 0, 1: 0}
            for wi, (src, dh_, dl_, idx) in enumerate(srcs):
                for rc in range(DC):
                    it = wi * DC + rc
                    sl = it % 2
                    W("sync", sV, winck[sl])
                    E("sync", lambda src=src, rc=rc, sl=sl: e.dma_start(
                        out=win[sl][:, 0:D],
                        in_=ap[src][rc * 128:(rc + 1) * 128, :]), (dW[sl], 16))
                    W("vector", dW[sl])
                    W("vector", gW[sl])
                    E("vector", lambda sl=sl: e.tensor_copy(
                        whi[sl][:, :], win[sl][:, 0:D]), (sV, 1))
                    W("vector", sV)
                    E("vector", lambda sl=sl: e.tensor_sub(
                        wlo[sl][:, :], win[sl][:, 0:D], whi[sl][:, :]), (sV, 1))
                    winck[sl] = sV.c
                    W("gpsimd", sV)
                    if idx is None:
                        dsth, dstl = ap[dh_][rc], ap[dl_][rc]
                    else:
                        dsth, dstl = ap[dh_][idx, rc], ap[dl_][idx, rc]
                    E("gpsimd", lambda d=dsth, sl=sl: e.dma_start(
                        out=d[:, :], in_=whi[sl][:, :]), (gW[sl], 16))
                    E("gpsimd", lambda d=dstl, sl=sl: e.dma_start(
                        out=d[:, :], in_=wlo[sl][:, :]), (gW[sl], 16))

            for (src, dst, wsn, NR, NW) in [
                    ("wgT", "wscr_g", "wsg", DC, I),
                    ("wuT", "wscr_u", "wsu", DC, I),
                    ("wdT", "wscr_d", "wsd", IC, D)]:
                for rc in range(NR):
                    sl = rc % 2
                    W("sync", sV, winck[sl])
                    E("sync", lambda src=src, rc=rc, sl=sl, NW=NW: e.dma_start(
                        out=win[sl][:, 0:NW],
                        in_=ap[src][rc * 128:(rc + 1) * 128, :]), (dW[sl], 16))
                    W("vector", dW[sl])
                    E("vector", lambda rc=rc, sl=sl, NW=NW: e.tensor_reduce(
                        wms[:, rc:rc + 1], win[sl][:, 0:NW], axis=AX.X,
                        op=Op.add, apply_absolute_value=True), (sV, 1))
                    winck[sl] = sV.c
                W("vector", sV)
                E("vector", lambda NR=NR: e.tensor_reduce(
                    wsc[:, 0:1], wms[:, 0:NR], axis=AX.X, op=Op.add), (sV, 1))
                W("tensor", sV)
                W("tensor", dC)
                E("tensor", lambda: e.transpose(
                    ps[0:1, 6, 0:128], wsc[:, 0:1], g_["id32"][:, :]), (sT, 1))
                W("vector", sT)
                E("vector", lambda: e.tensor_reduce(
                    wsc[0:1, 1:2], ps[0:1, 6, 0:128], axis=AX.X, op=Op.add), (sV, 1))
                W("vector", sV)
                E("vector", lambda NR=NR, NW=NW: e.tensor_scalar(
                    wsc[0:1, 2:3], wsc[0:1, 1:2], 1.0 / (NR * 128 * NW), EPS,
                    op0=Op.mult, op1=Op.max), (sV, 1))
                W("tensor", sV)
                E("tensor", lambda: e.matmul(
                    ps[:, 6, 0:1], g_["ones1"][0:1, :], wsc[0:1, 2:3],
                    start=True, stop=True), (sT, 1))
                W("scalar", sT)
                E("scalar", lambda wsn=wsn: e.activation(
                    g_[wsn][:, 0:1], ps[:, 6, 0:1], AF.Copy, bias=0.0,
                    scale=1.0), (sA, 1))
                W("vector", sA)
                E("vector", lambda wsn=wsn: e.tensor_scalar(
                    whn[:, 0:1], g_[wsn][:, 0:1], 0.5, None, op0=Op.mult), (sV, 1))
                W("vector", sV)
                E("vector", lambda: e.tensor_scalar(
                    whn[:, 1:2], whn[:, 0:1], -1.0, None, op0=Op.mult), (sV, 1))
                for rc in range(NR):
                    sl = rc % 2
                    W("sync", sV, winck[sl])
                    E("sync", lambda src=src, rc=rc, sl=sl, NW=NW: e.dma_start(
                        out=win[sl][:, 0:NW],
                        in_=ap[src][rc * 128:(rc + 1) * 128, :]), (dW[sl], 16))
                    W("vector", dW[sl])
                    W("vector", sV)
                    E("vector", lambda sl=sl, NW=NW: e.tensor_scalar(
                        wle[:, 0:NW], win[sl][:, 0:NW], whn[:, 1:2], None,
                        op0=Op.is_le), (sV, 1))
                    W("vector", sV)
                    W("vector", gW[sl])
                    E("vector", lambda sl=sl, NW=NW: e.scalar_tensor_tensor(
                        wt8[sl][:, 0:NW], win[sl][:, 0:NW], whn[:, 0:1],
                        wle[:, 0:NW], op0=Op.is_ge, op1=Op.subtract), (sV, 1))
                    winck[sl] = sV.c
                    W("gpsimd", sV)
                    E("gpsimd", lambda dst=dst, rc=rc, sl=sl, NW=NW: e.dma_start(
                        out=ap[dst][rc], in_=wt8[sl][:, 0:NW]), (gW[sl], 16))

    if '0' in stages:
        run_stage(ctx, s0)

    def make_sA(own):
        xsrc = "xo" if own else "xf"
        tab = "tabq" if own else "tabk"
        ait = AIT2 if own else AIT
        niter = (OT if own else T) // ait
        tpi = ait // 128

        def sA_(ctx, e, eng):
            E, W, B = mk_EW(e, eng, ctx.all_sems, ctx.own_map[eng])
            B()
            W("tensor", dC)
            if True:
                ax = ctx.balloc2("a_x", [128, 2, D], F32)
                axn = ctx.balloc("a_xn", [128, D], F32)
                ast = ctx.balloc("a_st", [128, 8], F32)
                axh = ctx.balloc("a_xh", [128, DC * ait], BF16)
                axl = ctx.balloc("a_xl", [128, DC * ait], BF16)
                awh = ctx.balloc2("a_wh", [128, 2, DC * WBLK], BF16)
                awl = ctx.balloc2("a_wl", [128, 2, DC * WBLK], BF16)
                act_ = ctx.balloc("a_ct", [128, tpi * 64], F32)
                astb = ctx.balloc("a_stb", [128, tpi * 64], F32)
                ajnk = ctx.balloc("a_jnk", [128, D], F32)
                aro = ctx.balloc("a_ro", [128, 128], F32)
                arh = ctx.balloc("a_rh", [128, 128], BF16)
                arl = ctx.balloc("a_rl", [128, 128], BF16)
                ake = ctx.balloc2("a_ke", [128, 2, 256], BF16)
                ave = ctx.balloc2("a_ve", [128, 2, WBLK], F32)
                anrm = ctx.balloc("a_nrm", [128, tpi * H], F32)
                wtick = {0: 0, 1: 0}
                ketick = {0: 0, 1: 0}
                vetick = {0: 0, 1: 0}
                abct = {"ke": 0, "ve": 0}

                def rope_head(bank, tc, gt, hglob):
                    x0 = ps[:, bank, (hglob % HPB) * 128:(hglob % HPB) * 128 + 64]
                    x1 = ps[:, bank, (hglob % HPB) * 128 + 64:(hglob % HPB) * 128 + 128]
                    ct = act_[:, tc * 64:(tc + 1) * 64]
                    st_ = astb[:, tc * 64:(tc + 1) * 64]
                    W("vector", sT)
                    E("vector", lambda: e.tensor_mul(ajnk[:, 0:64], x0, ct), (sV, 1))
                    E("vector", lambda: e.tensor_mul(ajnk[:, 64:128], x1, st_), (sV, 1))
                    E("vector", lambda: e.tensor_mul(ajnk[:, 128:192], x0, st_), (sV, 1))
                    E("vector", lambda: e.tensor_mul(ajnk[:, 192:256], x1, ct), (sV, 1))
                    W("vector", sV)
                    E("vector", lambda: e.tensor_sub(
                        aro[:, 0:64], ajnk[:, 0:64], ajnk[:, 64:128]), (sV, 1))
                    E("vector", lambda: e.tensor_add(
                        aro[:, 64:128], ajnk[:, 128:192], ajnk[:, 192:256]), (sV, 1))
                    W("vector", sV)
                    if own:
                        ncol = g_["qn2"][:, gt * H + hglob: gt * H + hglob + 1]
                    else:
                        ncol = g_["kn2"][:, hglob * TT + gt: hglob * TT + gt + 1]
                    E("vector", lambda ncol=ncol: e.scalar_tensor_tensor(
                        ajnk[:, 0:128], aro[:, :], 1.0, aro[:, :],
                        op0=Op.mult, op1=Op.mult, accum_out=ncol), (sV, 1))
                    E("vector", lambda: e.tensor_copy(arh[:, :], aro[:, :]), (sV, 1))
                    W("vector", sV)
                    E("vector", lambda: e.tensor_sub(arl[:, :], aro[:, :], arh[:, :]),
                      (sV, 1))
                    W("tensor", sV)
                    W("tensor", sA)
                    E("tensor", lambda: e.transpose(
                        psb[:, 0:128], arh[:, :], g_["idbf"][:, :]))
                    E("tensor", lambda: e.transpose(
                        psb[:, 128:256], arl[:, :], g_["idbf"][:, :]), (sT, 1))
                    kesl = abct["ke"] % 2
                    abct["ke"] += 1
                    W("scalar", sT)
                    W("scalar", gK[kesl])
                    E("scalar", lambda: e.activation(
                        ake[kesl][:, :], psb[:, 0:256], AF.Copy, bias=0.0,
                        scale=1.0), (sA, 1))
                    W("gpsimd", sA)
                    kdst = "qscr" if own else "kscr"
                    E("gpsimd", lambda: e.dma_start(
                        out=ap[kdst + "_h"][hglob][:, gt * 128:(gt + 1) * 128],
                        in_=ake[kesl][:, 0:128]), (gK[kesl], 16))
                    E("gpsimd", lambda: e.dma_start(
                        out=ap[kdst + "_l"][hglob][:, gt * 128:(gt + 1) * 128],
                        in_=ake[kesl][:, 128:256]), (gK[kesl], 16))

                def proj_tile(it, tc, blk, sl, isrope, xtick):
                    bank = 4 + (tc % 2)
                    gt = it * tpi + tc
                    W("tensor", dW[sl])
                    W("tensor", sV, xtick[tc])
                    W("tensor", sV)
                    W("tensor", sA)
                    k, n3 = 0, 3 * DC
                    for dc in range(DC):
                        c0 = dc * ait + tc * 128
                        xh_t = axh[:, c0:c0 + 128]
                        xl_t = axl[:, c0:c0 + 128]
                        wh_t = awh[sl][:, dc * WBLK:(dc + 1) * WBLK]
                        wl_t = awl[sl][:, dc * WBLK:(dc + 1) * WBLK]
                        for (sx, mv) in ((xh_t, wh_t), (xh_t, wl_t), (xl_t, wh_t)):
                            k += 1
                            E("tensor", lambda sx=sx, mv=mv, bank=bank, k=k, n3=n3:
                              e.matmul(ps[:, bank, 0:WBLK], sx, mv, start=(k == 1),
                                       stop=(k == n3)),
                                *([(sT, 1)] if k == n3 else []))
                    if isrope:
                        for hh in range(HPB):
                            rope_head(bank, tc, gt, blk * HPB + hh)
                    else:
                        vsl = abct["ve"] % 2
                        abct["ve"] += 1
                        W("scalar", sT)
                        W("scalar", gV[vsl])
                        E("scalar", lambda: e.activation(
                            ave[vsl][:, :], ps[:, bank, 0:WBLK], AF.Copy, bias=0.0,
                            scale=1.0), (sA, 1))
                        W("gpsimd", sA)
                        E("gpsimd", lambda: e.dma_start(
                            out=ap["vscr"][gt * 128:(gt + 1) * 128,
            blk * WBLK:(blk + 1) * WBLK],
                            in_=ave[vsl][:, :]), (gV[vsl], 16))

                for it in range(niter):
                    W("vector", sT)
                    W("vector", gX)
                    W("scalar", sT)
                    W("scalar", gX)
                    W("sync", sV)
                    E("sync", lambda it=it: e.dma_start(
                        out=act_[:, :],
                        in_=ap[tab][0][:, it * tpi * 64:(it + 1) * tpi * 64]),
                        (dT, 16))
                    E("sync", lambda it=it: e.dma_start(
                        out=ajnk[:, 0:tpi * 64],
                        in_=ap[tab][1][:, it * tpi * 64:(it + 1) * tpi * 64]),
                        (dT, 16))
                    W("vector", dT)
                    E("vector", lambda: e.tensor_add(
                        act_[:, :], act_[:, :], ajnk[:, 0:tpi * 64]), (sV, 1))
                    E("sync", lambda it=it: e.dma_start(
                        out=astb[:, :],
                        in_=ap[tab][2][:, it * tpi * 64:(it + 1) * tpi * 64]),
                        (dT2, 16))
                    W("sync", sV)
                    E("sync", lambda it=it: e.dma_start(
                        out=ajnk[:, 0:tpi * 64],
                        in_=ap[tab][3][:, it * tpi * 64:(it + 1) * tpi * 64]),
                        (dT2, 16))
                    W("vector", dT2)
                    E("vector", lambda: e.tensor_add(
                        astb[:, :], astb[:, :], ajnk[:, 0:tpi * 64]), (sV, 1))

                    import os as _os
                    _tpr = int(_os.environ.get("KLN", tpi) if _os.environ.get("KLN", "") != "off" else 0)
                    xtick = []
                    for tc in range(_tpr):
                        sl = tc % 2
                        if tc >= 2:
                            W("sync", sV, xtick[tc - 2])
                        E("sync", lambda it=it, tc=tc, sl=sl: e.dma_start(
                            out=ax[sl][:, :],
                            in_=ap[xsrc][(it * tpi + tc) * 128:
                                         (it * tpi + tc + 1) * 128, :]),
                            (dX[sl], 16))
                        _kcut = int(_os.environ.get("KCUT", 5))
                        if _kcut < 2:
                            xtick.append(sV.c)
                            continue
                        W("vector", dX[sl])
                        E("vector", lambda sl=sl: e.tensor_scalar(
                            ajnk[:, 0:D], ax[sl][:, :], 1.0, 0.0, op0=Op.mult,
                            op1=Op.add, accum_out=ast[:, 0:1]), (sV, 1))
                        W("vector", sV)
                        E("vector", lambda sl=sl: e.scalar_tensor_tensor(
                            ajnk[:, 0:D], ax[sl][:, :], 1.0, ax[sl][:, :],
                            op0=Op.mult, op1=Op.mult,
                            accum_out=ast[:, 1:2]), (sV, 1))
                        if _kcut < 3:
                            xtick.append(sV.c)
                            continue
                        W("vector", sV)
                        E("vector", lambda: e.tensor_scalar(
                            ast[:, 2:3], ast[:, 0:1], 1.0 / D, None, op0=Op.mult),
                            (sV, 1))
                        W("vector", sV)
                        E("vector", lambda: e.tensor_scalar(
                            ast[:, 3:4], ast[:, 1:2], 1.0 / D, None, op0=Op.mult),
                            (sV, 1))
                        W("vector", sV)
                        E("vector", lambda: e.scalar_tensor_tensor(
                            ast[:, 4:5], ast[:, 2:3], ast[:, 2:3], ast[:, 3:4],
                            op0=Op.mult, op1=Op.subtract), (sV, 1))
                        W("scalar", sV)
                        E("scalar", lambda: e.activation(
                            ast[:, 5:6], ast[:, 4:5], AF.Sqrt,
                            bias=g_["kc0"][:, 1:2], scale=-1.0), (sA, 1))
                        W("vector", sA)
                        E("vector", lambda: e.reciprocal(ast[:, 6:7], ast[:, 5:6]),
                          (sV, 1))
                        W("vector", sV)
                        E("vector", lambda sl=sl: e.tensor_scalar(
                            axn[:, :], ax[sl][:, :], ast[:, 2:3], ast[:, 6:7],
                            op0=Op.subtract, op1=Op.mult), (sV, 1))
                        if _kcut < 4:
                            xtick.append(sV.c)
                            continue
                        nb = (DC + 3) // 4
                        for bk in range(nb):
                            cnt = min(4, DC - bk * 4)
                            W("tensor", sV)
                            W("tensor", sA)
                            for q in range(cnt):
                                dc = bk * 4 + q
                                E("tensor", lambda dc=dc, bk=bk, q=q: e.transpose(
                                    ps[:, bk % 4, q * 128:(q + 1) * 128],
                                    axn[:, dc * 128:(dc + 1) * 128],
                                    g_["id32"][:, :]),
                                    *([(sT, 1)] if q == cnt - 1 else []))
                            for q in range(cnt):
                                if _kcut < 5:
                                    continue
                                dc = bk * 4 + q
                                col = dc * ait + tc * 128
                                if _os.environ.get("KEVAC", "dve") == "dve":
                                    W("vector", sT)
                                    E("vector", lambda bk=bk, q=q, col=col:
                                      e.tensor_copy(
                                          axh[:, col:col + 128],
                                          ps[:, bk % 4, q * 128:(q + 1) * 128]),
                                        (sA, 1))
                                else:
                                    W("scalar", sT)
                                    E("scalar", lambda bk=bk, q=q, col=col: e.activation(
                                        axh[:, col:col + 128],
                                        ps[:, bk % 4, q * 128:(q + 1) * 128],
                                        AF.Copy, bias=0.0, scale=1.0), (sA, 1))
                                W("vector", sA)
                                E("vector", lambda bk=bk, q=q, col=col:
                                  e.scalar_tensor_tensor(
                                      axl[:, col:col + 128],
                                      ps[:, bk % 4, q * 128:(q + 1) * 128], 0.0,
                                      axh[:, col:col + 128], op0=Op.add,
                                      op1=Op.subtract), (sV, 1))
                        xtick.append(sV.c)
                    if own:
                        W("gpsimd", sV)
                        for dc in range(DC):
                            E("gpsimd", lambda dc=dc, it=it: e.dma_start(
                                out=ap["xnscr_h"][dc][:, it * ait:(it + 1) * ait],
                                in_=axh[:, dc * ait:(dc + 1) * ait]), (gX, 16))
                            E("gpsimd", lambda dc=dc, it=it: e.dma_start(
                                out=ap["xnscr_l"][dc][:, it * ait:(it + 1) * ait],
                                in_=axl[:, dc * ait:(dc + 1) * ait]), (gX, 16))

                    projs = [(0, True)] if own else [(1, True), (2, False)]
                    import os
                    _pj = os.environ.get("KPROJ", "")
                    if _pj == "none":
                        projs = []
                    elif _pj == "k" and not own:
                        projs = [(1, True)]
                    elif _pj == "v" and not own:
                        projs = [(2, False)]
                    for (widx, isrope) in projs:
                        for blk in range(DBLK):
                            gwi = (it * 3 + widx) * DBLK + blk
                            sl = gwi % 2
                            W("sync", sT, wtick[sl])
                            for part, buf in (("wt_h", awh), ("wt_l", awl)):
                                E("sync", lambda part=part, widx=widx, blk=blk,
                                  buf=buf, sl=sl: e.dma_start(
                                      out=buf[sl][:, :].rearrange(
            "p (c j) -> p c j", c=DC),
                                      in_=ap[part][widx][:, :,
            blk * WBLK:(blk + 1) * WBLK]
                                      .rearrange("c p j -> p c j")), (dW[sl], 16))
                            for tc in range(tpi):
                                proj_tile(it, tc, blk, sl, isrope, xtick)
                            wtick[sl] = sT.c
        return sA_

    if 'A1' in stages:
        run_stage(ctx, make_sA(False))
    if 'A2' in stages:
        run_stage(ctx, make_sA(True))

    def sB(ctx, e, eng):
        E, W, B = mk_EW(e, eng, ctx.all_sems, ctx.own_map[eng])
        B()
        W("tensor", dC)
        NGKB = NG * KB
        if True:
            bmk = ctx.balloc("b_mk", [128, NT * KB], BF16)
            bqh = ctx.balloc("b_qh", [128, H * 256], BF16)
            bql = ctx.balloc("b_ql", [128, H * 256], BF16)
            bkh = ctx.balloc2("b_kh", [128, 2, NGKB], BF16)
            bkl = ctx.balloc2("b_kl", [128, 2, NGKB], BF16)
            bv = ctx.balloc2("b_v", [128, 2, NGKB], F32)
            bp = ctx.balloc("b_p", [128, 2 * KB], F32)
            bpt = ctx.balloc("b_pt", [128, 2 * KB], F32)
            blt = ctx.balloc("b_lt", [128, 2 * (NG + 2)], F32)
            bsc = ctx.balloc("b_sc", [128, 4], F32)
            blb = ctx.balloc("b_lb", [128, 256], F32)
            bhb = ctx.balloc2("b_hb", [128, 2, 256], F32)
            bhe = ctx.balloc2("b_he", [128, 2, 256], BF16)
            for jt in range(NT):
                E("sync", lambda jt=jt: e.dma_start(
                    out=bmk[:, jt * KB:(jt + 1) * KB], in_=ap["masks"][jt]),
                    (dC, 16))
            W("vector", dC)
            W("sync", gK[0])
            W("sync", gK[1])
            W("sync", gV[0])
            W("sync", gV[1])
            W("sync", gX)
            for h in range(H):
                E("vector", lambda h=h: e.tensor_reduce(
                    g_["kmax"][:, h:h + 1],
                    g_["kn2"][:, h * TT:(h + 1) * TT], axis=AX.X, op=Op.max),
                    (sV, 1))
            W("tensor", sV)
            E("tensor", lambda: e.transpose(
                ps[0:H, 6, 0:128], g_["kmax"][:, 0:H], g_["id32"][:, :]), (sT, 1))
            W("vector", sT)
            E("vector", lambda: e.tensor_reduce(
                bsc[0:H, 0:1], ps[0:H, 6, 0:128], axis=AX.X, op=Op.max), (sV, 1))
            W("scalar", sV)
            E("scalar", lambda: e.activation(
                bsc[0:H, 1:2], bsc[0:H, 0:1], AF.Sqrt, bias=g_["kc0"][0:H, 0:1],
                scale=1.0), (sA, 1))
            W("vector", sA)
            E("vector", lambda: e.tensor_scalar(
                bsc[0:H, 2:3], bsc[0:H, 1:2], -1.02, None, op0=Op.mult), (sV, 1))
            W("tensor", sV)
            E("tensor", lambda: e.transpose(
                ps[0:1, 6, 0:H], bsc[0:H, 2:3], g_["id32"][0:H, 0:H]), (sT, 1))
            W("scalar", sT)
            E("scalar", lambda: e.activation(
                bsc[0:1, 0:H] if H <= 4 else blb[0:1, 0:H],
                ps[0:1, 6, 0:H], AF.Copy, bias=0.0, scale=1.0), (sA, 1))
            W("tensor", sA)
            E("tensor", lambda: e.matmul(
                ps[:, 6, 0:H], g_["ones1"][0:1, :],
                bsc[0:1, 0:H] if H <= 4 else blb[0:1, 0:H],
                start=True, stop=True), (sT, 1))
            W("scalar", sT)
            E("scalar", lambda: e.activation(
                g_["kmax"][:, 0:H], ps[:, 6, 0:H], AF.Copy, bias=0.0,
                scale=1.0), (sA, 1))
            E("scalar", lambda: e.activation(
                g_["cbias"][:, 0:NT * H], g_["qn2"][:, 0:NT * H], AF.Sqrt,
                bias=g_["kc0"][:, 0:1], scale=1.0), (sA, 1))
            W("vector", sA)
            for jt in range(NT):
                E("vector", lambda jt=jt: e.tensor_mul(
                    g_["cbias"][:, jt * H:(jt + 1) * H],
                    g_["cbias"][:, jt * H:(jt + 1) * H], g_["kmax"][:, 0:H]),
                    (sV, 1))

            qtick = 0
            kvtick = {0: 0, 1: 0}
            avfree = {0: 0, 1: 0}
            hetick = {0: 0, 1: 0}
            hect = 0
            for g in range(NG):
                kb = kb_of_group(g)
                kbK = kb * KB
                W("sync", sT, qtick)
                for h in range(H):
                    E("sync", lambda g=g, h=h: e.dma_start(
                        out=bqh[:, h * 256:(h + 1) * 256],
                        in_=ap["qscr_h"][h][:, g * 256:(g + 1) * 256]), (dQ, 16))
                    E("sync", lambda g=g, h=h: e.dma_start(
                        out=bql[:, h * 256:(h + 1) * 256],
                        in_=ap["qscr_l"][h][:, g * 256:(g + 1) * 256]), (dQ, 16))
                for h in range(H):
                    sl = h % 2
                    if g * H + h >= 2:
                        W("sync", sT, kvtick[sl])
                    E("sync", lambda h=h, kbK=kbK, sl=sl: e.dma_start(
                        out=bkh[sl][:, 0:kbK], in_=ap["kscr_h"][h][:, 0:kbK]),
                        (dK[sl], 16))
                    E("sync", lambda h=h, kbK=kbK, sl=sl: e.dma_start(
                        out=bkl[sl][:, 0:kbK], in_=ap["kscr_l"][h][:, 0:kbK]),
                        (dK[sl], 16))
                    nvt = kbK // 128
                    for v0 in range(0, nvt, 8):
                        nv = min(8, nvt - v0)
                        E("sync", lambda h=h, v0=v0, nv=nv, sl=sl: e.dma_start(
                            out=bv[sl][:, v0 * 128:(v0 + nv) * 128].rearrange(
                                "p (n j) -> p n j", n=nv),
                            in_=ap["vscr"][v0 * 128:(v0 + nv) * 128,
            h * 128:(h + 1) * 128].rearrange(
                                "(n p) j -> p n j", p=128)), (dK[sl], 16))
                    avbank = 4 + sl
                    first_av = True
                    for kbi in range(kb):
                        for qt in range(2):
                            bnk = qt * 2 + (kbi % 2)
                            jt = 2 * g + qt
                            W("tensor", dQ)
                            W("tensor", dK[sl])
                            W("tensor", sA)
                            W("tensor", sV)
                            if first_av:
                                W("tensor", sV, avfree[sl])
                            qh_t = bqh[:, h * 256 + qt * 128: h * 256 + qt * 128 + 128]
                            ql_t = bql[:, h * 256 + qt * 128: h * 256 + qt * 128 + 128]
                            kh_t = bkh[sl][:, kbi * KB:(kbi + 1) * KB]
                            kl_t = bkl[sl][:, kbi * KB:(kbi + 1) * KB]
                            k = 0
                            for (sx, mv) in ((qh_t, kh_t), (qh_t, kl_t), (ql_t, kh_t)):
                                k += 1
                                E("tensor", lambda sx=sx, mv=mv, bnk=bnk, k=k:
                                  e.matmul(ps[:, bnk, 0:KB], sx, mv,
            start=(k == 1), stop=(k == 3)),
                                    *([(sT, 1)] if k == 3 else []))
                            if kbi == kb - 1:
                                W("vector", sT)
                                E("vector", lambda bnk=bnk, jt=jt: e.tensor_add(
                                    ps[:, bnk, 0:KB], ps[:, bnk, 0:KB],
                                    bmk[:, jt * KB:(jt + 1) * KB]), (sV, 1))
                            W("scalar", sT)
                            W("scalar", sV)
                            E("scalar", lambda bnk=bnk, qt=qt, jt=jt, kbi=kbi:
                              e.activation(
                                  bp[:, qt * KB:(qt + 1) * KB], ps[:, bnk, 0:KB],
                                  AF.Exp, bias=g_["cbias"][:, jt * H + h:jt * H + h + 1],
                                  scale=1.0,
                                  accum_out=blt[:, qt * (NG + 2) + kbi + 1:
            qt * (NG + 2) + kbi + 2]), (sA, 1))
                        tbank = 6
                        nkc = KB // 128
                        for kc in range(nkc):
                            if kc % 2 == 0:
                                W("tensor", sA)
                            for qt in range(2):
                                E("tensor", lambda kc=kc, qt=qt, tbank=tbank:
                                  e.transpose(
                                      ps[:, tbank, (kc % 2) * 256 + qt * 128:
            (kc % 2) * 256 + qt * 128 + 128],
                                      bp[:, qt * KB + kc * 128:
            qt * KB + (kc + 1) * 128],
                                      g_["id32"][:, :]),
                                    *([(sT, 1)] if qt == 1 else []))
                            if kc % 2 == 1 or kc == nkc - 1:
                                W("scalar", sT)
                                E("scalar", lambda kc=kc, tbank=tbank: e.activation(
                                    bpt[:, (kc - kc % 2) * 256:(kc + 1) * 256],
                                    ps[:, tbank, 0:(kc % 2 + 1) * 256],
                                    AF.Copy, bias=0.0, scale=1.0), (sA, 1))
                        W("tensor", sA)
                        for kc in range(nkc):
                            kt_glob = kbi * nkc + kc
                            vt = bv[sl][:, kt_glob * 128:(kt_glob + 1) * 128]
                            lastmm = (kbi == kb - 1 and kc == nkc - 1)
                            E("tensor", lambda vt=vt, kc=kc, avbank=avbank,
                              first_av=first_av, lastmm=lastmm: e.matmul(
                                  ps[:, avbank, 0:256], vt,
                                  bpt[:, kc * 256:(kc + 1) * 256],
                                  start=first_av, stop=lastmm),
                                *([(sT, 1)] if lastmm else []))
                            first_av = False
                    kvtick[sl] = sT.c
                    W("vector", sA)
                    for qt in range(2):
                        jt = 2 * g + qt
                        E("vector", lambda qt=qt, kb=kb: e.tensor_reduce(
                            blt[:, qt * (NG + 2):qt * (NG + 2) + 1],
                            blt[:, qt * (NG + 2) + 1:qt * (NG + 2) + 1 + kb],
                            axis=AX.X, op=Op.add), (sV, 1))
                        W("vector", sV)
                        E("vector", lambda qt=qt, jt=jt, h=h: e.reciprocal(
                            g_["linv"][:, jt * H + h:jt * H + h + 1],
                            blt[:, qt * (NG + 2):qt * (NG + 2) + 1]), (sV, 1))
                    W("tensor", sV)
                    W("tensor", sA)
                    for qt in range(2):
                        jt = 2 * g + qt
                        E("tensor", lambda jt=jt, h=h, qt=qt: e.transpose(
                            ps[0:1, 3, qt * 128:qt * 128 + 128],
                            g_["linv"][:, jt * H + h:jt * H + h + 1],
                            g_["id32"][:, :]), *([(sT, 1)] if qt == 1 else []))
                    W("scalar", sT)
                    E("scalar", lambda: e.activation(
                        blb[0:1, 0:256], ps[0:1, 3, 0:256], AF.Copy, bias=0.0,
                        scale=1.0), (sA, 1))
                    W("tensor", sA)
                    E("tensor", lambda: e.matmul(
                        ps[:, 3, 0:256], g_["ones1"][0:1, :], blb[0:1, 0:256],
                        start=True, stop=True), (sT, 1))
                    W("scalar", sT)
                    E("scalar", lambda: e.activation(
                        blb[:, 0:256], ps[:, 3, 0:256], AF.Copy, bias=0.0,
                        scale=1.0), (sA, 1))
                    hesl = hect % 2
                    hect += 1
                    W("vector", sA)
                    W("vector", sT)
                    W("vector", gH[hesl])
                    E("vector", lambda avbank=avbank, hesl=hesl: e.tensor_mul(
                        bhb[hesl][:, :], ps[:, avbank, 0:256], blb[:, 0:256]),
                        (sV, 1))
                    avfree[sl] = sV.c
                    W("vector", sV)
                    E("vector", lambda hesl=hesl: e.tensor_copy(
                        bhe[hesl][:, :], bhb[hesl][:, :]), (sV, 1))
                    W("vector", sV)
                    E("vector", lambda hesl=hesl: e.tensor_sub(
                        bhb[hesl][:, :], bhb[hesl][:, :], bhe[hesl][:, :]), (sV, 1))
                    W("gpsimd", sV)
                    E("gpsimd", lambda h=h, g=g, hesl=hesl: e.dma_start(
                        out=ap["hscr_h"][h][:, g * 256:(g + 1) * 256],
                        in_=bhe[hesl][:, :]), (gH[hesl], 16))
                    E("gpsimd", lambda h=h, g=g, hesl=hesl: e.dma_start(
                        out=ap["hscr_l"][h][:, g * 256:(g + 1) * 256],
                        in_=bhb[hesl][:, :]), (gH[hesl], 16))
                qtick = sT.c

    if 'B' in stages:
        run_stage(ctx, sB)

    def sC(ctx, e, eng):
        E, W, B = mk_EW(e, eng, ctx.all_sems, ctx.own_map[eng])
        B()
        W("tensor", dC)
        TCB = min(512, OT)
        NCB = OT // TCB
        TPC = TCB // 128
        if True:
            chh = ctx.balloc("c_hh", [128, H * TCB], BF16)
            chl = ctx.balloc("c_hl", [128, H * TCB], BF16)
            cwh = ctx.balloc2("c_wh", [128, 2, H * WBLK], BF16)
            cwl = ctx.balloc2("c_wl", [128, 2, H * WBLK], BF16)
            cx = ctx.balloc("c_x", [128, TPC * D], F32)
            co = ctx.balloc2("c_o", [128, 2, WBLK], F32)
            wtick = {0: 0, 1: 0}
            oct_ = 0
            W("sync", gH[0])
            W("sync", gH[1])
            for tcb in range(NCB):
                W("sync", sT)
                W("sync", sV)
                for h in range(H):
                    E("sync", lambda h=h, tcb=tcb: e.dma_start(
                        out=chh[:, h * TCB:(h + 1) * TCB],
                        in_=ap["hscr_h"][h][:, tcb * TCB:(tcb + 1) * TCB]),
                        (dX[0], 16))
                    E("sync", lambda h=h, tcb=tcb: e.dma_start(
                        out=chl[:, h * TCB:(h + 1) * TCB],
                        in_=ap["hscr_l"][h][:, tcb * TCB:(tcb + 1) * TCB]),
                        (dX[0], 16))
                for tc in range(TPC):
                    E("sync", lambda tcb=tcb, tc=tc: e.dma_start(
                        out=cx[:, tc * D:(tc + 1) * D],
                        in_=ap["xo"][(tcb * TPC + tc) * 128:
                                     (tcb * TPC + tc + 1) * 128, :]), (dX[0], 16))
                for dob in range(DBLK):
                    sl = (tcb * DBLK + dob) % 2
                    if tcb * DBLK + dob >= 2:
                        W("sync", sT, wtick[sl])
                    for part, buf in (("wot_h", cwh), ("wot_l", cwl)):
                        E("sync", lambda part=part, dob=dob, buf=buf, sl=sl:
                          e.dma_start(
                              out=buf[sl][:, :].rearrange("p (c j) -> p c j", c=H),
                              in_=ap[part][:, :, dob * WBLK:(dob + 1) * WBLK]
                              .rearrange("c p j -> p c j")), (dW[sl], 16))
                    for tc in range(TPC):
                        bank = tc % 2
                        osl = oct_ % 2
                        oct_ += 1
                        W("tensor", dW[sl])
                        W("tensor", dX[0])
                        W("tensor", sV)
                        k, n3 = 0, 3 * H
                        for dc in range(H):
                            hh_t = chh[:, dc * TCB + tc * 128: dc * TCB + tc * 128 + 128]
                            hl_t = chl[:, dc * TCB + tc * 128: dc * TCB + tc * 128 + 128]
                            wh_t = cwh[sl][:, dc * WBLK:(dc + 1) * WBLK]
                            wl_t = cwl[sl][:, dc * WBLK:(dc + 1) * WBLK]
                            for (sx, mv) in ((hh_t, wh_t), (hh_t, wl_t), (hl_t, wh_t)):
                                k += 1
                                E("tensor", lambda sx=sx, mv=mv, bank=bank, k=k,
                                  n3=n3: e.matmul(
                                      ps[:, bank, 0:WBLK], sx, mv, start=(k == 1),
                                      stop=(k == n3)),
                                    *([(sT, 1)] if k == n3 else []))
                        W("vector", sT)
                        W("vector", dX[0])
                        W("vector", gO[osl])
                        E("vector", lambda bank=bank, tc=tc, dob=dob, osl=osl:
                          e.tensor_add(
                              co[osl][:, :], ps[:, bank, 0:WBLK],
                              cx[:, tc * D + dob * WBLK: tc * D + (dob + 1) * WBLK]),
                            (sV, 1))
                        W("gpsimd", sV)
                        E("gpsimd", lambda tcb=tcb, tc=tc, dob=dob, osl=osl:
                          e.dma_start(
                              out=ap["oscr"][(tcb * TPC + tc) * 128:
            (tcb * TPC + tc + 1) * 128,
            dob * WBLK:(dob + 1) * WBLK],
                              in_=co[osl][:, :]), (gO[osl], 16))
                    wtick[sl] = sT.c

    if 'C' in stages:
        run_stage(ctx, sC)

    def sD(ctx, e, eng):
        E, W, B = mk_EW(e, eng, ctx.all_sems, ctx.own_map[eng])
        B()
        W("tensor", dC)
        OC = IC
        HQ = min(2048, max(I // 2, D))
        NQ2 = I // HQ
        LVL = [float(1 << k) for k in range(7, -1, -1)]
        CONST = []
        c = 0.0
        for t in LVL:
            CONST.append(-(t + c))
            c += t / 2.0
        CFIN = c
        if True:
            dxh = ctx.balloc("d_xh", [128, DC * FMT], BF16)
            dxl = ctx.balloc("d_xl", [128, DC * FMT], BF16)
            df1 = ctx.balloc("d_f1", [128, DC * FMT], F32)
            dsc2 = ctx.balloc("d_sc2", [128, HQ], F32)
            dtm = ctx.balloc("d_tm", [128, D], F32)
            dq1f = ctx.balloc2("d_q1f", [128, 2, max(DC * FMT, HQ)], BF16)
            dsc = ctx.balloc("d_dsc", [128, 16], F32)
            dldr = ctx.balloc("d_ldr", [128, 8], F32)
            dwg = ctx.balloc2("d_wg", [128, 2, DC * WBI], FP8)
            dwu = ctx.balloc2("d_wu", [128, 2, DC * WBI], FP8)
            dt1 = ctx.balloc("d_t1", [128, WBI], F32)
            dt2 = ctx.balloc("d_t2", [128, WBI], F32)
            dytm = ctx.balloc("d_ytm", [128, I], F32)
            dyfm = ctx.balloc("d_yfm", [128, IC * FMT], F32)
            dysh = ctx.balloc("d_ysh", [128, 2 * FMT], BF16)
            dysl = ctx.balloc("d_ysl", [128, 2 * FMT], BF16)
            dq2t = ctx.balloc("d_q2t", [128, I // 2], BF16)
            dq2f = ctx.balloc("d_q2f", [128, IC * FMT], BF16)
            dwd = ctx.balloc2("d_wd", [128, 2, (IC // 2) * WBLK], FP8)
            dos = ctx.balloc("d_os", [128, WBLK], F32)
            dou = ctx.balloc2("d_ou", [128, 1, WBLK], F32)

            for k in range(8):
                E("vector", lambda k=k: e.memset(dldr[:, k:k + 1], CONST[k]),
                  (sV, 1))

            def ladder(zbuf, wid, bits, outq, scvec, qoff):
                E("vector", lambda: e.tensor_scalar(
                    zbuf, zbuf, scvec, -128.0, op0=Op.mult, op1=Op.max), (sV, 1))
                E("vector", lambda: e.tensor_scalar(
                    zbuf, zbuf, 127.49, 128.5, op0=Op.min, op1=Op.add), (sV, 1))
                E("vector", lambda wid=wid: e.tensor_scalar(
                    dsc2[:, 0:wid], zbuf, 1.0, None, op0=Op.mult), (sV, 1))
                for k in range(8):
                    W("scalar", sV)
                    E("scalar", lambda k=k: e.activation(
                        bits, zbuf, AF.Sign, bias=dldr[:, k:k + 1], scale=1.0),
                        (sA, 1))
                    W("vector", sA)
                    E("vector", lambda k=k: e.scalar_tensor_tensor(
                        zbuf, bits, -LVL[k] / 2.0, zbuf, op0=Op.mult, op1=Op.add),
                        (sV, 1))
                E("vector", lambda wid=wid: e.tensor_sub(
                    zbuf, dsc2[:, 0:wid], zbuf), (sV, 1))
                E("vector", lambda: e.tensor_scalar(
                    outq, zbuf, CFIN - 128.0, 127.0, op0=Op.add, op1=Op.min),
                    (sV, 1))

            xn_tick = 0
            y_tick = {}
            bank_sV = {}

            def bankw(b):
                W("tensor", sA, y_tick.get(b, 0))
                W("tensor", sV, bank_sV.get(b, 0))
            gu_bank_tick = {}
            wg_tick = {0: 0, 1: 0}
            wd_tick = {0: 0, 1: 0}
            dq2f_tick = 0
            dos_tick = 0
            dou_tick = 0
            for mi in range(NM):
                t0 = mi * FMT
                xsl = mi % 2
                W("sync", sT, xn_tick)
                for dc in range(DC):
                    E("sync", lambda dc=dc, t0=t0: e.dma_start(
                        out=dxh[:, dc * FMT:(dc + 1) * FMT],
                        in_=ap["xnscr_h"][dc][:, t0:t0 + FMT]), (dX[0], 16))
                    E("sync", lambda dc=dc, t0=t0: e.dma_start(
                        out=dxl[:, dc * FMT:(dc + 1) * FMT],
                        in_=ap["xnscr_l"][dc][:, t0:t0 + FMT]), (dX[0], 16))
                for dc in range(DC):
                    bank = dc % 2
                    W("tensor", dX[0])
                    bankw(bank)
                    E("tensor", lambda dc=dc, bank=bank: e.matmul(
                        ps[:, bank, 0:FMT], g_["hmat"][:, :],
                        dxh[:, dc * FMT:(dc + 1) * FMT], start=True, stop=False))
                    E("tensor", lambda dc=dc, bank=bank: e.matmul(
                        ps[:, bank, 0:FMT], g_["hmat"][:, :],
                        dxl[:, dc * FMT:(dc + 1) * FMT], start=False, stop=True),
                        (sT, 1))
                    if dc == DC - 1:
                        xn_tick = sT.c
                    W("scalar", sT)
                    W("scalar", sV)
                    E("scalar", lambda dc=dc, bank=bank: e.activation(
                        df1[:, dc * FMT:(dc + 1) * FMT], ps[:, bank, 0:FMT],
                        AF.Copy, bias=0.0, scale=1.0), (sA, 1))
                    y_tick[bank] = sA.c
                ap1 = df1[:, :].rearrange("p (c j) -> p c j", c=DC)
                np1 = DC // 2
                for lv in range(L1):
                    stp = 1 << lv
                    W("vector", sA if lv == 0 else sV)
                    E("vector", lambda stp=stp: e.tensor_copy(
                        dsc2[:, 0:np1 * FMT].rearrange("p (c j) -> p c j", c=np1),
                        _bsel(ap1, stp, 1)), (sV, 1))
                    E("vector", lambda stp=stp: e.tensor_sub(
                        _bsel(ap1, stp, 1), _bsel(ap1, stp, 0), _bsel(ap1, stp, 1)),
                        (sV, 1))
                    E("vector", lambda stp=stp: e.tensor_add(
                        _bsel(ap1, stp, 0), _bsel(ap1, stp, 0),
                        dsc2[:, 0:np1 * FMT].rearrange("p (c j) -> p c j", c=np1)),
                        (sV, 1))
                for dc in range(DC):
                    bank = 4 + ((dc // 4) % 2)
                    q = dc % 4
                    if q == 0:
                        W("tensor", sV)
                        bankw(bank)
                    E("tensor", lambda dc=dc, bank=bank, q=q: e.transpose(
                        ps[:, bank, q * 128:(q + 1) * 128],
                        df1[:, dc * FMT:(dc + 1) * FMT], g_["id32"][:, :]),
                        *([(sT, 1)] if (q == 3 or dc == DC - 1) else []))
                    if q == 3 or dc == DC - 1:
                        W("scalar", sT)
                        E("scalar", lambda dc=dc, bank=bank, q=q: e.activation(
                            dtm[:, (dc - q) * 128:(dc + 1) * 128],
                            ps[:, bank, 0:(q + 1) * 128], AF.Copy, bias=0.0,
                            scale=1.0), (sA, 1))
                        y_tick[bank] = sA.c
                W("vector", sA)
                E("vector", lambda: e.tensor_reduce(
                    dsc[:, 0:1], dtm[:, :], axis=AX.X, op=Op.max,
                    apply_absolute_value=True), (sV, 1))
                E("vector", lambda: e.tensor_scalar(
                    dsc[:, 1:2], dsc[:, 0:1], C1, EPS, op0=Op.mult, op1=Op.max),
                    (sV, 1))
                E("vector", lambda: e.tensor_scalar(
                    dsc[:, 1:2], dsc[:, 1:2], 1.0 / 127.0, None, op0=Op.mult),
                    (sV, 1))
                E("vector", lambda: e.reciprocal(dsc[:, 2:3], dsc[:, 1:2]), (sV, 1))
                E("vector", lambda: e.tensor_scalar(
                    dsc[:, 2:3], dsc[:, 2:3], C1, None, op0=Op.mult), (sV, 1))
                E("vector", lambda: e.tensor_scalar(
                    dsc[:, 3:4], dsc[:, 1:2], g_["wsg"][:, 0:1], None, op0=Op.mult),
                    (sV, 1))
                E("vector", lambda: e.tensor_scalar(
                    dsc[:, 4:5], dsc[:, 1:2], g_["wsu"][:, 0:1], None, op0=Op.mult),
                    (sV, 1))
                ladder(dtm[:, :], D, dq1f[xsl][:, 0:D], dq2t[:, 0:D],
                       dsc[:, 2:3], 0)
                for dc in range(DC):
                    half = (dc // 4) % 2
                    q = dc % 4
                    if q == 0:
                        W("tensor", sV)
                        W("tensor", sA, gu_bank_tick.get(6, 0))
                        W("tensor", sA, gu_bank_tick.get(7, 0))
                    E("tensor", lambda dc=dc, half=half, q=q: e.transpose(
                        psb[:, half * 512 + q * 128: half * 512 + (q + 1) * 128],
                        dq2t[:, dc * 128:(dc + 1) * 128], g_["idbf"][:, :]),
                        *([(sT, 1)] if (q == 3 or dc == DC - 1) else []))
                    if q == 3 or dc == DC - 1:
                        W("scalar", sT)
                        for qq in range(q + 1):
                            dcq = dc - q + qq
                            E("scalar", lambda dcq=dcq, half=half, qq=qq, xsl=xsl:
                              e.activation(
                                  dq1f[xsl][:, dcq * FMT:(dcq + 1) * FMT],
                                  psb[:, half * 512 + qq * 128:
                                      half * 512 + (qq + 1) * 128], AF.Copy,
                                  bias=0.0, scale=1.0),
                                *([(sA, 1)] if qq == q else []))
                        gu_bank_tick[6 + half] = sA.c
                dq1f_ready = sA.c
                for ob in range(OBI):
                    wsl = (mi * OBI + ob) % 2
                    if mi * OBI + ob >= 2:
                        W("sync", sT, wg_tick[wsl])
                    E("sync", lambda ob=ob, wsl=wsl: e.dma_start(
                        out=dwg[wsl][:, :].rearrange("p (c j) -> p c j", c=DC),
                        in_=ap["wscr_g"][:, :, ob * WBI:(ob + 1) * WBI]
                        .rearrange("c p j -> p c j")), (dW[wsl], 16))
                    E("sync", lambda ob=ob, wsl=wsl: e.dma_start(
                        out=dwu[wsl][:, :].rearrange("p (c j) -> p c j", c=DC),
                        in_=ap["wscr_u"][:, :, ob * WBI:(ob + 1) * WBI]
                        .rearrange("c p j -> p c j")), (dW[wsl], 16))
                    gbank = 2 if ob % 2 == 0 else 4
                    ubank = 3 if ob % 2 == 0 else 5
                    for (wbuf, bank) in ((dwg, gbank), (dwu, ubank)):
                        W("tensor", dW[wsl])
                        W("tensor", sA, dq1f_ready)
                        bankw(bank)
                        k = 0
                        for dc in range(DC):
                            k += 1
                            E("tensor", lambda wbuf=wbuf, dc=dc, bank=bank, k=k,
                              wsl=wsl, xsl=xsl: e.matmul(
                                  ps[:, bank, 0:WBI],
                                  dq1f[xsl][:, dc * FMT:(dc + 1) * FMT],
                                  wbuf[wsl][:, dc * WBI:(dc + 1) * WBI],
                                  start=(k == 1), stop=(k == DC)),
                                *([(sT, 1)] if k == DC else []))
                    wg_tick[wsl] = sT.c
                    W("scalar", sT)
                    W("scalar", sV)
                    E("scalar", lambda gbank=gbank: e.activation(
                        dt1[:, :], ps[:, gbank, 0:WBI], AF.Silu,
                        bias=g_["kc0"][:, 0:1], scale=dsc[:, 3:4]), (sA, 1))
                    E("scalar", lambda ubank=ubank: e.activation(
                        dt2[:, :], ps[:, ubank, 0:WBI], AF.Identity,
                        bias=g_["kc0"][:, 0:1], scale=dsc[:, 4:5]), (sA, 1))
                    y_tick[gbank] = sA.c
                    y_tick[ubank] = sA.c
                    W("vector", sA)
                    if ob == 0:
                        W("vector", sT, y_tick.get("ytm", 0))
                    E("vector", lambda ob=ob: e.tensor_mul(
                        dytm[:, ob * WBI:(ob + 1) * WBI], dt1[:, :], dt2[:, :]),
                        (sV, 1))
                for ic in range(IC):
                    bank = 4 + ((ic // 4) % 2)
                    q = ic % 4
                    if q == 0:
                        W("tensor", sV)
                        bankw(bank)
                    E("tensor", lambda ic=ic, bank=bank, q=q: e.transpose(
                        ps[:, bank, q * 128:(q + 1) * 128],
                        dytm[:, ic * 128:(ic + 1) * 128], g_["id32"][:, :]),
                        *([(sT, 1)] if (q == 3 or ic == IC - 1) else []))
                    if q == 3 or ic == IC - 1:
                        if ic == IC - 1:
                            y_tick["ytm"] = sT.c
                        W("scalar", sT)
                        E("scalar", lambda ic=ic, bank=bank, q=q: e.activation(
                            dyfm[:, (ic - q) * FMT:(ic + 1) * FMT],
                            ps[:, bank, 0:(q + 1) * 128], AF.Copy, bias=0.0,
                            scale=1.0), (sA, 1))
                        y_tick[bank] = sA.c
                for c0 in range(0, IC, 2):
                    ng = min(2, IC - c0)
                    W("vector", sA)
                    E("vector", lambda c0=c0, ng=ng: e.tensor_copy(
                        dysh[:, 0:ng * FMT], dyfm[:, c0 * FMT:(c0 + ng) * FMT]),
                        (sV, 1))
                    E("vector", lambda c0=c0, ng=ng: e.tensor_sub(
                        dysl[:, 0:ng * FMT], dyfm[:, c0 * FMT:(c0 + ng) * FMT],
                        dysh[:, 0:ng * FMT]), (sV, 1))
                    for ci in range(ng):
                        ic = c0 + ci
                        bank = ic % 2
                        W("tensor", sV)
                        bankw(bank)
                        E("tensor", lambda ci=ci, bank=bank: e.matmul(
                            ps[:, bank, 0:FMT], g_["hmat"][:, :],
                            dysh[:, ci * FMT:(ci + 1) * FMT], start=True,
                            stop=False))
                        E("tensor", lambda ci=ci, bank=bank: e.matmul(
                            ps[:, bank, 0:FMT], g_["hmat"][:, :],
                            dysl[:, ci * FMT:(ci + 1) * FMT], start=False,
                            stop=True), (sT, 1))
                        W("scalar", sT)
                        E("scalar", lambda ic=ic, bank=bank: e.activation(
                            dyfm[:, ic * FMT:(ic + 1) * FMT], ps[:, bank, 0:FMT],
                            AF.Copy, bias=0.0, scale=1.0), (sA, 1))
                        y_tick[bank] = sA.c
                ap2 = dyfm[:, :].rearrange("p (c j) -> p c j", c=IC)
                np2 = IC // 2
                nh = (np2 * FMT + HQ - 1) // HQ
                per = np2 // nh
                for lv in range(L2):
                    stp = 1 << lv
                    ng2 = IC // (2 * stp)
                    if ng2 >= nh:
                        tg, tc_ = ng2 // nh, stp
                    else:
                        tg, tc_ = ng2, stp // nh

                    def tview():
                        return dsc2[:, 0:tg * tc_ * FMT].rearrange(
                            "p (g c j) -> p g c j", g=tg, c=tc_)
                    W("vector", sA if lv == 0 else sV)
                    for hb in range(nh):
                        E("vector", lambda stp=stp, hb=hb, tview=tview:
                          e.tensor_copy(tview(), _bsel2(ap2, stp, 1, hb, nh)),
                            (sV, 1))
                        E("vector", lambda stp=stp, hb=hb: e.tensor_sub(
                            _bsel2(ap2, stp, 1, hb, nh),
                            _bsel2(ap2, stp, 0, hb, nh),
                            _bsel2(ap2, stp, 1, hb, nh)), (sV, 1))
                        E("vector", lambda stp=stp, hb=hb, tview=tview:
                          e.tensor_add(_bsel2(ap2, stp, 0, hb, nh),
                                       _bsel2(ap2, stp, 0, hb, nh), tview()),
                            (sV, 1))
                for ic in range(IC):
                    bank = 4 + ((ic // 4) % 2)
                    q = ic % 4
                    if q == 0:
                        W("tensor", sV)
                        bankw(bank)
                    E("tensor", lambda ic=ic, bank=bank, q=q: e.transpose(
                        ps[:, bank, q * 128:(q + 1) * 128],
                        dyfm[:, ic * FMT:(ic + 1) * FMT], g_["id32"][:, :]),
                        *([(sT, 1)] if (q == 3 or ic == IC - 1) else []))
                    if q == 3 or ic == IC - 1:
                        W("scalar", sT)
                        W("scalar", sV)
                        E("scalar", lambda ic=ic, bank=bank, q=q: e.activation(
                            dytm[:, (ic - q) * 128:(ic + 1) * 128],
                            ps[:, bank, 0:(q + 1) * 128], AF.Copy, bias=0.0,
                            scale=1.0), (sA, 1))
                        y_tick[bank] = sA.c
                W("vector", sA)
                E("vector", lambda: e.tensor_reduce(
                    dsc[:, 8:9], dytm[:, :], axis=AX.X, op=Op.max,
                    apply_absolute_value=True), (sV, 1))
                E("vector", lambda: e.tensor_scalar(
                    dsc[:, 9:10], dsc[:, 8:9], C2, EPS, op0=Op.mult, op1=Op.max),
                    (sV, 1))
                E("vector", lambda: e.tensor_scalar(
                    dsc[:, 9:10], dsc[:, 9:10], 1.0 / 127.0, None, op0=Op.mult),
                    (sV, 1))
                E("vector", lambda: e.reciprocal(dsc[:, 10:11], dsc[:, 9:10]),
                  (sV, 1))
                E("vector", lambda: e.tensor_scalar(
                    dsc[:, 10:11], dsc[:, 10:11], C2, None, op0=Op.mult), (sV, 1))
                E("vector", lambda: e.tensor_scalar(
                    dsc[:, 11:12], dsc[:, 9:10], g_["wsd"][:, 0:1], None,
                    op0=Op.mult), (sV, 1))
                for hf in range(2):
                    for qd in range(I // 2 // HQ):
                        o0 = hf * (I // 2) + qd * HQ
                        ladder(dytm[:, o0:o0 + HQ], HQ,
                               dq1f[xsl][:, 0:HQ],
                               dq2t[:, qd * HQ:(qd + 1) * HQ], dsc[:, 10:11], 0)
                    for ici in range(IC // 2):
                        ic = hf * (IC // 2) + ici
                        half = (ici // 4) % 2
                        q = ici % 4
                        if q == 0:
                            W("tensor", sV)
                            W("tensor", sA, gu_bank_tick.get(6, 0))
                        W("tensor", sA, gu_bank_tick.get(7, 0))
                        E("tensor", lambda ici=ici, half=half, q=q: e.transpose(
                            psb[:, half * 512 + q * 128: half * 512 + (q + 1) * 128],
                            dq2t[:, ici * 128:(ici + 1) * 128], g_["idbf"][:, :]),
                            *([(sT, 1)] if (q == 3 or ici == IC // 2 - 1) else []))
                        if q == 3 or ici == IC // 2 - 1:
                            W("scalar", sT)
                            W("scalar", sT, dq2f_tick)
                            for qq in range(q + 1):
                                icq = ic - q + qq
                                E("scalar", lambda icq=icq, half=half, qq=qq:
                                  e.activation(
                                      dq2f[:, icq * FMT:(icq + 1) * FMT],
                                      psb[:, half * 512 + qq * 128:
                                          half * 512 + (qq + 1) * 128], AF.Copy,
                                      bias=0.0, scale=1.0),
                                    *([(sA, 1)] if qq == q else []))
                            gu_bank_tick[6 + half] = sA.c
                for dob in range(DBLK):
                    W("sync", sV, dos_tick)
                    E("sync", lambda mi=mi, dob=dob: e.dma_start(
                        out=dos[:, :],
                        in_=ap["oscr"][mi * 128:(mi + 1) * 128,
                                       dob * WBLK:(dob + 1) * WBLK]), (dX[1], 16))
                    bank = dob % 2
                    for hw_ in range(2):
                        W("sync", sT, wd_tick[hw_])
                        E("sync", lambda dob=dob, hw_=hw_: e.dma_start(
                            out=dwd[hw_][:, :].rearrange(
                                "p (c j) -> p c j", c=IC // 2),
                            in_=ap["wscr_d"][hw_ * (IC // 2):(hw_ + 1) * (IC // 2),
                                             :, dob * WBLK:(dob + 1) * WBLK]
                            .rearrange("c p j -> p c j")), (dK[hw_], 16))
                    for hw_ in range(2):
                        W("tensor", dK[hw_])
                        if hw_ == 0:
                            W("tensor", sA)
                            bankw(bank)
                        k = 0
                        for ici in range(IC // 2):
                            ic = hw_ * (IC // 2) + ici
                            k += 1
                            E("tensor", lambda ic=ic, ici=ici, bank=bank, k=k,
                              hw_=hw_, xsl=xsl: e.matmul(
                                  ps[:, bank, 0:WBLK],
                                  dq2f[:, ic * FMT:(ic + 1) * FMT],
                                  dwd[hw_][:, ici * WBLK:(ici + 1) * WBLK],
                                  start=(hw_ == 0 and k == 1),
                                  stop=(hw_ == 1 and k == IC // 2)),
                                *([(sT, 1)] if (hw_ == 1 and k == IC // 2)
                                  else []))
                        wd_tick[hw_] = sT.c
                    if dob == DBLK - 1:
                        dq2f_tick = sT.c
                    W("vector", sT)
                    W("vector", dX[1])
                    W("vector", gU[0])
                    E("vector", lambda bank=bank: e.scalar_tensor_tensor(
                        dou[0][:, :], ps[:, bank, 0:WBLK], dsc[:, 11:12],
                        dos[:, :], op0=Op.mult, op1=Op.add), (sV, 1))
                    dos_tick = sV.c
                    bank_sV[bank] = sV.c
                    W("gpsimd", sV)
                    E("gpsimd", lambda mi=mi, dob=dob: e.dma_start(
                        out=ap["out"][mi * 128:(mi + 1) * 128,
                                      dob * WBLK:(dob + 1) * WBLK],
                        in_=dou[0][:, :]), (gU[0], 16))
            W("gpsimd", gU[0])

    if 'D' in stages:
        run_stage(ctx, sD)

    stk.close()
    return nc



_NC_CACHE = {}


def kernel(**inputs):
    cfg = default_cfg()
    cd = derived(cfg)
    if "nc" not in _NC_CACHE:
        _NC_CACHE["nc"] = build_nc(cfg)
    nc = _NC_CACHE["nc"]
    in_maps = [host_core_inputs(cfg, inputs, c) for c in range(cfg["n_cores"])]
    res = run_bass_kernel_spmd(nc, in_maps, core_ids=list(range(cfg["n_cores"])))
    B, T, D = cfg["B"], cfg["T"], cfg["D"]
    out = np.empty((B, T, D), dtype=np.float32)
    for c in range(cfg["n_cores"]):
        b, p = c // cd["CPB"], c % cd["CPB"]
        o = np.asarray(res.results[c]["out"])
        for jt in range(cd["NT"]):
            out[b, (2 * jt + p) * 128:(2 * jt + p + 1) * 128, :] = \
                o[jt * 128:(jt + 1) * 128, :]
    return out


# version 07
# speedup vs baseline: 1.1324x; 1.0086x over previous
import contextlib
import numpy as np
import ml_dtypes
import concourse.bass as bass
import concourse.mybir as mybir
from concourse.alu_op_type import AluOpType as Op
from concourse.bass_utils import run_bass_kernel_spmd
from concourse import library_config

F32 = mybir.dt.float32
BF16 = mybir.dt.bfloat16
FP8 = mybir.dt.float8e4
AF = mybir.ActivationFunctionType
AX = mybir.AxisListType

NEG = np.float32(-3.0e38)
EPS = 1e-5


def default_cfg():
    return dict(B=4, T=4096, D=2048, H=16, I=8192, n_cores=8)


def derived(cfg):
    d = dict(cfg)
    T, D, I = cfg["T"], cfg["D"], cfg["I"]
    d["DC"] = D // 128
    d["IC"] = I // 128
    d["OT"] = T // 2
    d["NT"] = d["OT"] // 128
    d["TT"] = T // 128
    d["NG"] = max(d["NT"] // 2, 1)
    d["KB"] = min(512, T)
    d["WBLK"] = min(512, D)
    d["DBLK"] = D // d["WBLK"]
    d["WBI"] = min(512, I)
    d["OBI"] = I // d["WBI"]
    d["AIT"] = min(1024, T)
    d["AIT2"] = min(1024, d["OT"])
    d["FMT"] = 128
    d["NM"] = d["OT"] // d["FMT"]
    d["CPB"] = cfg["n_cores"] // cfg["B"]
    return d


def kb_of_group(g):
    return g + 1



def _hadamard(n):
    h = np.array([[1.0]], dtype=np.float64)
    while h.shape[0] < n:
        h = np.block([[h, h], [h, -h]])
    return h


def _rope_tables(positions, scale):
    DH = 128
    j = np.arange(DH // 2, dtype=np.float64)
    theta = 10000.0 ** (-2.0 * j / DH)
    ang = positions.astype(np.float64)[:, None] * theta[None, :]
    out = []
    for t in (np.cos(ang) * scale, np.sin(ang) * scale):
        hi = t.astype(np.float32)
        lo = (t - hi).astype(np.float32)
        out += [hi, lo]
    return out


def _tile_major(a):
    T = a.shape[0]
    return np.ascontiguousarray(
        a.reshape(T // 128, 128, 64).transpose(1, 0, 2).reshape(128, -1))


def _deint_perm(D):
    DH = 128
    perm = []
    for h0 in range(0, D, DH):
        perm += [h0 + 2 * j for j in range(DH // 2)]
        perm += [h0 + 2 * j + 1 for j in range(DH // 2)]
    return np.array(perm)


def host_core_inputs(cfg, inputs, core, names=None):
    cd = derived(cfg)
    T, D = cfg["T"], cfg["D"]
    b, p = core // cd["CPB"], core % cd["CPB"]
    nm = names or {}
    X = np.asarray(inputs[nm.get("X", "L0_attn_X")])
    Wq = np.asarray(inputs[nm.get("Wq", "L0_attn_Wq")])
    Wk = np.asarray(inputs[nm.get("Wk", "L0_attn_Wk")])
    Wv = np.asarray(inputs[nm.get("Wv", "L0_attn_Wv")])
    Wo = np.asarray(inputs[nm.get("Wo", "L0_attn_Wo")])
    Wg = np.asarray(inputs[nm.get("Wg", "L0_ffn_w_gate")])
    Wu = np.asarray(inputs[nm.get("Wu", "L0_ffn_w_up")])
    Wd = np.asarray(inputs[nm.get("Wd", "L0_ffn_w_down")])
    own_rows = np.concatenate(
        [np.arange((2 * jt + p) * 128, (2 * jt + p + 1) * 128)
         for jt in range(cd["NT"])])
    perm = _deint_perm(D)

    m = {}
    m["xf"] = np.ascontiguousarray(X[b], dtype=np.float32)
    m["xo"] = np.ascontiguousarray(X[b][own_rows], dtype=np.float32)
    m["wqT"] = np.ascontiguousarray(Wq[perm].T, dtype=np.float32)
    m["wkT"] = np.ascontiguousarray(Wk[perm].T, dtype=np.float32)
    m["wvT"] = np.ascontiguousarray(Wv.T, dtype=np.float32)
    m["woT"] = np.ascontiguousarray(Wo.T, dtype=np.float32)
    m["wgT"] = np.ascontiguousarray(Wg.T, dtype=np.float32)
    m["wuT"] = np.ascontiguousarray(Wu.T, dtype=np.float32)
    m["wdT"] = np.ascontiguousarray(Wd.T, dtype=np.float32)

    kt = _rope_tables(np.arange(T), 1.0)
    m["tabk"] = np.stack([_tile_major(x) for x in kt], 0)
    qt = _rope_tables(own_rows, 128.0 ** -0.5)
    m["tabq"] = np.stack([_tile_major(x) for x in qt], 0)

    masks = np.zeros((cd["NT"], 128, cd["KB"]), dtype=np.float32)
    for jt in range(cd["NT"]):
        gq, g = 2 * jt + p, jt // 2
        k0 = (kb_of_group(g) - 1) * cd["KB"]
        i = np.arange(128)[:, None]
        jj = np.arange(cd["KB"])[None, :]
        masks[jt] = np.where(k0 + jj > gq * 128 + i, NEG, 0.0)
    m["masks"] = masks.astype(ml_dtypes.bfloat16)

    m["hmat"] = _hadamard(128).astype(ml_dtypes.bfloat16)
    m["id32"] = np.eye(128, dtype=np.float32)
    m["idbf"] = np.eye(128).astype(ml_dtypes.bfloat16)
    return m


def host_input_specs(cfg):
    cd = derived(cfg)
    T, D, I = cfg["T"], cfg["D"], cfg["I"]
    OT, NT, TT, KB = cd["OT"], cd["NT"], cd["TT"], cd["KB"]
    return [
        ("xf", [T, D], F32), ("xo", [OT, D], F32),
        ("wqT", [D, D], F32), ("wkT", [D, D], F32),
        ("wvT", [D, D], F32), ("woT", [D, D], F32),
        ("wgT", [D, I], F32), ("wuT", [D, I], F32), ("wdT", [I, D], F32),
        ("tabk", [4, 128, TT * 64], F32), ("tabq", [4, 128, NT * 64], F32),
        ("masks", [NT, 128, KB], BF16),
        ("hmat", [128, 128], BF16), ("id32", [128, 128], F32),
        ("idbf", [128, 128], BF16),
    ]



class Sm:
    __slots__ = ("h", "c")

    def __init__(self, h):
        self.h = h
        self.c = 0


ENGS = ("sync", "tensor", "vector", "scalar", "gpsimd")


class Ctx:
    def balloc(self, name, shape, dt):
        if name not in self.stage_cache:
            self.stage_cache[name] = self.stage_stack.enter_context(
                self.nc.sbuf_tensor("s%d_%s" % (self.stage_id, name), shape, dt))
        return self.stage_cache[name]

    def balloc2(self, name, shape, dt):
        if name not in self.stage_cache:
            self.stage_cache[name] = DBuf(self.stage_stack.enter_context(
                self.nc.sbuf_tensor("s%d_%s" % (self.stage_id, name), shape, dt)))
        return self.stage_cache[name]


def run_stage(ctx, fn):
    snap = [(s, s.c) for s in ctx.all_sems]
    ctx.stage_cache = {}
    ctx.stage_id = getattr(ctx, "stage_id", -1) + 1
    ctx.stage_stack = contextlib.ExitStack()
    with ctx.nc.Block() as blk:
        for eng in ENGS:
            for s, v in snap:
                s.c = v
            getattr(blk, eng)(_section(fn, ctx, eng))
    ctx.stage_stack.close()
    ctx.stage_cache = None


def _section(fn, ctx, eng):
    def sec(e):
        fn(ctx, e, eng)
    return sec


def mk_EW(e, eng, sems_all=None, own=None):
    last = [0]

    def E(tgt, make, *incs):
        if eng == tgt:
            if own is not None and own.c > last[0]:
                e.wait_ge(own.h, own.c)
                last[0] = own.c
            r = make()
            for (s, n) in incs:
                r = r.then_inc(s.h, n)
        for (s, n) in incs:
            s.c += n

    def W(tgt, s, v=None):
        if eng == tgt:
            e.wait_ge(s.h, s.c if v is None else v)

    def B():
        for s in (sems_all or ()):
            if s.c:
                e.wait_ge(s.h, s.c)
    return E, W, B



class DBuf:

    def __init__(self, t):
        self.t = t

    def __getitem__(self, sl):
        return _Slot(self.t, sl)


class _Slot:
    def __init__(self, t, sl):
        self.t = t
        self.sl = sl

    def __getitem__(self, idx):
        p, f = idx
        return self.t[p, self.sl, f]


def _bsel2(apat, stp, which, hb, nh):
    nch = apat.shape[1]
    ng = nch // (2 * stp)
    r = apat.rearrange("p (g t c) j -> p g t c j", g=ng, t=2, c=stp)
    sel = r[:, :, which, :, :]
    if ng >= nh:
        gp = ng // nh
        return sel[:, hb * gp:(hb + 1) * gp, :, :]
    cp = stp // nh
    return sel[:, :, hb * cp:(hb + 1) * cp, :]


def _bsel(apat, stp, which):
    nch = apat.shape[1]
    ng = nch // (2 * stp)
    r = apat.rearrange("p (g t c) j -> p g t c j", g=ng, t=2, c=stp)
    return r[:, :, which, :, :]



def build_nc(cfg, stages=('0', 'A1', 'A2', 'B', 'C', 'D')):
    cd = derived(cfg)
    T, D, H, I = cfg["T"], cfg["D"], cfg["H"], cfg["I"]
    DC, IC, OT, NT, TT, NG, KB = (cd["DC"], cd["IC"], cd["OT"], cd["NT"],
                                  cd["TT"], cd["NG"], cd["KB"])
    WBLK, DBLK, WBI, OBI = cd["WBLK"], cd["DBLK"], cd["WBI"], cd["OBI"]
    AIT, AIT2, FMT, NM = cd["AIT"], cd["AIT2"], cd["FMT"], cd["NM"]
    HPB = WBLK // 128
    C1 = float(1.0 / np.sqrt(D))
    C2 = float(1.0 / np.sqrt(I))
    L1 = int(np.log2(DC))
    L2 = int(np.log2(IC))

    nc = bass.Bass("TRN2", target_bir_lowering=False)
    ctx = Ctx()
    ctx.nc = nc
    stk = contextlib.ExitStack()

    ap = {}
    for name, shape, dt in host_input_specs(cfg):
        ap[name] = nc.dram_tensor(name, shape, dt, kind="ExternalInput").ap()
    ap["out"] = nc.dram_tensor("out", [OT, D], F32, kind="ExternalOutput").ap()
    for name, shape, dt in [
            ("kscr_h", [H, 128, T], BF16), ("kscr_l", [H, 128, T], BF16),
            ("vscr", [T, D], F32),
            ("qscr_h", [H, 128, OT], BF16), ("qscr_l", [H, 128, OT], BF16),
            ("xnscr_h", [DC, 128, OT], BF16), ("xnscr_l", [DC, 128, OT], BF16),
            ("wt_h", [3, DC, 128, D], BF16), ("wt_l", [3, DC, 128, D], BF16),
            ("wot_h", [DC, 128, D], BF16), ("wot_l", [DC, 128, D], BF16),
            ("wscr_g", [DC, 128, I], FP8), ("wscr_u", [DC, 128, I], FP8),
            ("wscr_d", [IC, 128, D], FP8),
            ("hscr_h", [H, 128, OT], BF16), ("hscr_l", [H, 128, OT], BF16),
            ("oscr", [OT, D], F32)]:
        ap[name] = nc.dram_tensor(name, shape, dt, kind="Internal").ap()

    sems = {}
    for name in ("sT", "sV", "sA", "sP",
                 "dC", "dT", "dT2", "dW0", "dW1", "dX0", "dX1",
                 "dK0", "dK1", "dQ", "dD",
                 "gW0", "gW1", "gK0", "gK1", "gV0", "gV1", "gX",
                 "gH0", "gH1", "gO0", "gO1", "gU0", "gU1"):
        sems[name] = Sm(stk.enter_context(nc.semaphore(name)))
    ctx.all_sems = list(sems.values())
    ctx.own_map = {"tensor": sems["sT"], "vector": sems["sV"],
                   "scalar": sems["sA"], "gpsimd": sems["sP"], "sync": None}
    sT, sV, sA, sP = (sems[k] for k in ("sT", "sV", "sA", "sP"))
    dC, dT, dT2, dQ, dD, gX = (sems[k] for k in ("dC", "dT", "dT2", "dQ", "dD", "gX"))
    dW = [sems["dW0"], sems["dW1"]]
    dX = [sems["dX0"], sems["dX1"]]
    dK = [sems["dK0"], sems["dK1"]]
    gW = [sems["gW0"], sems["gW1"]]
    gK = [sems["gK0"], sems["gK1"]]
    gV = [sems["gV0"], sems["gV1"]]
    gH = [sems["gH0"], sems["gH1"]]
    gO = [sems["gO0"], sems["gO1"]]
    gU = [sems["gU0"], sems["gU1"]]

    g_ = {}
    for name, shape, dt in [
            ("hmat", [128, 128], BF16), ("id32", [128, 128], F32),
            ("idbf", [128, 128], BF16), ("cbias", [128, NT * H], F32),
            ("linv", [128, NT * H], F32), ("qn2", [128, NT * H], F32),
            ("kn2", [128, H * TT], F32), ("kmax", [128, H], F32),
            ("wsg", [128, 1], F32), ("wsu", [128, 1], F32),
            ("wsd", [128, 1], F32), ("kc0", [128, 2], F32), ("ones1", [1, 128], F32)]:
        g_[name] = stk.enter_context(nc.sbuf_tensor("g_" + name, shape, dt))

    ps = stk.enter_context(nc.psum_tensor("ps", [128, 7, 512], F32))
    psb = stk.enter_context(nc.psum_tensor("psb", [128, 1024], BF16))

    def s0(ctx, e, eng):
        E, W, B = mk_EW(e, eng, ctx.all_sems, ctx.own_map[eng])
        B()
        for n in ("hmat", "id32", "idbf"):
            E("sync", lambda n=n: e.dma_start(out=g_[n][:, :], in_=ap[n][:, :]),
              (dC, 16))
        E("vector", lambda: e.memset(g_["ones1"][0:1, :], 1.0), (sV, 1))
        E("vector", lambda: e.memset(g_["kc0"][:, 0:1], 0.0), (sV, 1))
        E("vector", lambda: e.memset(g_["kc0"][:, 1:2], float(EPS)), (sV, 1))
        if True:
            win = ctx.balloc2("w0_in", [128, 2, max(I, D)], F32)
            whi = ctx.balloc2("w0_hi", [128, 2, D], BF16)
            wlo = ctx.balloc2("w0_lo", [128, 2, D], BF16)
            wt8 = ctx.balloc2("w0_t8", [128, 2, max(I, D)], FP8)
            wle = ctx.balloc("w0_le", [128, max(I, D)], F32)
            wms = ctx.balloc("w0_ms", [128, max(DC, IC)], F32)
            wsc = ctx.balloc("w0_sc", [128, 8], F32)
            whn = ctx.balloc("w0_hn", [128, 2], F32)
            srcs = [("wqT", "wt_h", "wt_l", 0), ("wkT", "wt_h", "wt_l", 1),
                    ("wvT", "wt_h", "wt_l", 2), ("woT", "wot_h", "wot_l", None)]
            winck = {0: 0, 1: 0}
            for wi, (src, dh_, dl_, idx) in enumerate(srcs):
                for rc in range(DC):
                    it = wi * DC + rc
                    sl = it % 2
                    W("sync", sV, winck[sl])
                    E("sync", lambda src=src, rc=rc, sl=sl: e.dma_start(
                        out=win[sl][:, 0:D],
                        in_=ap[src][rc * 128:(rc + 1) * 128, :]), (dW[sl], 16))
                    W("vector", dW[sl])
                    W("vector", gW[sl])
                    E("vector", lambda sl=sl: e.tensor_copy(
                        whi[sl][:, :], win[sl][:, 0:D]), (sV, 1))
                    W("vector", sV)
                    E("vector", lambda sl=sl: e.tensor_sub(
                        wlo[sl][:, :], win[sl][:, 0:D], whi[sl][:, :]), (sV, 1))
                    winck[sl] = sV.c
                    W("gpsimd", sV)
                    if idx is None:
                        dsth, dstl = ap[dh_][rc], ap[dl_][rc]
                    else:
                        dsth, dstl = ap[dh_][idx, rc], ap[dl_][idx, rc]
                    E("gpsimd", lambda d=dsth, sl=sl: e.dma_start(
                        out=d[:, :], in_=whi[sl][:, :]), (gW[sl], 16))
                    E("gpsimd", lambda d=dstl, sl=sl: e.dma_start(
                        out=d[:, :], in_=wlo[sl][:, :]), (gW[sl], 16))

            for (src, dst, wsn, NR, NW) in [
                    ("wgT", "wscr_g", "wsg", DC, I),
                    ("wuT", "wscr_u", "wsu", DC, I),
                    ("wdT", "wscr_d", "wsd", IC, D)]:
                for rc in range(NR):
                    sl = rc % 2
                    W("sync", sV, winck[sl])
                    E("sync", lambda src=src, rc=rc, sl=sl, NW=NW: e.dma_start(
                        out=win[sl][:, 0:NW],
                        in_=ap[src][rc * 128:(rc + 1) * 128, :]), (dW[sl], 16))
                    W("vector", dW[sl])
                    E("vector", lambda rc=rc, sl=sl, NW=NW: e.tensor_reduce(
                        wms[:, rc:rc + 1], win[sl][:, 0:NW], axis=AX.X,
                        op=Op.add, apply_absolute_value=True), (sV, 1))
                    winck[sl] = sV.c
                W("vector", sV)
                E("vector", lambda NR=NR: e.tensor_reduce(
                    wsc[:, 0:1], wms[:, 0:NR], axis=AX.X, op=Op.add), (sV, 1))
                W("tensor", sV)
                W("tensor", dC)
                E("tensor", lambda: e.transpose(
                    ps[0:1, 6, 0:128], wsc[:, 0:1], g_["id32"][:, :]), (sT, 1))
                W("vector", sT)
                E("vector", lambda: e.tensor_reduce(
                    wsc[0:1, 1:2], ps[0:1, 6, 0:128], axis=AX.X, op=Op.add), (sV, 1))
                W("vector", sV)
                E("vector", lambda NR=NR, NW=NW: e.tensor_scalar(
                    wsc[0:1, 2:3], wsc[0:1, 1:2], 1.0 / (NR * 128 * NW), EPS,
                    op0=Op.mult, op1=Op.max), (sV, 1))
                W("tensor", sV)
                E("tensor", lambda: e.matmul(
                    ps[:, 6, 0:1], g_["ones1"][0:1, :], wsc[0:1, 2:3],
                    start=True, stop=True), (sT, 1))
                W("scalar", sT)
                E("scalar", lambda wsn=wsn: e.activation(
                    g_[wsn][:, 0:1], ps[:, 6, 0:1], AF.Copy, bias=0.0,
                    scale=1.0), (sA, 1))
                W("vector", sA)
                E("vector", lambda wsn=wsn: e.tensor_scalar(
                    whn[:, 0:1], g_[wsn][:, 0:1], 0.5, None, op0=Op.mult), (sV, 1))
                W("vector", sV)
                E("vector", lambda: e.tensor_scalar(
                    whn[:, 1:2], whn[:, 0:1], -1.0, None, op0=Op.mult), (sV, 1))
                for rc in range(NR):
                    sl = rc % 2
                    W("sync", sV, winck[sl])
                    E("sync", lambda src=src, rc=rc, sl=sl, NW=NW: e.dma_start(
                        out=win[sl][:, 0:NW],
                        in_=ap[src][rc * 128:(rc + 1) * 128, :]), (dW[sl], 16))
                    W("vector", dW[sl])
                    W("vector", sV)
                    E("vector", lambda sl=sl, NW=NW: e.tensor_scalar(
                        wle[:, 0:NW], win[sl][:, 0:NW], whn[:, 1:2], None,
                        op0=Op.is_le), (sV, 1))
                    W("vector", sV)
                    W("vector", gW[sl])
                    E("vector", lambda sl=sl, NW=NW: e.scalar_tensor_tensor(
                        wt8[sl][:, 0:NW], win[sl][:, 0:NW], whn[:, 0:1],
                        wle[:, 0:NW], op0=Op.is_ge, op1=Op.subtract), (sV, 1))
                    winck[sl] = sV.c
                    W("gpsimd", sV)
                    E("gpsimd", lambda dst=dst, rc=rc, sl=sl, NW=NW: e.dma_start(
                        out=ap[dst][rc], in_=wt8[sl][:, 0:NW]), (gW[sl], 16))

    if '0' in stages:
        run_stage(ctx, s0)

    def make_sA(own):
        xsrc = "xo" if own else "xf"
        tab = "tabq" if own else "tabk"
        ait = AIT2 if own else AIT
        niter = (OT if own else T) // ait
        tpi = ait // 128

        def sA_(ctx, e, eng):
            E, W, B = mk_EW(e, eng, ctx.all_sems, ctx.own_map[eng])
            B()
            W("tensor", dC)
            if True:
                ax = ctx.balloc2("a_x", [128, 2, D], F32)
                axn = ctx.balloc("a_xn", [128, D], F32)
                ast = ctx.balloc("a_st", [128, 8], F32)
                axh = ctx.balloc("a_xh", [128, DC * ait], BF16)
                axl = ctx.balloc("a_xl", [128, DC * ait], BF16)
                awh = ctx.balloc2("a_wh", [128, 2, DC * WBLK], BF16)
                awl = ctx.balloc2("a_wl", [128, 2, DC * WBLK], BF16)
                act_ = ctx.balloc("a_ct", [128, tpi * 64], F32)
                astb = ctx.balloc("a_stb", [128, tpi * 64], F32)
                ajnk = ctx.balloc("a_jnk", [128, D], F32)
                aro = ctx.balloc("a_ro", [128, 128], F32)
                arh = ctx.balloc("a_rh", [128, 128], BF16)
                arl = ctx.balloc("a_rl", [128, 128], BF16)
                ake = ctx.balloc2("a_ke", [128, 2, 256], BF16)
                ave = ctx.balloc2("a_ve", [128, 2, WBLK], F32)
                anrm = ctx.balloc("a_nrm", [128, tpi * H], F32)
                wtick = {0: 0, 1: 0}
                ketick = {0: 0, 1: 0}
                vetick = {0: 0, 1: 0}
                abct = {"ke": 0, "ve": 0}

                def rope_head(bank, tc, gt, hglob):
                    x0 = ps[:, bank, (hglob % HPB) * 128:(hglob % HPB) * 128 + 64]
                    x1 = ps[:, bank, (hglob % HPB) * 128 + 64:(hglob % HPB) * 128 + 128]
                    ct = act_[:, tc * 64:(tc + 1) * 64]
                    st_ = astb[:, tc * 64:(tc + 1) * 64]
                    W("vector", sT)
                    E("vector", lambda: e.tensor_mul(ajnk[:, 0:64], x0, ct), (sV, 1))
                    E("vector", lambda: e.tensor_mul(ajnk[:, 64:128], x1, st_), (sV, 1))
                    E("vector", lambda: e.tensor_mul(ajnk[:, 128:192], x0, st_), (sV, 1))
                    E("vector", lambda: e.tensor_mul(ajnk[:, 192:256], x1, ct), (sV, 1))
                    W("vector", sV)
                    E("vector", lambda: e.tensor_sub(
                        aro[:, 0:64], ajnk[:, 0:64], ajnk[:, 64:128]), (sV, 1))
                    E("vector", lambda: e.tensor_add(
                        aro[:, 64:128], ajnk[:, 128:192], ajnk[:, 192:256]), (sV, 1))
                    W("vector", sV)
                    if own:
                        ncol = g_["qn2"][:, gt * H + hglob: gt * H + hglob + 1]
                    else:
                        ncol = g_["kn2"][:, hglob * TT + gt: hglob * TT + gt + 1]
                    E("vector", lambda ncol=ncol: e.scalar_tensor_tensor(
                        ajnk[:, 0:128], aro[:, :], 1.0, aro[:, :],
                        op0=Op.mult, op1=Op.mult, accum_out=ncol), (sV, 1))
                    E("vector", lambda: e.tensor_copy(arh[:, :], aro[:, :]), (sV, 1))
                    W("vector", sV)
                    E("vector", lambda: e.tensor_sub(arl[:, :], aro[:, :], arh[:, :]),
                      (sV, 1))
                    W("tensor", sV)
                    W("tensor", sA)
                    E("tensor", lambda: e.transpose(
                        psb[:, 0:128], arh[:, :], g_["idbf"][:, :]))
                    E("tensor", lambda: e.transpose(
                        psb[:, 128:256], arl[:, :], g_["idbf"][:, :]), (sT, 1))
                    kesl = abct["ke"] % 2
                    abct["ke"] += 1
                    W("scalar", sT)
                    W("scalar", gK[kesl])
                    E("scalar", lambda: e.activation(
                        ake[kesl][:, :], psb[:, 0:256], AF.Copy, bias=0.0,
                        scale=1.0), (sA, 1))
                    W("gpsimd", sA)
                    kdst = "qscr" if own else "kscr"
                    E("gpsimd", lambda: e.dma_start(
                        out=ap[kdst + "_h"][hglob][:, gt * 128:(gt + 1) * 128],
                        in_=ake[kesl][:, 0:128]), (gK[kesl], 16))
                    E("gpsimd", lambda: e.dma_start(
                        out=ap[kdst + "_l"][hglob][:, gt * 128:(gt + 1) * 128],
                        in_=ake[kesl][:, 128:256]), (gK[kesl], 16))

                def proj_tile(it, tc, blk, sl, isrope, xtick):
                    bank = 4 + (tc % 2)
                    gt = it * tpi + tc
                    W("tensor", dW[sl])
                    W("tensor", sV, xtick[tc])
                    W("tensor", sV)
                    W("tensor", sA)
                    k, n3 = 0, 3 * DC
                    for dc in range(DC):
                        c0 = dc * ait + tc * 128
                        xh_t = axh[:, c0:c0 + 128]
                        xl_t = axl[:, c0:c0 + 128]
                        wh_t = awh[sl][:, dc * WBLK:(dc + 1) * WBLK]
                        wl_t = awl[sl][:, dc * WBLK:(dc + 1) * WBLK]
                        for (sx, mv) in ((xh_t, wh_t), (xh_t, wl_t), (xl_t, wh_t)):
                            k += 1
                            E("tensor", lambda sx=sx, mv=mv, bank=bank, k=k, n3=n3:
                              e.matmul(ps[:, bank, 0:WBLK], sx, mv, start=(k == 1),
                                       stop=(k == n3)),
                                *([(sT, 1)] if k == n3 else []))
                    if isrope:
                        for hh in range(HPB):
                            rope_head(bank, tc, gt, blk * HPB + hh)
                    else:
                        vsl = abct["ve"] % 2
                        abct["ve"] += 1
                        W("scalar", sT)
                        W("scalar", gV[vsl])
                        E("scalar", lambda: e.activation(
                            ave[vsl][:, :], ps[:, bank, 0:WBLK], AF.Copy, bias=0.0,
                            scale=1.0), (sA, 1))
                        W("gpsimd", sA)
                        E("gpsimd", lambda: e.dma_start(
                            out=ap["vscr"][gt * 128:(gt + 1) * 128,
            blk * WBLK:(blk + 1) * WBLK],
                            in_=ave[vsl][:, :]), (gV[vsl], 16))

                for it in range(niter):
                    W("vector", sT)
                    W("vector", gX)
                    W("scalar", sT)
                    W("scalar", gX)
                    W("sync", sV)
                    E("sync", lambda it=it: e.dma_start(
                        out=act_[:, :],
                        in_=ap[tab][0][:, it * tpi * 64:(it + 1) * tpi * 64]),
                        (dT, 16))
                    E("sync", lambda it=it: e.dma_start(
                        out=ajnk[:, 0:tpi * 64],
                        in_=ap[tab][1][:, it * tpi * 64:(it + 1) * tpi * 64]),
                        (dT, 16))
                    W("vector", dT)
                    E("vector", lambda: e.tensor_add(
                        act_[:, :], act_[:, :], ajnk[:, 0:tpi * 64]), (sV, 1))
                    E("sync", lambda it=it: e.dma_start(
                        out=astb[:, :],
                        in_=ap[tab][2][:, it * tpi * 64:(it + 1) * tpi * 64]),
                        (dT2, 16))
                    W("sync", sV)
                    E("sync", lambda it=it: e.dma_start(
                        out=ajnk[:, 0:tpi * 64],
                        in_=ap[tab][3][:, it * tpi * 64:(it + 1) * tpi * 64]),
                        (dT2, 16))
                    W("vector", dT2)
                    E("vector", lambda: e.tensor_add(
                        astb[:, :], astb[:, :], ajnk[:, 0:tpi * 64]), (sV, 1))

                    import os as _os
                    _tpr = int(_os.environ.get("KLN", tpi) if _os.environ.get("KLN", "") != "off" else 0)
                    xtick = []
                    for tc in range(_tpr):
                        sl = tc % 2
                        if tc >= 2:
                            W("sync", sV, xtick[tc - 2])
                        E("sync", lambda it=it, tc=tc, sl=sl: e.dma_start(
                            out=ax[sl][:, :],
                            in_=ap[xsrc][(it * tpi + tc) * 128:
                                         (it * tpi + tc + 1) * 128, :]),
                            (dX[sl], 16))
                        _kcut = int(_os.environ.get("KCUT", 5))
                        if _kcut < 2:
                            xtick.append(sV.c)
                            continue
                        W("vector", dX[sl])
                        E("vector", lambda sl=sl: e.tensor_scalar(
                            ajnk[:, 0:D], ax[sl][:, :], 1.0, 0.0, op0=Op.mult,
                            op1=Op.add, accum_out=ast[:, 0:1]), (sV, 1))
                        W("vector", sV)
                        E("vector", lambda sl=sl: e.scalar_tensor_tensor(
                            ajnk[:, 0:D], ax[sl][:, :], 1.0, ax[sl][:, :],
                            op0=Op.mult, op1=Op.mult,
                            accum_out=ast[:, 1:2]), (sV, 1))
                        if _kcut < 3:
                            xtick.append(sV.c)
                            continue
                        W("vector", sV)
                        E("vector", lambda: e.tensor_scalar(
                            ast[:, 2:3], ast[:, 0:1], 1.0 / D, None, op0=Op.mult),
                            (sV, 1))
                        W("vector", sV)
                        E("vector", lambda: e.tensor_scalar(
                            ast[:, 3:4], ast[:, 1:2], 1.0 / D, None, op0=Op.mult),
                            (sV, 1))
                        W("vector", sV)
                        E("vector", lambda: e.scalar_tensor_tensor(
                            ast[:, 4:5], ast[:, 2:3], ast[:, 2:3], ast[:, 3:4],
                            op0=Op.mult, op1=Op.subtract), (sV, 1))
                        W("scalar", sV)
                        E("scalar", lambda: e.activation(
                            ast[:, 5:6], ast[:, 4:5], AF.Sqrt,
                            bias=g_["kc0"][:, 1:2], scale=-1.0), (sA, 1))
                        W("vector", sA)
                        E("vector", lambda: e.reciprocal(ast[:, 6:7], ast[:, 5:6]),
                          (sV, 1))
                        W("vector", sV)
                        E("vector", lambda sl=sl: e.tensor_scalar(
                            axn[:, :], ax[sl][:, :], ast[:, 2:3], ast[:, 6:7],
                            op0=Op.subtract, op1=Op.mult), (sV, 1))
                        if _kcut < 4:
                            xtick.append(sV.c)
                            continue
                        nb = (DC + 3) // 4
                        for bk in range(nb):
                            cnt = min(4, DC - bk * 4)
                            W("tensor", sV)
                            W("tensor", sA)
                            for q in range(cnt):
                                dc = bk * 4 + q
                                E("tensor", lambda dc=dc, bk=bk, q=q: e.transpose(
                                    ps[:, bk % 4, q * 128:(q + 1) * 128],
                                    axn[:, dc * 128:(dc + 1) * 128],
                                    g_["id32"][:, :]),
                                    *([(sT, 1)] if q == cnt - 1 else []))
                            for q in range(cnt):
                                if _kcut < 5:
                                    continue
                                dc = bk * 4 + q
                                col = dc * ait + tc * 128
                                if _os.environ.get("KEVAC", "dve") == "dve":
                                    W("vector", sT)
                                    E("vector", lambda bk=bk, q=q, col=col:
                                      e.tensor_copy(
                                          axh[:, col:col + 128],
                                          ps[:, bk % 4, q * 128:(q + 1) * 128]),
                                        (sA, 1))
                                else:
                                    W("scalar", sT)
                                    E("scalar", lambda bk=bk, q=q, col=col: e.activation(
                                        axh[:, col:col + 128],
                                        ps[:, bk % 4, q * 128:(q + 1) * 128],
                                        AF.Copy, bias=0.0, scale=1.0), (sA, 1))
                                W("vector", sA)
                                E("vector", lambda bk=bk, q=q, col=col:
                                  e.scalar_tensor_tensor(
                                      axl[:, col:col + 128],
                                      ps[:, bk % 4, q * 128:(q + 1) * 128], 0.0,
                                      axh[:, col:col + 128], op0=Op.add,
                                      op1=Op.subtract), (sV, 1))
                        xtick.append(sV.c)
                    if own:
                        W("gpsimd", sV)
                        for dc in range(DC):
                            E("gpsimd", lambda dc=dc, it=it: e.dma_start(
                                out=ap["xnscr_h"][dc][:, it * ait:(it + 1) * ait],
                                in_=axh[:, dc * ait:(dc + 1) * ait]), (gX, 16))
                            E("gpsimd", lambda dc=dc, it=it: e.dma_start(
                                out=ap["xnscr_l"][dc][:, it * ait:(it + 1) * ait],
                                in_=axl[:, dc * ait:(dc + 1) * ait]), (gX, 16))

                    projs = [(0, True)] if own else [(1, True), (2, False)]
                    import os
                    _pj = os.environ.get("KPROJ", "")
                    if _pj == "none":
                        projs = []
                    elif _pj == "k" and not own:
                        projs = [(1, True)]
                    elif _pj == "v" and not own:
                        projs = [(2, False)]
                    for (widx, isrope) in projs:
                        for blk in range(DBLK):
                            gwi = (it * 3 + widx) * DBLK + blk
                            sl = gwi % 2
                            W("sync", sT, wtick[sl])
                            for part, buf in (("wt_h", awh), ("wt_l", awl)):
                                E("sync", lambda part=part, widx=widx, blk=blk,
                                  buf=buf, sl=sl: e.dma_start(
                                      out=buf[sl][:, :].rearrange(
            "p (c j) -> p c j", c=DC),
                                      in_=ap[part][widx][:, :,
            blk * WBLK:(blk + 1) * WBLK]
                                      .rearrange("c p j -> p c j")), (dW[sl], 16))
                            for tc in range(tpi):
                                proj_tile(it, tc, blk, sl, isrope, xtick)
                            wtick[sl] = sT.c
        return sA_

    if 'A1' in stages:
        run_stage(ctx, make_sA(False))
    if 'A2' in stages:
        run_stage(ctx, make_sA(True))

    def sB(ctx, e, eng):
        E, W, B = mk_EW(e, eng, ctx.all_sems, ctx.own_map[eng])
        B()
        W("tensor", dC)
        NGKB = NG * KB
        if True:
            bmk = ctx.balloc("b_mk", [128, NT * KB], BF16)
            bqh = ctx.balloc("b_qh", [128, H * 256], BF16)
            bql = ctx.balloc("b_ql", [128, H * 256], BF16)
            bkh = ctx.balloc2("b_kh", [128, 2, NGKB], BF16)
            bkl = ctx.balloc2("b_kl", [128, 2, NGKB], BF16)
            bv = ctx.balloc2("b_v", [128, 2, NGKB], F32)
            bp = ctx.balloc("b_p", [128, 2 * KB], F32)
            bpt = ctx.balloc("b_pt", [128, 2 * KB], F32)
            blt = ctx.balloc("b_lt", [128, 2 * (NG + 2)], F32)
            bsc = ctx.balloc("b_sc", [128, 4], F32)
            blb = ctx.balloc("b_lb", [128, 256], F32)
            bhb = ctx.balloc2("b_hb", [128, 2, 256], F32)
            bhe = ctx.balloc2("b_he", [128, 2, 256], BF16)
            for jt in range(NT):
                E("sync", lambda jt=jt: e.dma_start(
                    out=bmk[:, jt * KB:(jt + 1) * KB], in_=ap["masks"][jt]),
                    (dC, 16))
            W("vector", dC)
            W("sync", gK[0])
            W("sync", gK[1])
            W("sync", gV[0])
            W("sync", gV[1])
            W("sync", gX)
            for h in range(H):
                E("vector", lambda h=h: e.tensor_reduce(
                    g_["kmax"][:, h:h + 1],
                    g_["kn2"][:, h * TT:(h + 1) * TT], axis=AX.X, op=Op.max),
                    (sV, 1))
            W("tensor", sV)
            E("tensor", lambda: e.transpose(
                ps[0:H, 6, 0:128], g_["kmax"][:, 0:H], g_["id32"][:, :]), (sT, 1))
            W("vector", sT)
            E("vector", lambda: e.tensor_reduce(
                bsc[0:H, 0:1], ps[0:H, 6, 0:128], axis=AX.X, op=Op.max), (sV, 1))
            W("scalar", sV)
            E("scalar", lambda: e.activation(
                bsc[0:H, 1:2], bsc[0:H, 0:1], AF.Sqrt, bias=g_["kc0"][0:H, 0:1],
                scale=1.0), (sA, 1))
            W("vector", sA)
            E("vector", lambda: e.tensor_scalar(
                bsc[0:H, 2:3], bsc[0:H, 1:2], -1.02, None, op0=Op.mult), (sV, 1))
            W("tensor", sV)
            E("tensor", lambda: e.transpose(
                ps[0:1, 6, 0:H], bsc[0:H, 2:3], g_["id32"][0:H, 0:H]), (sT, 1))
            W("scalar", sT)
            E("scalar", lambda: e.activation(
                bsc[0:1, 0:H] if H <= 4 else blb[0:1, 0:H],
                ps[0:1, 6, 0:H], AF.Copy, bias=0.0, scale=1.0), (sA, 1))
            W("tensor", sA)
            E("tensor", lambda: e.matmul(
                ps[:, 6, 0:H], g_["ones1"][0:1, :],
                bsc[0:1, 0:H] if H <= 4 else blb[0:1, 0:H],
                start=True, stop=True), (sT, 1))
            W("scalar", sT)
            E("scalar", lambda: e.activation(
                g_["kmax"][:, 0:H], ps[:, 6, 0:H], AF.Copy, bias=0.0,
                scale=1.0), (sA, 1))
            E("scalar", lambda: e.activation(
                g_["cbias"][:, 0:NT * H], g_["qn2"][:, 0:NT * H], AF.Sqrt,
                bias=g_["kc0"][:, 0:1], scale=1.0), (sA, 1))
            W("vector", sA)
            for jt in range(NT):
                E("vector", lambda jt=jt: e.tensor_mul(
                    g_["cbias"][:, jt * H:(jt + 1) * H],
                    g_["cbias"][:, jt * H:(jt + 1) * H], g_["kmax"][:, 0:H]),
                    (sV, 1))

            qtick = 0
            kvtick = {0: 0, 1: 0}
            avfree = {0: 0, 1: 0}
            hetick = {0: 0, 1: 0}
            hect = 0
            for g in range(NG):
                kb = kb_of_group(g)
                kbK = kb * KB
                W("sync", sT, qtick)
                for h in range(H):
                    E("sync", lambda g=g, h=h: e.dma_start(
                        out=bqh[:, h * 256:(h + 1) * 256],
                        in_=ap["qscr_h"][h][:, g * 256:(g + 1) * 256]), (dQ, 16))
                    E("sync", lambda g=g, h=h: e.dma_start(
                        out=bql[:, h * 256:(h + 1) * 256],
                        in_=ap["qscr_l"][h][:, g * 256:(g + 1) * 256]), (dQ, 16))
                for h in range(H):
                    sl = h % 2
                    if g * H + h >= 2:
                        W("sync", sT, kvtick[sl])
                    E("sync", lambda h=h, kbK=kbK, sl=sl: e.dma_start(
                        out=bkh[sl][:, 0:kbK], in_=ap["kscr_h"][h][:, 0:kbK]),
                        (dK[sl], 16))
                    E("sync", lambda h=h, kbK=kbK, sl=sl: e.dma_start(
                        out=bkl[sl][:, 0:kbK], in_=ap["kscr_l"][h][:, 0:kbK]),
                        (dK[sl], 16))
                    nvt = kbK // 128
                    for v0 in range(0, nvt, 8):
                        nv = min(8, nvt - v0)
                        E("sync", lambda h=h, v0=v0, nv=nv, sl=sl: e.dma_start(
                            out=bv[sl][:, v0 * 128:(v0 + nv) * 128].rearrange(
                                "p (n j) -> p n j", n=nv),
                            in_=ap["vscr"][v0 * 128:(v0 + nv) * 128,
            h * 128:(h + 1) * 128].rearrange(
                                "(n p) j -> p n j", p=128)), (dK[sl], 16))
                    avbank = 4 + sl
                    first_av = True
                    for kbi in range(kb):
                        for qt in range(2):
                            bnk = qt * 2 + (kbi % 2)
                            jt = 2 * g + qt
                            W("tensor", dQ)
                            W("tensor", dK[sl])
                            W("tensor", sA)
                            W("tensor", sV)
                            if first_av:
                                W("tensor", sV, avfree[sl])
                            qh_t = bqh[:, h * 256 + qt * 128: h * 256 + qt * 128 + 128]
                            ql_t = bql[:, h * 256 + qt * 128: h * 256 + qt * 128 + 128]
                            kh_t = bkh[sl][:, kbi * KB:(kbi + 1) * KB]
                            kl_t = bkl[sl][:, kbi * KB:(kbi + 1) * KB]
                            k = 0
                            for (sx, mv) in ((qh_t, kh_t), (qh_t, kl_t), (ql_t, kh_t)):
                                k += 1
                                E("tensor", lambda sx=sx, mv=mv, bnk=bnk, k=k:
                                  e.matmul(ps[:, bnk, 0:KB], sx, mv,
            start=(k == 1), stop=(k == 3)),
                                    *([(sT, 1)] if k == 3 else []))
                            if kbi == kb - 1:
                                W("vector", sT)
                                E("vector", lambda bnk=bnk, jt=jt: e.tensor_add(
                                    ps[:, bnk, 0:KB], ps[:, bnk, 0:KB],
                                    bmk[:, jt * KB:(jt + 1) * KB]), (sV, 1))
                            W("scalar", sT)
                            W("scalar", sV)
                            E("scalar", lambda bnk=bnk, qt=qt, jt=jt, kbi=kbi:
                              e.activation(
                                  bp[:, qt * KB:(qt + 1) * KB], ps[:, bnk, 0:KB],
                                  AF.Exp, bias=g_["cbias"][:, jt * H + h:jt * H + h + 1],
                                  scale=1.0,
                                  accum_out=blt[:, qt * (NG + 2) + kbi + 1:
            qt * (NG + 2) + kbi + 2]), (sA, 1))
                        tbank = 6
                        nkc = KB // 128
                        for kc in range(nkc):
                            if kc % 2 == 0:
                                W("tensor", sA)
                            for qt in range(2):
                                E("tensor", lambda kc=kc, qt=qt, tbank=tbank:
                                  e.transpose(
                                      ps[:, tbank, (kc % 2) * 256 + qt * 128:
            (kc % 2) * 256 + qt * 128 + 128],
                                      bp[:, qt * KB + kc * 128:
            qt * KB + (kc + 1) * 128],
                                      g_["id32"][:, :]),
                                    *([(sT, 1)] if qt == 1 else []))
                            if kc % 2 == 1 or kc == nkc - 1:
                                W("scalar", sT)
                                E("scalar", lambda kc=kc, tbank=tbank: e.activation(
                                    bpt[:, (kc - kc % 2) * 256:(kc + 1) * 256],
                                    ps[:, tbank, 0:(kc % 2 + 1) * 256],
                                    AF.Copy, bias=0.0, scale=1.0), (sA, 1))
                        W("tensor", sA)
                        for kc in range(nkc):
                            kt_glob = kbi * nkc + kc
                            vt = bv[sl][:, kt_glob * 128:(kt_glob + 1) * 128]
                            lastmm = (kbi == kb - 1 and kc == nkc - 1)
                            E("tensor", lambda vt=vt, kc=kc, avbank=avbank,
                              first_av=first_av, lastmm=lastmm: e.matmul(
                                  ps[:, avbank, 0:256], vt,
                                  bpt[:, kc * 256:(kc + 1) * 256],
                                  start=first_av, stop=lastmm),
                                *([(sT, 1)] if lastmm else []))
                            first_av = False
                    kvtick[sl] = sT.c
                    W("vector", sA)
                    for qt in range(2):
                        jt = 2 * g + qt
                        E("vector", lambda qt=qt, kb=kb: e.tensor_reduce(
                            blt[:, qt * (NG + 2):qt * (NG + 2) + 1],
                            blt[:, qt * (NG + 2) + 1:qt * (NG + 2) + 1 + kb],
                            axis=AX.X, op=Op.add), (sV, 1))
                        W("vector", sV)
                        E("vector", lambda qt=qt, jt=jt, h=h: e.reciprocal(
                            g_["linv"][:, jt * H + h:jt * H + h + 1],
                            blt[:, qt * (NG + 2):qt * (NG + 2) + 1]), (sV, 1))
                    W("tensor", sV)
                    W("tensor", sA)
                    for qt in range(2):
                        jt = 2 * g + qt
                        E("tensor", lambda jt=jt, h=h, qt=qt: e.transpose(
                            ps[0:1, 3, qt * 128:qt * 128 + 128],
                            g_["linv"][:, jt * H + h:jt * H + h + 1],
                            g_["id32"][:, :]), *([(sT, 1)] if qt == 1 else []))
                    W("scalar", sT)
                    E("scalar", lambda: e.activation(
                        blb[0:1, 0:256], ps[0:1, 3, 0:256], AF.Copy, bias=0.0,
                        scale=1.0), (sA, 1))
                    W("tensor", sA)
                    E("tensor", lambda: e.matmul(
                        ps[:, 3, 0:256], g_["ones1"][0:1, :], blb[0:1, 0:256],
                        start=True, stop=True), (sT, 1))
                    W("scalar", sT)
                    E("scalar", lambda: e.activation(
                        blb[:, 0:256], ps[:, 3, 0:256], AF.Copy, bias=0.0,
                        scale=1.0), (sA, 1))
                    hesl = hect % 2
                    hect += 1
                    W("vector", sA)
                    W("vector", sT)
                    W("vector", gH[hesl])
                    E("vector", lambda avbank=avbank, hesl=hesl: e.tensor_mul(
                        bhb[hesl][:, :], ps[:, avbank, 0:256], blb[:, 0:256]),
                        (sV, 1))
                    avfree[sl] = sV.c
                    W("vector", sV)
                    E("vector", lambda hesl=hesl: e.tensor_copy(
                        bhe[hesl][:, :], bhb[hesl][:, :]), (sV, 1))
                    W("vector", sV)
                    E("vector", lambda hesl=hesl: e.tensor_sub(
                        bhb[hesl][:, :], bhb[hesl][:, :], bhe[hesl][:, :]), (sV, 1))
                    W("gpsimd", sV)
                    E("gpsimd", lambda h=h, g=g, hesl=hesl: e.dma_start(
                        out=ap["hscr_h"][h][:, g * 256:(g + 1) * 256],
                        in_=bhe[hesl][:, :]), (gH[hesl], 16))
                    E("gpsimd", lambda h=h, g=g, hesl=hesl: e.dma_start(
                        out=ap["hscr_l"][h][:, g * 256:(g + 1) * 256],
                        in_=bhb[hesl][:, :]), (gH[hesl], 16))
                qtick = sT.c

    if 'B' in stages:
        run_stage(ctx, sB)

    def sC(ctx, e, eng):
        E, W, B = mk_EW(e, eng, ctx.all_sems, ctx.own_map[eng])
        B()
        W("tensor", dC)
        TCB = min(512, OT)
        NCB = OT // TCB
        TPC = TCB // 128
        if True:
            chh = ctx.balloc("c_hh", [128, H * TCB], BF16)
            chl = ctx.balloc("c_hl", [128, H * TCB], BF16)
            cwh = ctx.balloc2("c_wh", [128, 2, H * WBLK], BF16)
            cwl = ctx.balloc2("c_wl", [128, 2, H * WBLK], BF16)
            cx = ctx.balloc("c_x", [128, TPC * D], F32)
            co = ctx.balloc2("c_o", [128, 2, WBLK], F32)
            wtick = {0: 0, 1: 0}
            oct_ = 0
            W("sync", gH[0])
            W("sync", gH[1])
            for tcb in range(NCB):
                W("sync", sT)
                W("sync", sV)
                for h in range(H):
                    E("sync", lambda h=h, tcb=tcb: e.dma_start(
                        out=chh[:, h * TCB:(h + 1) * TCB],
                        in_=ap["hscr_h"][h][:, tcb * TCB:(tcb + 1) * TCB]),
                        (dX[0], 16))
                    E("sync", lambda h=h, tcb=tcb: e.dma_start(
                        out=chl[:, h * TCB:(h + 1) * TCB],
                        in_=ap["hscr_l"][h][:, tcb * TCB:(tcb + 1) * TCB]),
                        (dX[0], 16))
                for tc in range(TPC):
                    E("sync", lambda tcb=tcb, tc=tc: e.dma_start(
                        out=cx[:, tc * D:(tc + 1) * D],
                        in_=ap["xo"][(tcb * TPC + tc) * 128:
                                     (tcb * TPC + tc + 1) * 128, :]), (dX[0], 16))
                for dob in range(DBLK):
                    sl = (tcb * DBLK + dob) % 2
                    if tcb * DBLK + dob >= 2:
                        W("sync", sT, wtick[sl])
                    for part, buf in (("wot_h", cwh), ("wot_l", cwl)):
                        E("sync", lambda part=part, dob=dob, buf=buf, sl=sl:
                          e.dma_start(
                              out=buf[sl][:, :].rearrange("p (c j) -> p c j", c=H),
                              in_=ap[part][:, :, dob * WBLK:(dob + 1) * WBLK]
                              .rearrange("c p j -> p c j")), (dW[sl], 16))
                    for tc in range(TPC):
                        bank = tc % 2
                        osl = oct_ % 2
                        oct_ += 1
                        W("tensor", dW[sl])
                        W("tensor", dX[0])
                        W("tensor", sV)
                        k, n3 = 0, 3 * H
                        for dc in range(H):
                            hh_t = chh[:, dc * TCB + tc * 128: dc * TCB + tc * 128 + 128]
                            hl_t = chl[:, dc * TCB + tc * 128: dc * TCB + tc * 128 + 128]
                            wh_t = cwh[sl][:, dc * WBLK:(dc + 1) * WBLK]
                            wl_t = cwl[sl][:, dc * WBLK:(dc + 1) * WBLK]
                            for (sx, mv) in ((hh_t, wh_t), (hh_t, wl_t), (hl_t, wh_t)):
                                k += 1
                                E("tensor", lambda sx=sx, mv=mv, bank=bank, k=k,
                                  n3=n3: e.matmul(
                                      ps[:, bank, 0:WBLK], sx, mv, start=(k == 1),
                                      stop=(k == n3)),
                                    *([(sT, 1)] if k == n3 else []))
                        W("vector", sT)
                        W("vector", dX[0])
                        W("vector", gO[osl])
                        E("vector", lambda bank=bank, tc=tc, dob=dob, osl=osl:
                          e.tensor_add(
                              co[osl][:, :], ps[:, bank, 0:WBLK],
                              cx[:, tc * D + dob * WBLK: tc * D + (dob + 1) * WBLK]),
                            (sV, 1))
                        W("gpsimd", sV)
                        E("gpsimd", lambda tcb=tcb, tc=tc, dob=dob, osl=osl:
                          e.dma_start(
                              out=ap["oscr"][(tcb * TPC + tc) * 128:
            (tcb * TPC + tc + 1) * 128,
            dob * WBLK:(dob + 1) * WBLK],
                              in_=co[osl][:, :]), (gO[osl], 16))
                    wtick[sl] = sT.c

    if 'C' in stages:
        run_stage(ctx, sC)

    def sD(ctx, e, eng):
        E, W, B = mk_EW(e, eng, ctx.all_sems, ctx.own_map[eng])
        B()
        W("tensor", dC)
        OC = IC
        HQ = min(2048, max(I // 2, D))
        NQ2 = I // HQ
        LVL = [float(1 << k) for k in range(7, -1, -1)]
        CONST = []
        c = 0.0
        for t in LVL:
            CONST.append(-(t + c))
            c += t / 2.0
        CFIN = c
        if True:
            dxh = ctx.balloc("d_xh", [128, DC * FMT], BF16)
            dxl = ctx.balloc("d_xl", [128, DC * FMT], BF16)
            df1 = ctx.balloc("d_f1", [128, DC * FMT], F32)
            dsc2 = ctx.balloc("d_sc2", [128, HQ], F32)
            dtm = ctx.balloc("d_tm", [128, D], F32)
            dq1f = ctx.balloc2("d_q1f", [128, 2, max(DC * FMT, HQ)], BF16)
            dsc = ctx.balloc("d_dsc", [128, 16], F32)
            dldr = ctx.balloc("d_ldr", [128, 8], F32)
            dwg = ctx.balloc2("d_wg", [128, 2, DC * WBI], FP8)
            dwu = ctx.balloc2("d_wu", [128, 2, DC * WBI], FP8)
            dt1 = ctx.balloc("d_t1", [128, WBI], F32)
            dt2 = ctx.balloc("d_t2", [128, WBI], F32)
            dytm = ctx.balloc("d_ytm", [128, I], F32)
            dyfm = ctx.balloc("d_yfm", [128, IC * FMT], F32)
            dysh = ctx.balloc("d_ysh", [128, 2 * FMT], BF16)
            dysl = ctx.balloc("d_ysl", [128, 2 * FMT], BF16)
            dq2t = ctx.balloc("d_q2t", [128, I // 2], BF16)
            dq2f = ctx.balloc("d_q2f", [128, IC * FMT], BF16)
            dwd = ctx.balloc2("d_wd", [128, 2, (IC // 2) * WBLK], FP8)
            dos = ctx.balloc("d_os", [128, WBLK], F32)
            dou = ctx.balloc2("d_ou", [128, 1, WBLK], F32)

            for k in range(8):
                E("vector", lambda k=k: e.memset(dldr[:, k:k + 1], CONST[k]),
                  (sV, 1))

            def ladder_multi(specs, scvec):
                svt = {}
                sat = {}
                for ci, (zbuf, wid, bits, wcopy, outq) in enumerate(specs):
                    E("vector", lambda zbuf=zbuf, scvec=scvec: e.tensor_scalar(
                        zbuf, zbuf, scvec, -128.0, op0=Op.mult, op1=Op.max),
                        (sV, 1))
                    E("vector", lambda zbuf=zbuf: e.tensor_scalar(
                        zbuf, zbuf, 127.49, 128.5, op0=Op.min, op1=Op.add),
                        (sV, 1))
                    E("vector", lambda zbuf=zbuf, wcopy=wcopy: e.tensor_scalar(
                        wcopy, zbuf, 1.0, None, op0=Op.mult), (sV, 1))
                    svt[ci] = sV.c
                for k in range(8):
                    for ci, (zbuf, wid, bits, wcopy, outq) in enumerate(specs):
                        W("scalar", sV, svt[ci])
                        E("scalar", lambda zbuf=zbuf, bits=bits, k=k: e.activation(
                            bits, zbuf, AF.Sign, bias=dldr[:, k:k + 1],
                            scale=1.0), (sA, 1))
                        sat[ci] = sA.c
                        W("vector", sA, sat[ci])
                        E("vector", lambda zbuf=zbuf, bits=bits, k=k:
                          e.scalar_tensor_tensor(
                              zbuf, bits, -LVL[k] / 2.0, zbuf, op0=Op.mult,
                              op1=Op.add), (sV, 1))
                        svt[ci] = sV.c
                for ci, (zbuf, wid, bits, wcopy, outq) in enumerate(specs):
                    E("vector", lambda zbuf=zbuf, wcopy=wcopy: e.tensor_sub(
                        zbuf, wcopy, zbuf), (sV, 1))
                    E("vector", lambda zbuf=zbuf, outq=outq: e.tensor_scalar(
                        outq, zbuf, CFIN - 128.0, 127.0, op0=Op.add, op1=Op.min),
                        (sV, 1))

            xn_tick = 0
            df1_free = 0
            df1v_free = 0
            y_tick = {}
            bank_sV = {}

            def bankw(b):
                W("tensor", sA, y_tick.get(b, 0))
                W("tensor", sV, bank_sV.get(b, 0))
            gu_bank_tick = {}
            wg_tick = {0: 0, 1: 0}
            wd_tick = {0: 0, 1: 0}
            dq2f_tick = 0
            dos_tick = 0
            dou_tick = 0
            for mi in range(NM):
                t0 = mi * FMT
                xsl = mi % 2
                W("sync", sT, xn_tick)
                for dc in range(DC):
                    E("sync", lambda dc=dc, t0=t0: e.dma_start(
                        out=dxh[:, dc * FMT:(dc + 1) * FMT],
                        in_=ap["xnscr_h"][dc][:, t0:t0 + FMT]), (dX[0], 16))
                    E("sync", lambda dc=dc, t0=t0: e.dma_start(
                        out=dxl[:, dc * FMT:(dc + 1) * FMT],
                        in_=ap["xnscr_l"][dc][:, t0:t0 + FMT]), (dX[0], 16))
                for dc in range(DC):
                    hb6 = dc % 2
                    W("tensor", dX[0])
                    W("tensor", sA, y_tick.get((6, 0), 0))
                    W("tensor", sA, y_tick.get((6, 1), 0))
                    W("tensor", sV, y_tick.get((6, "v"), 0))
                    E("tensor", lambda dc=dc, hb6=hb6: e.matmul(
                        ps[:, 6, hb6 * FMT:(hb6 + 1) * FMT], g_["hmat"][:, :],
                        dxh[:, dc * FMT:(dc + 1) * FMT], start=True, stop=False,
                        skip_group_check=True))
                    E("tensor", lambda dc=dc, hb6=hb6: e.matmul(
                        ps[:, 6, hb6 * FMT:(hb6 + 1) * FMT], g_["hmat"][:, :],
                        dxl[:, dc * FMT:(dc + 1) * FMT], start=False, stop=True,
                        skip_group_check=True), (sT, 1))
                    if dc == DC - 1:
                        xn_tick = sT.c
                    W("scalar", sT)
                    W("scalar", sT, df1_free)
                    W("scalar", sV, df1v_free)
                    E("scalar", lambda dc=dc, hb6=hb6: e.activation(
                        df1[:, dc * FMT:(dc + 1) * FMT],
                        ps[:, 6, hb6 * FMT:(hb6 + 1) * FMT],
                        AF.Copy, bias=0.0, scale=1.0), (sA, 1))
                    y_tick[(6, hb6)] = sA.c
                ap1 = df1[:, :].rearrange("p (c j) -> p c j", c=DC)
                np1 = DC // 2
                for lv in range(L1):
                    stp = 1 << lv
                    W("vector", sA if lv == 0 else sV)
                    E("vector", lambda stp=stp: e.tensor_copy(
                        dsc2[:, 0:np1 * FMT].rearrange("p (c j) -> p c j", c=np1),
                        _bsel(ap1, stp, 1)), (sV, 1))
                    E("vector", lambda stp=stp: e.tensor_sub(
                        _bsel(ap1, stp, 1), _bsel(ap1, stp, 0), _bsel(ap1, stp, 1)),
                        (sV, 1))
                    E("vector", lambda stp=stp: e.tensor_add(
                        _bsel(ap1, stp, 0), _bsel(ap1, stp, 0),
                        dsc2[:, 0:np1 * FMT].rearrange("p (c j) -> p c j", c=np1)),
                        (sV, 1))
                bf1_tick = sV.c
                for dc in range(DC):
                    q = dc % 4
                    if q == 0:
                        W("tensor", sV, bf1_tick)
                        W("tensor", sA, y_tick.get((6, 0), 0))
                        W("tensor", sA, y_tick.get((6, 1), 0))
                    E("tensor", lambda dc=dc, q=q: e.transpose(
                        ps[:, 6, q * 128:(q + 1) * 128],
                        df1[:, dc * FMT:(dc + 1) * FMT], g_["id32"][:, :]),
                        *([(sT, 1)] if (q == 3 or dc == DC - 1) else []))
                    if q == 3 or dc == DC - 1:
                        W("scalar", sT)
                        E("scalar", lambda dc=dc, q=q: e.activation(
                            dtm[:, (dc - q) * 128:(dc + 1) * 128],
                            ps[:, 6, 0:(q + 1) * 128], AF.Copy, bias=0.0,
                            scale=1.0), (sA, 1))
                        y_tick[(6, 0)] = sA.c
                        y_tick[(6, 1)] = sA.c
                W("vector", sA)
                E("vector", lambda: e.tensor_reduce(
                    dsc[:, 0:1], dtm[:, :], axis=AX.X, op=Op.max,
                    apply_absolute_value=True), (sV, 1))
                E("vector", lambda: e.tensor_scalar(
                    dsc[:, 1:2], dsc[:, 0:1], C1, EPS, op0=Op.mult, op1=Op.max),
                    (sV, 1))
                E("vector", lambda: e.tensor_scalar(
                    dsc[:, 1:2], dsc[:, 1:2], 1.0 / 127.0, None, op0=Op.mult),
                    (sV, 1))
                E("vector", lambda: e.reciprocal(dsc[:, 2:3], dsc[:, 1:2]), (sV, 1))
                E("vector", lambda: e.tensor_scalar(
                    dsc[:, 2:3], dsc[:, 2:3], C1, None, op0=Op.mult), (sV, 1))
                E("vector", lambda: e.tensor_scalar(
                    dsc[:, 3:4], dsc[:, 1:2], g_["wsg"][:, 0:1], None, op0=Op.mult),
                    (sV, 1))
                E("vector", lambda: e.tensor_scalar(
                    dsc[:, 4:5], dsc[:, 1:2], g_["wsu"][:, 0:1], None, op0=Op.mult),
                    (sV, 1))
                ladder_multi([(dtm[:, :], D, dq2t[:, 0:D], dsc2[:, 0:D],
                               df1[:, 0:D])], dsc[:, 2:3])
                q1l_tick = sV.c
                for dc in range(DC):
                    q = dc % 4
                    if q == 0:
                        W("tensor", sV, q1l_tick)
                        W("tensor", sV, y_tick.get((6, "v"), 0))
                        W("tensor", sA, y_tick.get((6, 0), 0))
                        W("tensor", sA, y_tick.get((6, 1), 0))
                    E("tensor", lambda dc=dc, q=q: e.transpose(
                        ps[:, 6, q * 128:(q + 1) * 128],
                        df1[:, dc * 128:(dc + 1) * 128], g_["id32"][:, :]),
                        *([(sT, 1)] if (q == 3 or dc == DC - 1) else []))
                    if q == 3 or dc == DC - 1:
                        W("vector", sT)
                        E("vector", lambda dc=dc, q=q, xsl=xsl: e.tensor_copy(
                            dq1f[xsl][:, (dc - q) * FMT:(dc + 1) * FMT],
                            ps[:, 6, 0:(q + 1) * 128]), (sV, 1))
                        y_tick[(6, "v")] = sV.c
                dq1f_ready = sV.c
                df1_free = sT.c
                for ob in range(OBI):
                    wsl = (mi * OBI + ob) % 2
                    if mi * OBI + ob >= 2:
                        W("sync", sT, wg_tick[wsl])
                    E("sync", lambda ob=ob, wsl=wsl: e.dma_start(
                        out=dwg[wsl][:, :].rearrange("p (c j) -> p c j", c=DC),
                        in_=ap["wscr_g"][:, :, ob * WBI:(ob + 1) * WBI]
                        .rearrange("c p j -> p c j")), (dW[wsl], 16))
                    E("sync", lambda ob=ob, wsl=wsl: e.dma_start(
                        out=dwu[wsl][:, :].rearrange("p (c j) -> p c j", c=DC),
                        in_=ap["wscr_u"][:, :, ob * WBI:(ob + 1) * WBI]
                        .rearrange("c p j -> p c j")), (dW[wsl], 16))
                    gbank, ubank = 2, 3
                    for (wbuf, bank) in ((dwg, gbank), (dwu, ubank)):
                        W("tensor", dW[wsl])
                        W("tensor", sV, dq1f_ready)
                        bankw(bank)
                        k = 0
                        for dc in range(DC):
                            k += 1
                            E("tensor", lambda wbuf=wbuf, dc=dc, bank=bank, k=k,
                              wsl=wsl, xsl=xsl: e.matmul(
                                  ps[:, bank, 0:WBI],
                                  dq1f[xsl][:, dc * FMT:(dc + 1) * FMT],
                                  wbuf[wsl][:, dc * WBI:(dc + 1) * WBI],
                                  start=(k == 1), stop=(k == DC)),
                                *([(sT, 1)] if k == DC else []))
                    wg_tick[wsl] = sT.c
                    W("scalar", sT)
                    W("scalar", sV)
                    E("scalar", lambda gbank=gbank: e.activation(
                        dt1[:, :], ps[:, gbank, 0:WBI], AF.Silu,
                        bias=g_["kc0"][:, 0:1], scale=dsc[:, 3:4]), (sA, 1))
                    E("scalar", lambda ubank=ubank: e.activation(
                        dt2[:, :], ps[:, ubank, 0:WBI], AF.Identity,
                        bias=g_["kc0"][:, 0:1], scale=dsc[:, 4:5]), (sA, 1))
                    y_tick[gbank] = sA.c
                    y_tick[ubank] = sA.c
                    W("vector", sA)
                    if ob == 0:
                        W("vector", sT, y_tick.get("ytm", 0))
                    E("vector", lambda ob=ob: e.tensor_mul(
                        dytm[:, ob * WBI:(ob + 1) * WBI], dt1[:, :], dt2[:, :]),
                        (sV, 1))
                for ic in range(IC):
                    bank = 4 + ((ic // 4) % 2)
                    q = ic % 4
                    if q == 0:
                        W("tensor", sV)
                        bankw(bank)
                    E("tensor", lambda ic=ic, bank=bank, q=q: e.transpose(
                        ps[:, bank, q * 128:(q + 1) * 128],
                        dytm[:, ic * 128:(ic + 1) * 128], g_["id32"][:, :]),
                        *([(sT, 1)] if (q == 3 or ic == IC - 1) else []))
                    if q == 3 or ic == IC - 1:
                        if ic == IC - 1:
                            y_tick["ytm"] = sT.c
                        W("scalar", sT)
                        E("scalar", lambda ic=ic, bank=bank, q=q: e.activation(
                            dyfm[:, (ic - q) * FMT:(ic + 1) * FMT],
                            ps[:, bank, 0:(q + 1) * 128], AF.Copy, bias=0.0,
                            scale=1.0), (sA, 1))
                        y_tick[bank] = sA.c
                for c0 in range(0, IC, 2):
                    ng = min(2, IC - c0)
                    W("vector", sA)
                    E("vector", lambda c0=c0, ng=ng: e.tensor_copy(
                        dysh[:, 0:ng * FMT], dyfm[:, c0 * FMT:(c0 + ng) * FMT]),
                        (sV, 1))
                    E("vector", lambda c0=c0, ng=ng: e.tensor_sub(
                        dysl[:, 0:ng * FMT], dyfm[:, c0 * FMT:(c0 + ng) * FMT],
                        dysh[:, 0:ng * FMT]), (sV, 1))
                    for ci in range(ng):
                        ic = c0 + ci
                        bank = ic % 2
                        W("tensor", sV)
                        bankw(bank)
                        E("tensor", lambda ci=ci, bank=bank: e.matmul(
                            ps[:, bank, 0:FMT], g_["hmat"][:, :],
                            dysh[:, ci * FMT:(ci + 1) * FMT], start=True,
                            stop=False))
                        E("tensor", lambda ci=ci, bank=bank: e.matmul(
                            ps[:, bank, 0:FMT], g_["hmat"][:, :],
                            dysl[:, ci * FMT:(ci + 1) * FMT], start=False,
                            stop=True), (sT, 1))
                        W("scalar", sT)
                        E("scalar", lambda ic=ic, bank=bank: e.activation(
                            dyfm[:, ic * FMT:(ic + 1) * FMT], ps[:, bank, 0:FMT],
                            AF.Copy, bias=0.0, scale=1.0), (sA, 1))
                        y_tick[bank] = sA.c
                ap2 = dyfm[:, :].rearrange("p (c j) -> p c j", c=IC)
                np2 = IC // 2
                nh = (np2 * FMT + HQ - 1) // HQ
                per = np2 // nh
                for lv in range(L2):
                    stp = 1 << lv
                    ng2 = IC // (2 * stp)
                    if ng2 >= nh:
                        tg, tc_ = ng2 // nh, stp
                    else:
                        tg, tc_ = ng2, stp // nh

                    def tview():
                        return dsc2[:, 0:tg * tc_ * FMT].rearrange(
                            "p (g c j) -> p g c j", g=tg, c=tc_)
                    W("vector", sA if lv == 0 else sV)
                    for hb in range(nh):
                        E("vector", lambda stp=stp, hb=hb, tview=tview:
                          e.tensor_copy(tview(), _bsel2(ap2, stp, 1, hb, nh)),
                            (sV, 1))
                        E("vector", lambda stp=stp, hb=hb: e.tensor_sub(
                            _bsel2(ap2, stp, 1, hb, nh),
                            _bsel2(ap2, stp, 0, hb, nh),
                            _bsel2(ap2, stp, 1, hb, nh)), (sV, 1))
                        E("vector", lambda stp=stp, hb=hb, tview=tview:
                          e.tensor_add(_bsel2(ap2, stp, 0, hb, nh),
                                       _bsel2(ap2, stp, 0, hb, nh), tview()),
                            (sV, 1))
                for ic in range(IC):
                    bank = 4 + ((ic // 4) % 2)
                    q = ic % 4
                    if q == 0:
                        W("tensor", sV)
                        bankw(bank)
                    E("tensor", lambda ic=ic, bank=bank, q=q: e.transpose(
                        ps[:, bank, q * 128:(q + 1) * 128],
                        dyfm[:, ic * FMT:(ic + 1) * FMT], g_["id32"][:, :]),
                        *([(sT, 1)] if (q == 3 or ic == IC - 1) else []))
                    if q == 3 or ic == IC - 1:
                        W("scalar", sT)
                        W("scalar", sV)
                        E("scalar", lambda ic=ic, bank=bank, q=q: e.activation(
                            dytm[:, (ic - q) * 128:(ic + 1) * 128],
                            ps[:, bank, 0:(q + 1) * 128], AF.Copy, bias=0.0,
                            scale=1.0), (sA, 1))
                        y_tick[bank] = sA.c
                W("vector", sA)
                E("vector", lambda: e.tensor_reduce(
                    dsc[:, 8:9], dytm[:, :], axis=AX.X, op=Op.max,
                    apply_absolute_value=True), (sV, 1))
                E("vector", lambda: e.tensor_scalar(
                    dsc[:, 9:10], dsc[:, 8:9], C2, EPS, op0=Op.mult, op1=Op.max),
                    (sV, 1))
                E("vector", lambda: e.tensor_scalar(
                    dsc[:, 9:10], dsc[:, 9:10], 1.0 / 127.0, None, op0=Op.mult),
                    (sV, 1))
                E("vector", lambda: e.reciprocal(dsc[:, 10:11], dsc[:, 9:10]),
                  (sV, 1))
                E("vector", lambda: e.tensor_scalar(
                    dsc[:, 10:11], dsc[:, 10:11], C2, None, op0=Op.mult), (sV, 1))
                E("vector", lambda: e.tensor_scalar(
                    dsc[:, 11:12], dsc[:, 9:10], g_["wsd"][:, 0:1], None,
                    op0=Op.mult), (sV, 1))
                for hf in range(2):
                    nq = max(I // 2 // HQ, 1)
                    specs = []
                    for qd in range(nq):
                        o0 = hf * (I // 2) + qd * HQ
                        specs.append((dytm[:, o0:o0 + HQ], HQ,
                                      dq1f[qd % 2][:, 0:HQ],
                                      dsc2[:, 0:HQ] if qd % 2 == 0
                                      else df1[:, 0:HQ],
                                      dq2t[:, qd * HQ:(qd + 1) * HQ]))
                    for p0 in range(0, nq, 2):
                        ladder_multi(specs[p0:p0 + 2], dsc[:, 10:11])
                    df1v_free = sV.c
                    for ici in range(IC // 2):
                        ic = hf * (IC // 2) + ici
                        half = (ici // 4) % 2
                        q = ici % 4
                        if q == 0:
                            W("tensor", sV)
                            W("tensor", sA, gu_bank_tick.get(6, 0))
                        W("tensor", sA, gu_bank_tick.get(7, 0))
                        E("tensor", lambda ici=ici, half=half, q=q: e.transpose(
                            psb[:, half * 512 + q * 128: half * 512 + (q + 1) * 128],
                            dq2t[:, ici * 128:(ici + 1) * 128], g_["idbf"][:, :]),
                            *([(sT, 1)] if (q == 3 or ici == IC // 2 - 1) else []))
                        if q == 3 or ici == IC // 2 - 1:
                            W("scalar", sT)
                            W("scalar", sT, dq2f_tick)
                            for qq in range(q + 1):
                                icq = ic - q + qq
                                E("scalar", lambda icq=icq, half=half, qq=qq:
                                  e.activation(
                                      dq2f[:, icq * FMT:(icq + 1) * FMT],
                                      psb[:, half * 512 + qq * 128:
                                          half * 512 + (qq + 1) * 128], AF.Copy,
                                      bias=0.0, scale=1.0),
                                    *([(sA, 1)] if qq == q else []))
                            gu_bank_tick[6 + half] = sA.c
                for dob in range(DBLK):
                    W("sync", sV, dos_tick)
                    E("sync", lambda mi=mi, dob=dob: e.dma_start(
                        out=dos[:, :],
                        in_=ap["oscr"][mi * 128:(mi + 1) * 128,
                                       dob * WBLK:(dob + 1) * WBLK]), (dX[1], 16))
                    bank = dob % 2
                    for hw_ in range(2):
                        W("sync", sT, wd_tick[hw_])
                        E("sync", lambda dob=dob, hw_=hw_: e.dma_start(
                            out=dwd[hw_][:, :].rearrange(
                                "p (c j) -> p c j", c=IC // 2),
                            in_=ap["wscr_d"][hw_ * (IC // 2):(hw_ + 1) * (IC // 2),
                                             :, dob * WBLK:(dob + 1) * WBLK]
                            .rearrange("c p j -> p c j")), (dK[hw_], 16))
                    for hw_ in range(2):
                        W("tensor", dK[hw_])
                        if hw_ == 0:
                            W("tensor", sA)
                            bankw(bank)
                        k = 0
                        for ici in range(IC // 2):
                            ic = hw_ * (IC // 2) + ici
                            k += 1
                            E("tensor", lambda ic=ic, ici=ici, bank=bank, k=k,
                              hw_=hw_, xsl=xsl: e.matmul(
                                  ps[:, bank, 0:WBLK],
                                  dq2f[:, ic * FMT:(ic + 1) * FMT],
                                  dwd[hw_][:, ici * WBLK:(ici + 1) * WBLK],
                                  start=(hw_ == 0 and k == 1),
                                  stop=(hw_ == 1 and k == IC // 2)),
                                *([(sT, 1)] if (hw_ == 1 and k == IC // 2)
                                  else []))
                        wd_tick[hw_] = sT.c
                    if dob == DBLK - 1:
                        dq2f_tick = sT.c
                    W("vector", sT)
                    W("vector", dX[1])
                    W("vector", gU[0])
                    E("vector", lambda bank=bank: e.scalar_tensor_tensor(
                        dou[0][:, :], ps[:, bank, 0:WBLK], dsc[:, 11:12],
                        dos[:, :], op0=Op.mult, op1=Op.add), (sV, 1))
                    dos_tick = sV.c
                    bank_sV[bank] = sV.c
                    W("gpsimd", sV)
                    E("gpsimd", lambda mi=mi, dob=dob: e.dma_start(
                        out=ap["out"][mi * 128:(mi + 1) * 128,
                                      dob * WBLK:(dob + 1) * WBLK],
                        in_=dou[0][:, :]), (gU[0], 16))
            W("gpsimd", gU[0])

    if 'D' in stages:
        run_stage(ctx, sD)

    stk.close()
    return nc



_NC_CACHE = {}


def kernel(**inputs):
    cfg = default_cfg()
    cd = derived(cfg)
    if "nc" not in _NC_CACHE:
        _NC_CACHE["nc"] = build_nc(cfg)
    nc = _NC_CACHE["nc"]
    in_maps = [host_core_inputs(cfg, inputs, c) for c in range(cfg["n_cores"])]
    res = run_bass_kernel_spmd(nc, in_maps, core_ids=list(range(cfg["n_cores"])))
    B, T, D = cfg["B"], cfg["T"], cfg["D"]
    out = np.empty((B, T, D), dtype=np.float32)
    for c in range(cfg["n_cores"]):
        b, p = c // cd["CPB"], c % cd["CPB"]
        o = np.asarray(res.results[c]["out"])
        for jt in range(cd["NT"]):
            out[b, (2 * jt + p) * 128:(2 * jt + p + 1) * 128, :] = \
                o[jt * 128:(jt + 1) * 128, :]
    return out


# version 09
# speedup vs baseline: 1.2724x; 1.1236x over previous
import contextlib
import numpy as np
import ml_dtypes
import concourse.bass as bass
import concourse.mybir as mybir
from concourse.alu_op_type import AluOpType as Op
from concourse.bass_utils import run_bass_kernel_spmd
from concourse import library_config

F32 = mybir.dt.float32
BF16 = mybir.dt.bfloat16
FP8 = mybir.dt.float8e4
AF = mybir.ActivationFunctionType
AX = mybir.AxisListType

NEG = np.float32(-3.0e38)
EPS = 1e-5


def default_cfg():
    return dict(B=4, T=4096, D=2048, H=16, I=8192, n_cores=8)


def derived(cfg):
    d = dict(cfg)
    T, D, I = cfg["T"], cfg["D"], cfg["I"]
    d["DC"] = D // 128
    d["IC"] = I // 128
    d["OT"] = T // 2
    d["NT"] = d["OT"] // 128
    d["TT"] = T // 128
    d["NG"] = max(d["NT"] // 2, 1)
    d["KB"] = min(512, T)
    d["WBLK"] = min(512, D)
    d["DBLK"] = D // d["WBLK"]
    d["WBI"] = min(512, I)
    d["OBI"] = I // d["WBI"]
    d["AIT"] = min(1024, T)
    d["AIT2"] = min(1024, d["OT"])
    d["FMT"] = 128
    d["NM"] = d["OT"] // d["FMT"]
    d["CPB"] = cfg["n_cores"] // cfg["B"]
    return d


def kb_of_group(g):
    return g + 1



def _hadamard(n):
    h = np.array([[1.0]], dtype=np.float64)
    while h.shape[0] < n:
        h = np.block([[h, h], [h, -h]])
    return h


def _rope_tables(positions, scale):
    DH = 128
    j = np.arange(DH // 2, dtype=np.float64)
    theta = 10000.0 ** (-2.0 * j / DH)
    ang = positions.astype(np.float64)[:, None] * theta[None, :]
    out = []
    for t in (np.cos(ang) * scale, np.sin(ang) * scale):
        hi = t.astype(np.float32)
        lo = (t - hi).astype(np.float32)
        out += [hi, lo]
    return out


def _tile_major(a):
    T = a.shape[0]
    return np.ascontiguousarray(
        a.reshape(T // 128, 128, 64).transpose(1, 0, 2).reshape(128, -1))


def _deint_perm(D):
    DH = 128
    perm = []
    for h0 in range(0, D, DH):
        perm += [h0 + 2 * j for j in range(DH // 2)]
        perm += [h0 + 2 * j + 1 for j in range(DH // 2)]
    return np.array(perm)


def host_core_inputs(cfg, inputs, core, names=None):
    cd = derived(cfg)
    T, D = cfg["T"], cfg["D"]
    b, p = core // cd["CPB"], core % cd["CPB"]
    nm = names or {}
    X = np.asarray(inputs[nm.get("X", "L0_attn_X")])
    Wq = np.asarray(inputs[nm.get("Wq", "L0_attn_Wq")])
    Wk = np.asarray(inputs[nm.get("Wk", "L0_attn_Wk")])
    Wv = np.asarray(inputs[nm.get("Wv", "L0_attn_Wv")])
    Wo = np.asarray(inputs[nm.get("Wo", "L0_attn_Wo")])
    Wg = np.asarray(inputs[nm.get("Wg", "L0_ffn_w_gate")])
    Wu = np.asarray(inputs[nm.get("Wu", "L0_ffn_w_up")])
    Wd = np.asarray(inputs[nm.get("Wd", "L0_ffn_w_down")])
    own_rows = np.concatenate(
        [np.arange((2 * jt + p) * 128, (2 * jt + p + 1) * 128)
         for jt in range(cd["NT"])])
    perm = _deint_perm(D)

    m = {}
    m["xf"] = np.ascontiguousarray(X[b], dtype=np.float32)
    m["xo"] = np.ascontiguousarray(X[b][own_rows], dtype=np.float32)
    m["wqT"] = np.ascontiguousarray(Wq[perm].T, dtype=np.float32)
    m["wkT"] = np.ascontiguousarray(Wk[perm].T, dtype=np.float32)
    m["wvT"] = np.ascontiguousarray(Wv.T, dtype=np.float32)
    m["woT"] = np.ascontiguousarray(Wo.T, dtype=np.float32)
    m["wgT"] = np.ascontiguousarray(Wg.T, dtype=np.float32)
    m["wuT"] = np.ascontiguousarray(Wu.T, dtype=np.float32)
    m["wdT"] = np.ascontiguousarray(Wd.T, dtype=np.float32)

    kt = _rope_tables(np.arange(T), 1.0)
    m["tabk"] = np.stack([_tile_major(x) for x in kt], 0)
    qt = _rope_tables(own_rows, 128.0 ** -0.5)
    m["tabq"] = np.stack([_tile_major(x) for x in qt], 0)

    masks = np.zeros((cd["NT"], 128, cd["KB"]), dtype=np.float32)
    for jt in range(cd["NT"]):
        gq, g = 2 * jt + p, jt // 2
        k0 = (kb_of_group(g) - 1) * cd["KB"]
        i = np.arange(128)[:, None]
        jj = np.arange(cd["KB"])[None, :]
        masks[jt] = np.where(k0 + jj > gq * 128 + i, NEG, 0.0)
    m["masks"] = masks.astype(ml_dtypes.bfloat16)

    m["hmat"] = _hadamard(128).astype(ml_dtypes.bfloat16)
    m["id32"] = np.eye(128, dtype=np.float32)
    m["idbf"] = np.eye(128).astype(ml_dtypes.bfloat16)
    return m


def host_input_specs(cfg):
    cd = derived(cfg)
    T, D, I = cfg["T"], cfg["D"], cfg["I"]
    OT, NT, TT, KB = cd["OT"], cd["NT"], cd["TT"], cd["KB"]
    return [
        ("xf", [T, D], F32), ("xo", [OT, D], F32),
        ("wqT", [D, D], F32), ("wkT", [D, D], F32),
        ("wvT", [D, D], F32), ("woT", [D, D], F32),
        ("wgT", [D, I], F32), ("wuT", [D, I], F32), ("wdT", [I, D], F32),
        ("tabk", [4, 128, TT * 64], F32), ("tabq", [4, 128, NT * 64], F32),
        ("masks", [NT, 128, KB], BF16),
        ("hmat", [128, 128], BF16), ("id32", [128, 128], F32),
        ("idbf", [128, 128], BF16),
    ]



class Sm:
    __slots__ = ("h", "c")

    def __init__(self, h):
        self.h = h
        self.c = 0


ENGS = ("sync", "tensor", "vector", "scalar", "gpsimd")


class Ctx:
    def balloc(self, name, shape, dt):
        if name not in self.stage_cache:
            self.stage_cache[name] = self.stage_stack.enter_context(
                self.nc.sbuf_tensor("s%d_%s" % (self.stage_id, name), shape, dt))
        return self.stage_cache[name]

    def balloc2(self, name, shape, dt):
        if name not in self.stage_cache:
            self.stage_cache[name] = DBuf(self.stage_stack.enter_context(
                self.nc.sbuf_tensor("s%d_%s" % (self.stage_id, name), shape, dt)))
        return self.stage_cache[name]


def run_stage(ctx, fn):
    snap = [(s, s.c) for s in ctx.all_sems]
    ctx.stage_cache = {}
    ctx.stage_id = getattr(ctx, "stage_id", -1) + 1
    ctx.stage_stack = contextlib.ExitStack()
    with ctx.nc.Block() as blk:
        for eng in ENGS:
            for s, v in snap:
                s.c = v
            getattr(blk, eng)(_section(fn, ctx, eng))
    ctx.stage_stack.close()
    ctx.stage_cache = None


def _section(fn, ctx, eng):
    def sec(e):
        fn(ctx, e, eng)
    return sec


def mk_EW(e, eng, sems_all=None, own=None):
    last = [0]

    def E(tgt, make, *incs):
        if eng == tgt:
            if own is not None and own.c > last[0]:
                e.wait_ge(own.h, own.c)
                last[0] = own.c
            r = make()
            for (s, n) in incs:
                r = r.then_inc(s.h, n)
        for (s, n) in incs:
            s.c += n

    def W(tgt, s, v=None):
        if eng == tgt:
            e.wait_ge(s.h, s.c if v is None else v)

    def B():
        for s in (sems_all or ()):
            if s.c:
                e.wait_ge(s.h, s.c)
    return E, W, B



class DBuf:

    def __init__(self, t):
        self.t = t

    def __getitem__(self, sl):
        return _Slot(self.t, sl)


class _Slot:
    def __init__(self, t, sl):
        self.t = t
        self.sl = sl

    def __getitem__(self, idx):
        p, f = idx
        return self.t[p, self.sl, f]


def _bsel2(apat, stp, which, hb, nh):
    nch = apat.shape[1]
    ng = nch // (2 * stp)
    r = apat.rearrange("p (g t c) j -> p g t c j", g=ng, t=2, c=stp)
    sel = r[:, :, which, :, :]
    if ng >= nh:
        gp = ng // nh
        return sel[:, hb * gp:(hb + 1) * gp, :, :]
    cp = stp // nh
    return sel[:, :, hb * cp:(hb + 1) * cp, :]


def _bsel(apat, stp, which):
    nch = apat.shape[1]
    ng = nch // (2 * stp)
    r = apat.rearrange("p (g t c) j -> p g t c j", g=ng, t=2, c=stp)
    return r[:, :, which, :, :]



def build_nc(cfg, stages=('0', 'A1', 'A2', 'B', 'C', 'D')):
    cd = derived(cfg)
    T, D, H, I = cfg["T"], cfg["D"], cfg["H"], cfg["I"]
    DC, IC, OT, NT, TT, NG, KB = (cd["DC"], cd["IC"], cd["OT"], cd["NT"],
                                  cd["TT"], cd["NG"], cd["KB"])
    WBLK, DBLK, WBI, OBI = cd["WBLK"], cd["DBLK"], cd["WBI"], cd["OBI"]
    AIT, AIT2, FMT, NM = cd["AIT"], cd["AIT2"], cd["FMT"], cd["NM"]
    HPB = WBLK // 128
    C1 = float(1.0 / np.sqrt(D))
    C2 = float(1.0 / np.sqrt(I))
    L1 = int(np.log2(DC))
    L2 = int(np.log2(IC))

    nc = bass.Bass("TRN2", target_bir_lowering=False)
    ctx = Ctx()
    ctx.nc = nc
    stk = contextlib.ExitStack()

    ap = {}
    for name, shape, dt in host_input_specs(cfg):
        ap[name] = nc.dram_tensor(name, shape, dt, kind="ExternalInput").ap()
    ap["out"] = nc.dram_tensor("out", [OT, D], F32, kind="ExternalOutput").ap()
    for name, shape, dt in [
            ("kscr_h", [T, D], BF16), ("kscr_l", [T, D], BF16),
            ("vscr", [T, D], F32),
            ("qscr_h", [OT, D], BF16), ("qscr_l", [OT, D], BF16),
            ("xnscr_h", [DC, 128, OT], BF16), ("xnscr_l", [DC, 128, OT], BF16),
            ("wt_h", [3, DC, 128, D], BF16), ("wt_l", [3, DC, 128, D], BF16),
            ("wot_h", [DC, 128, D], BF16), ("wot_l", [DC, 128, D], BF16),
            ("wscr_g", [DC, 128, I], FP8), ("wscr_u", [DC, 128, I], FP8),
            ("wscr_d", [IC, 128, D], FP8),
            ("hscr_h", [H, 128, OT], BF16), ("hscr_l", [H, 128, OT], BF16),
            ("oscr", [OT, D], F32)]:
        ap[name] = nc.dram_tensor(name, shape, dt, kind="Internal").ap()

    sems = {}
    for name in ("sT", "sV", "sA", "sP",
                 "dC", "dT", "dT2", "dW0", "dW1", "dX0", "dX1",
                 "dK0", "dK1", "dQ", "dD",
                 "gW0", "gW1", "gK0", "gK1", "gV0", "gV1", "gX",
                 "gH0", "gH1", "gO0", "gO1", "gU0", "gU1"):
        sems[name] = Sm(stk.enter_context(nc.semaphore(name)))
    ctx.all_sems = list(sems.values())
    ctx.own_map = {"tensor": sems["sT"], "vector": sems["sV"],
                   "scalar": sems["sA"], "gpsimd": sems["sP"], "sync": None}
    sT, sV, sA, sP = (sems[k] for k in ("sT", "sV", "sA", "sP"))
    dC, dT, dT2, dQ, dD, gX = (sems[k] for k in ("dC", "dT", "dT2", "dQ", "dD", "gX"))
    dW = [sems["dW0"], sems["dW1"]]
    dX = [sems["dX0"], sems["dX1"]]
    dK = [sems["dK0"], sems["dK1"]]
    gW = [sems["gW0"], sems["gW1"]]
    gK = [sems["gK0"], sems["gK1"]]
    gV = [sems["gV0"], sems["gV1"]]
    gH = [sems["gH0"], sems["gH1"]]
    gO = [sems["gO0"], sems["gO1"]]
    gU = [sems["gU0"], sems["gU1"]]

    g_ = {}
    for name, shape, dt in [
            ("hmat", [128, 128], BF16), ("id32", [128, 128], F32),
            ("idbf", [128, 128], BF16), ("cbias", [128, NT * H], F32),
            ("linv", [128, NT * H], F32), ("qn2", [128, NT * H], F32),
            ("kn2", [128, H * TT], F32), ("kmax", [128, H], F32),
            ("wsg", [128, 1], F32), ("wsu", [128, 1], F32),
            ("wsd", [128, 1], F32), ("kc0", [128, 2], F32), ("ones1", [1, 128], F32)]:
        g_[name] = stk.enter_context(nc.sbuf_tensor("g_" + name, shape, dt))

    ps = stk.enter_context(nc.psum_tensor("ps", [128, 7, 512], F32))
    psb = stk.enter_context(nc.psum_tensor("psb", [128, 1024], BF16))

    def s0(ctx, e, eng):
        E, W, B = mk_EW(e, eng, ctx.all_sems, ctx.own_map[eng])
        B()
        for n in ("hmat", "id32", "idbf"):
            E("sync", lambda n=n: e.dma_start(out=g_[n][:, :], in_=ap[n][:, :]),
              (dC, 16))
        E("vector", lambda: e.memset(g_["ones1"][0:1, :], 1.0), (sV, 1))
        E("vector", lambda: e.memset(g_["kc0"][:, 0:1], 0.0), (sV, 1))
        E("vector", lambda: e.memset(g_["kc0"][:, 1:2], float(EPS)), (sV, 1))
        if True:
            win = ctx.balloc2("w0_in", [128, 2, max(I, D)], F32)
            whi = ctx.balloc2("w0_hi", [128, 2, D], BF16)
            wlo = ctx.balloc2("w0_lo", [128, 2, D], BF16)
            wt8 = ctx.balloc2("w0_t8", [128, 2, max(I, D)], FP8)
            wle = ctx.balloc("w0_le", [128, max(I, D)], F32)
            wms = ctx.balloc("w0_ms", [128, max(DC, IC)], F32)
            wsc = ctx.balloc("w0_sc", [128, 8], F32)
            whn = ctx.balloc("w0_hn", [128, 2], F32)
            srcs = [("wqT", "wt_h", "wt_l", 0), ("wkT", "wt_h", "wt_l", 1),
                    ("wvT", "wt_h", "wt_l", 2), ("woT", "wot_h", "wot_l", None)]
            winck = {0: 0, 1: 0}
            for wi, (src, dh_, dl_, idx) in enumerate(srcs):
                for rc in range(DC):
                    it = wi * DC + rc
                    sl = it % 2
                    W("sync", sV, winck[sl])
                    E("sync", lambda src=src, rc=rc, sl=sl: e.dma_start(
                        out=win[sl][:, 0:D],
                        in_=ap[src][rc * 128:(rc + 1) * 128, :]), (dW[sl], 16))
                    W("vector", dW[sl])
                    W("vector", gW[sl])
                    E("vector", lambda sl=sl: e.tensor_copy(
                        whi[sl][:, :], win[sl][:, 0:D]), (sV, 1))
                    W("vector", sV)
                    E("vector", lambda sl=sl: e.tensor_sub(
                        wlo[sl][:, :], win[sl][:, 0:D], whi[sl][:, :]), (sV, 1))
                    winck[sl] = sV.c
                    W("gpsimd", sV)
                    if idx is None:
                        dsth, dstl = ap[dh_][rc], ap[dl_][rc]
                    else:
                        dsth, dstl = ap[dh_][idx, rc], ap[dl_][idx, rc]
                    E("gpsimd", lambda d=dsth, sl=sl: e.dma_start(
                        out=d[:, :], in_=whi[sl][:, :]), (gW[sl], 16))
                    E("gpsimd", lambda d=dstl, sl=sl: e.dma_start(
                        out=d[:, :], in_=wlo[sl][:, :]), (gW[sl], 16))

            for (src, dst, wsn, NR, NW) in [
                    ("wgT", "wscr_g", "wsg", DC, I),
                    ("wuT", "wscr_u", "wsu", DC, I),
                    ("wdT", "wscr_d", "wsd", IC, D)]:
                for rc in range(NR):
                    sl = rc % 2
                    W("sync", sV, winck[sl])
                    E("sync", lambda src=src, rc=rc, sl=sl, NW=NW: e.dma_start(
                        out=win[sl][:, 0:NW],
                        in_=ap[src][rc * 128:(rc + 1) * 128, :]), (dW[sl], 16))
                    W("vector", dW[sl])
                    E("vector", lambda rc=rc, sl=sl, NW=NW: e.tensor_reduce(
                        wms[:, rc:rc + 1], win[sl][:, 0:NW], axis=AX.X,
                        op=Op.add, apply_absolute_value=True), (sV, 1))
                    winck[sl] = sV.c
                W("vector", sV)
                E("vector", lambda NR=NR: e.tensor_reduce(
                    wsc[:, 0:1], wms[:, 0:NR], axis=AX.X, op=Op.add), (sV, 1))
                W("tensor", sV)
                W("tensor", dC)
                E("tensor", lambda: e.transpose(
                    ps[0:1, 6, 0:128], wsc[:, 0:1], g_["id32"][:, :]), (sT, 1))
                W("vector", sT)
                E("vector", lambda: e.tensor_reduce(
                    wsc[0:1, 1:2], ps[0:1, 6, 0:128], axis=AX.X, op=Op.add), (sV, 1))
                W("vector", sV)
                E("vector", lambda NR=NR, NW=NW: e.tensor_scalar(
                    wsc[0:1, 2:3], wsc[0:1, 1:2], 1.0 / (NR * 128 * NW), EPS,
                    op0=Op.mult, op1=Op.max), (sV, 1))
                W("tensor", sV)
                E("tensor", lambda: e.matmul(
                    ps[:, 6, 0:1], g_["ones1"][0:1, :], wsc[0:1, 2:3],
                    start=True, stop=True), (sT, 1))
                W("scalar", sT)
                E("scalar", lambda wsn=wsn: e.activation(
                    g_[wsn][:, 0:1], ps[:, 6, 0:1], AF.Copy, bias=0.0,
                    scale=1.0), (sA, 1))
                W("vector", sA)
                E("vector", lambda wsn=wsn: e.tensor_scalar(
                    whn[:, 0:1], g_[wsn][:, 0:1], 0.5, None, op0=Op.mult), (sV, 1))
                W("vector", sV)
                E("vector", lambda: e.tensor_scalar(
                    whn[:, 1:2], whn[:, 0:1], -1.0, None, op0=Op.mult), (sV, 1))
                for rc in range(NR):
                    sl = rc % 2
                    W("sync", sV, winck[sl])
                    E("sync", lambda src=src, rc=rc, sl=sl, NW=NW: e.dma_start(
                        out=win[sl][:, 0:NW],
                        in_=ap[src][rc * 128:(rc + 1) * 128, :]), (dW[sl], 16))
                    W("vector", dW[sl])
                    W("vector", sV)
                    E("vector", lambda sl=sl, NW=NW: e.tensor_scalar(
                        wle[:, 0:NW], win[sl][:, 0:NW], whn[:, 1:2], None,
                        op0=Op.is_le), (sV, 1))
                    W("vector", sV)
                    W("vector", gW[sl])
                    E("vector", lambda sl=sl, NW=NW: e.scalar_tensor_tensor(
                        wt8[sl][:, 0:NW], win[sl][:, 0:NW], whn[:, 0:1],
                        wle[:, 0:NW], op0=Op.is_ge, op1=Op.subtract), (sV, 1))
                    winck[sl] = sV.c
                    W("gpsimd", sV)
                    E("gpsimd", lambda dst=dst, rc=rc, sl=sl, NW=NW: e.dma_start(
                        out=ap[dst][rc], in_=wt8[sl][:, 0:NW]), (gW[sl], 16))

    if '0' in stages:
        run_stage(ctx, s0)

    def make_sA(own):
        xsrc = "xo" if own else "xf"
        tab = "tabq" if own else "tabk"
        ait = AIT2 if own else AIT
        niter = (OT if own else T) // ait
        tpi = ait // 128

        def sA_(ctx, e, eng):
            E, W, B = mk_EW(e, eng, ctx.all_sems, ctx.own_map[eng])
            B()
            W("tensor", dC)
            if True:
                ax = ctx.balloc2("a_x", [128, 2, D], F32)
                axn = ctx.balloc("a_xn", [128, D], F32)
                ast = ctx.balloc("a_st", [128, 8], F32)
                axh = ctx.balloc("a_xh", [128, DC * ait], BF16)
                axl = ctx.balloc("a_xl", [128, DC * ait], BF16)
                awh = ctx.balloc2("a_wh", [128, 2, DC * WBLK], BF16)
                awl = ctx.balloc2("a_wl", [128, 2, DC * WBLK], BF16)
                act_ = ctx.balloc("a_ct", [128, tpi * 64], F32)
                astb = ctx.balloc("a_stb", [128, tpi * 64], F32)
                ajnk = ctx.balloc("a_jnk", [128, D], F32)
                aro = ctx.balloc("a_ro", [128, 128], F32)
                arh = ctx.balloc("a_rh", [128, 128], BF16)
                arl = ctx.balloc("a_rl", [128, 128], BF16)
                ake = ctx.balloc2("a_ke", [128, 2, 256], BF16)
                ave = ctx.balloc2("a_ve", [128, 2, WBLK], F32)
                anrm = ctx.balloc("a_nrm", [128, tpi * H], F32)
                wtick = {0: 0, 1: 0}
                pb_tick = {}
                ketick = {0: 0, 1: 0}
                vetick = {0: 0, 1: 0}
                abct = {"ke": 0, "ve": 0}

                def rope_head(bank, tc, gt, hglob):
                    x0 = ps[:, bank, (hglob % HPB) * 128:(hglob % HPB) * 128 + 64]
                    x1 = ps[:, bank, (hglob % HPB) * 128 + 64:(hglob % HPB) * 128 + 128]
                    ct = act_[:, tc * 64:(tc + 1) * 64]
                    st_ = astb[:, tc * 64:(tc + 1) * 64]
                    W("vector", sT)
                    E("vector", lambda: e.tensor_mul(ajnk[:, 0:64], x0, ct), (sV, 1))
                    E("vector", lambda: e.tensor_mul(ajnk[:, 64:128], x1, st_), (sV, 1))
                    E("vector", lambda: e.tensor_mul(ajnk[:, 128:192], x0, st_), (sV, 1))
                    E("vector", lambda: e.tensor_mul(ajnk[:, 192:256], x1, ct), (sV, 1))
                    W("vector", sV)
                    E("vector", lambda: e.tensor_sub(
                        aro[:, 0:64], ajnk[:, 0:64], ajnk[:, 64:128]), (sV, 1))
                    E("vector", lambda: e.tensor_add(
                        aro[:, 64:128], ajnk[:, 128:192], ajnk[:, 192:256]), (sV, 1))
                    W("vector", sV)
                    if own:
                        ncol = g_["qn2"][:, gt * H + hglob: gt * H + hglob + 1]
                    else:
                        ncol = g_["kn2"][:, hglob * TT + gt: hglob * TT + gt + 1]
                    E("vector", lambda ncol=ncol: e.scalar_tensor_tensor(
                        ajnk[:, 0:128], aro[:, :], 1.0, aro[:, :],
                        op0=Op.mult, op1=Op.mult, accum_out=ncol), (sV, 1))
                    kesl = abct["ke"] % 2
                    abct["ke"] += 1
                    W("vector", gK[kesl])
                    E("vector", lambda: e.tensor_copy(
                        ake[kesl][:, 0:128], aro[:, :]), (sV, 1))
                    W("vector", sV)
                    E("vector", lambda: e.tensor_sub(
                        ake[kesl][:, 128:256], aro[:, :], ake[kesl][:, 0:128]),
                        (sV, 1))
                    W("gpsimd", sV)
                    kdst = "qscr" if own else "kscr"
                    E("gpsimd", lambda: e.dma_start(
                        out=ap[kdst + "_h"][gt * 128:(gt + 1) * 128,
                                            hglob * 128:(hglob + 1) * 128],
                        in_=ake[kesl][:, 0:128]), (gK[kesl], 16))
                    E("gpsimd", lambda: e.dma_start(
                        out=ap[kdst + "_l"][gt * 128:(gt + 1) * 128,
                                            hglob * 128:(hglob + 1) * 128],
                        in_=ake[kesl][:, 128:256]), (gK[kesl], 16))

                def proj_tile(it, tc, blk, sl, isrope, xtick):
                    bank = 4 + (tc % 2)
                    gt = it * tpi + tc
                    W("tensor", dW[sl])
                    W("tensor", sV, xtick[tc])
                    W("tensor", sV, pb_tick.get((bank, "v"), 0))
                    W("tensor", sA, pb_tick.get((bank, "a"), 0))
                    k, n3 = 0, 3 * DC
                    for dc in range(DC):
                        c0 = dc * ait + tc * 128
                        xh_t = axh[:, c0:c0 + 128]
                        xl_t = axl[:, c0:c0 + 128]
                        wh_t = awh[sl][:, dc * WBLK:(dc + 1) * WBLK]
                        wl_t = awl[sl][:, dc * WBLK:(dc + 1) * WBLK]
                        for (sx, mv) in ((xh_t, wh_t), (xh_t, wl_t), (xl_t, wh_t)):
                            k += 1
                            E("tensor", lambda sx=sx, mv=mv, bank=bank, k=k, n3=n3:
                              e.matmul(ps[:, bank, 0:WBLK], sx, mv, start=(k == 1),
                                       stop=(k == n3)),
                                *([(sT, 1)] if k == n3 else []))
                    if isrope:
                        for hh in range(HPB):
                            rope_head(bank, tc, gt, blk * HPB + hh)
                        pb_tick[(bank, "v")] = sV.c
                    else:
                        vsl = abct["ve"] % 2
                        abct["ve"] += 1
                        W("scalar", sT)
                        W("scalar", gV[vsl])
                        E("scalar", lambda: e.activation(
                            ave[vsl][:, :], ps[:, bank, 0:WBLK], AF.Copy, bias=0.0,
                            scale=1.0), (sA, 1))
                        pb_tick[(bank, "a")] = sA.c
                        W("gpsimd", sA)
                        E("gpsimd", lambda: e.dma_start(
                            out=ap["vscr"][gt * 128:(gt + 1) * 128,
            blk * WBLK:(blk + 1) * WBLK],
                            in_=ave[vsl][:, :]), (gV[vsl], 16))

                for it in range(niter):
                    W("vector", sT)
                    W("vector", gX)
                    W("scalar", sT)
                    W("scalar", gX)
                    W("sync", sV)
                    E("sync", lambda it=it: e.dma_start(
                        out=act_[:, :],
                        in_=ap[tab][0][:, it * tpi * 64:(it + 1) * tpi * 64]),
                        (dT, 16))
                    E("sync", lambda it=it: e.dma_start(
                        out=ajnk[:, 0:tpi * 64],
                        in_=ap[tab][1][:, it * tpi * 64:(it + 1) * tpi * 64]),
                        (dT, 16))
                    W("vector", dT)
                    E("vector", lambda: e.tensor_add(
                        act_[:, :], act_[:, :], ajnk[:, 0:tpi * 64]), (sV, 1))
                    E("sync", lambda it=it: e.dma_start(
                        out=astb[:, :],
                        in_=ap[tab][2][:, it * tpi * 64:(it + 1) * tpi * 64]),
                        (dT2, 16))
                    W("sync", sV)
                    E("sync", lambda it=it: e.dma_start(
                        out=ajnk[:, 0:tpi * 64],
                        in_=ap[tab][3][:, it * tpi * 64:(it + 1) * tpi * 64]),
                        (dT2, 16))
                    W("vector", dT2)
                    E("vector", lambda: e.tensor_add(
                        astb[:, :], astb[:, :], ajnk[:, 0:tpi * 64]), (sV, 1))

                    import os as _os
                    _tpr = int(_os.environ.get("KLN", tpi) if _os.environ.get("KLN", "") != "off" else 0)
                    xtick = []
                    for tc in range(_tpr):
                        sl = tc % 2
                        if tc >= 2:
                            W("sync", sV, xtick[tc - 2])
                        E("sync", lambda it=it, tc=tc, sl=sl: e.dma_start(
                            out=ax[sl][:, :],
                            in_=ap[xsrc][(it * tpi + tc) * 128:
                                         (it * tpi + tc + 1) * 128, :]),
                            (dX[sl], 16))
                        _kcut = int(_os.environ.get("KCUT", 5))
                        if _kcut < 2:
                            xtick.append(sV.c)
                            continue
                        W("vector", dX[sl])
                        E("vector", lambda sl=sl: e.tensor_scalar(
                            ajnk[:, 0:D], ax[sl][:, :], 1.0, 0.0, op0=Op.mult,
                            op1=Op.add, accum_out=ast[:, 0:1]), (sV, 1))
                        W("vector", sV)
                        E("vector", lambda sl=sl: e.scalar_tensor_tensor(
                            ajnk[:, 0:D], ax[sl][:, :], 1.0, ax[sl][:, :],
                            op0=Op.mult, op1=Op.mult,
                            accum_out=ast[:, 1:2]), (sV, 1))
                        if _kcut < 3:
                            xtick.append(sV.c)
                            continue
                        W("vector", sV)
                        E("vector", lambda: e.tensor_scalar(
                            ast[:, 2:3], ast[:, 0:1], 1.0 / D, None, op0=Op.mult),
                            (sV, 1))
                        W("vector", sV)
                        E("vector", lambda: e.tensor_scalar(
                            ast[:, 3:4], ast[:, 1:2], 1.0 / D, None, op0=Op.mult),
                            (sV, 1))
                        W("vector", sV)
                        E("vector", lambda: e.scalar_tensor_tensor(
                            ast[:, 4:5], ast[:, 2:3], ast[:, 2:3], ast[:, 3:4],
                            op0=Op.mult, op1=Op.subtract), (sV, 1))
                        W("scalar", sV)
                        E("scalar", lambda: e.activation(
                            ast[:, 5:6], ast[:, 4:5], AF.Sqrt,
                            bias=g_["kc0"][:, 1:2], scale=-1.0), (sA, 1))
                        W("vector", sA)
                        E("vector", lambda: e.reciprocal(ast[:, 6:7], ast[:, 5:6]),
                          (sV, 1))
                        W("vector", sV)
                        E("vector", lambda sl=sl: e.tensor_scalar(
                            axn[:, :], ax[sl][:, :], ast[:, 2:3], ast[:, 6:7],
                            op0=Op.subtract, op1=Op.mult), (sV, 1))
                        if _kcut < 4:
                            xtick.append(sV.c)
                            continue
                        nb = (DC + 3) // 4
                        for bk in range(nb):
                            cnt = min(4, DC - bk * 4)
                            W("tensor", sV)
                            W("tensor", sA)
                            for q in range(cnt):
                                dc = bk * 4 + q
                                E("tensor", lambda dc=dc, bk=bk, q=q: e.transpose(
                                    ps[:, bk % 4, q * 128:(q + 1) * 128],
                                    axn[:, dc * 128:(dc + 1) * 128],
                                    g_["id32"][:, :]),
                                    *([(sT, 1)] if q == cnt - 1 else []))
                            for q in range(cnt):
                                if _kcut < 5:
                                    continue
                                dc = bk * 4 + q
                                col = dc * ait + tc * 128
                                if _os.environ.get("KEVAC", "dve") == "dve":
                                    W("vector", sT)
                                    E("vector", lambda bk=bk, q=q, col=col:
                                      e.tensor_copy(
                                          axh[:, col:col + 128],
                                          ps[:, bk % 4, q * 128:(q + 1) * 128]),
                                        (sA, 1))
                                else:
                                    W("scalar", sT)
                                    E("scalar", lambda bk=bk, q=q, col=col: e.activation(
                                        axh[:, col:col + 128],
                                        ps[:, bk % 4, q * 128:(q + 1) * 128],
                                        AF.Copy, bias=0.0, scale=1.0), (sA, 1))
                                W("vector", sA)
                                E("vector", lambda bk=bk, q=q, col=col:
                                  e.scalar_tensor_tensor(
                                      axl[:, col:col + 128],
                                      ps[:, bk % 4, q * 128:(q + 1) * 128], 0.0,
                                      axh[:, col:col + 128], op0=Op.add,
                                      op1=Op.subtract), (sV, 1))
                        xtick.append(sV.c)
                    if own:
                        W("gpsimd", sV)
                        for dc in range(DC):
                            E("gpsimd", lambda dc=dc, it=it: e.dma_start(
                                out=ap["xnscr_h"][dc][:, it * ait:(it + 1) * ait],
                                in_=axh[:, dc * ait:(dc + 1) * ait]), (gX, 16))
                            E("gpsimd", lambda dc=dc, it=it: e.dma_start(
                                out=ap["xnscr_l"][dc][:, it * ait:(it + 1) * ait],
                                in_=axl[:, dc * ait:(dc + 1) * ait]), (gX, 16))

                    projs = [(0, True)] if own else [(1, True), (2, False)]
                    import os
                    _pj = os.environ.get("KPROJ", "")
                    if _pj == "none":
                        projs = []
                    elif _pj == "k" and not own:
                        projs = [(1, True)]
                    elif _pj == "v" and not own:
                        projs = [(2, False)]
                    for (widx, isrope) in projs:
                        for blk in range(DBLK):
                            gwi = (it * 3 + widx) * DBLK + blk
                            sl = gwi % 2
                            W("sync", sT, wtick[sl])
                            for part, buf in (("wt_h", awh), ("wt_l", awl)):
                                E("sync", lambda part=part, widx=widx, blk=blk,
                                  buf=buf, sl=sl: e.dma_start(
                                      out=buf[sl][:, :].rearrange(
            "p (c j) -> p c j", c=DC),
                                      in_=ap[part][widx][:, :,
            blk * WBLK:(blk + 1) * WBLK]
                                      .rearrange("c p j -> p c j")), (dW[sl], 16))
                            for tc in range(tpi):
                                proj_tile(it, tc, blk, sl, isrope, xtick)
                            wtick[sl] = sT.c
        return sA_

    if 'A1' in stages:
        run_stage(ctx, make_sA(False))
    if 'A2' in stages:
        run_stage(ctx, make_sA(True))

    def sB(ctx, e, eng):
        E, W, B = mk_EW(e, eng, ctx.all_sems, ctx.own_map[eng])
        B()
        W("tensor", dC)
        NGKB = NG * KB
        if True:
            bmk = ctx.balloc("b_mk", [128, NT * KB], BF16)
            bqh = ctx.balloc("b_qh", [128, H * 256], BF16)
            bql = ctx.balloc("b_ql", [128, H * 256], BF16)
            bkh = ctx.balloc2("b_kh", [128, 2, NGKB], BF16)
            bkl = ctx.balloc2("b_kl", [128, 2, NGKB], BF16)
            bv = ctx.balloc2("b_v", [128, 2, NGKB], F32)
            bp = ctx.balloc("b_p", [128, 2 * KB], F32)
            bpt = ctx.balloc("b_pt", [128, 2 * KB], F32)
            blt = ctx.balloc("b_lt", [128, 2 * (NG + 2)], F32)
            bsc = ctx.balloc("b_sc", [128, 4], F32)
            blb = ctx.balloc("b_lb", [128, 256], F32)
            bhb = ctx.balloc2("b_hb", [128, 2, 256], F32)
            bhe = ctx.balloc2("b_he", [128, 2, 256], BF16)
            for jt in range(NT):
                E("sync", lambda jt=jt: e.dma_start(
                    out=bmk[:, jt * KB:(jt + 1) * KB], in_=ap["masks"][jt]),
                    (dC, 16))
            W("vector", dC)
            W("sync", gK[0])
            W("sync", gK[1])
            W("sync", gV[0])
            W("sync", gV[1])
            W("sync", gX)
            for h in range(H):
                E("vector", lambda h=h: e.tensor_reduce(
                    g_["kmax"][:, h:h + 1],
                    g_["kn2"][:, h * TT:(h + 1) * TT], axis=AX.X, op=Op.max),
                    (sV, 1))
            W("tensor", sV)
            E("tensor", lambda: e.transpose(
                ps[0:H, 6, 0:128], g_["kmax"][:, 0:H], g_["id32"][:, :]), (sT, 1))
            W("vector", sT)
            E("vector", lambda: e.tensor_reduce(
                bsc[0:H, 0:1], ps[0:H, 6, 0:128], axis=AX.X, op=Op.max), (sV, 1))
            W("scalar", sV)
            E("scalar", lambda: e.activation(
                bsc[0:H, 1:2], bsc[0:H, 0:1], AF.Sqrt, bias=g_["kc0"][0:H, 0:1],
                scale=1.0), (sA, 1))
            W("vector", sA)
            E("vector", lambda: e.tensor_scalar(
                bsc[0:H, 2:3], bsc[0:H, 1:2], -1.02, None, op0=Op.mult), (sV, 1))
            W("tensor", sV)
            E("tensor", lambda: e.transpose(
                ps[0:1, 6, 0:H], bsc[0:H, 2:3], g_["id32"][0:H, 0:H]), (sT, 1))
            W("scalar", sT)
            E("scalar", lambda: e.activation(
                bsc[0:1, 0:H] if H <= 4 else blb[0:1, 0:H],
                ps[0:1, 6, 0:H], AF.Copy, bias=0.0, scale=1.0), (sA, 1))
            W("tensor", sA)
            E("tensor", lambda: e.matmul(
                ps[:, 6, 0:H], g_["ones1"][0:1, :],
                bsc[0:1, 0:H] if H <= 4 else blb[0:1, 0:H],
                start=True, stop=True), (sT, 1))
            W("scalar", sT)
            E("scalar", lambda: e.activation(
                g_["kmax"][:, 0:H], ps[:, 6, 0:H], AF.Copy, bias=0.0,
                scale=1.0), (sA, 1))
            E("scalar", lambda: e.activation(
                g_["cbias"][:, 0:NT * H], g_["qn2"][:, 0:NT * H], AF.Sqrt,
                bias=g_["kc0"][:, 0:1], scale=1.0), (sA, 1))
            W("vector", sA)
            for jt in range(NT):
                E("vector", lambda jt=jt: e.tensor_mul(
                    g_["cbias"][:, jt * H:(jt + 1) * H],
                    g_["cbias"][:, jt * H:(jt + 1) * H], g_["kmax"][:, 0:H]),
                    (sV, 1))

            qtick = 0
            kvtick = {0: 0, 1: 0}
            avfree = {0: 0, 1: 0}
            hetick = {0: 0, 1: 0}
            hect = 0
            for g in range(NG):
                kb = kb_of_group(g)
                kbK = kb * KB
                W("sync", sT, qtick)
                for h in range(H):
                    E("sync", lambda g=g, h=h: e.dma_start(
                        out=bqh[:, h * 256:(h + 1) * 256],
                        in_=ap["qscr_h"][g * 256:(g + 1) * 256,
                                         h * 128:(h + 1) * 128],
                        transpose=True), (dQ, 16))
                    E("sync", lambda g=g, h=h: e.dma_start(
                        out=bql[:, h * 256:(h + 1) * 256],
                        in_=ap["qscr_l"][g * 256:(g + 1) * 256,
                                         h * 128:(h + 1) * 128],
                        transpose=True), (dQ, 16))
                for h in range(H):
                    sl = h % 2
                    if g * H + h >= 2:
                        W("sync", sT, kvtick[sl])
                    E("sync", lambda h=h, kbK=kbK, sl=sl: e.dma_start(
                        out=bkh[sl][:, 0:kbK],
                        in_=ap["kscr_h"][0:kbK, h * 128:(h + 1) * 128],
                        transpose=True), (dK[sl], 16))
                    E("sync", lambda h=h, kbK=kbK, sl=sl: e.dma_start(
                        out=bkl[sl][:, 0:kbK],
                        in_=ap["kscr_l"][0:kbK, h * 128:(h + 1) * 128],
                        transpose=True), (dK[sl], 16))
                    nvt = kbK // 128
                    for v0 in range(0, nvt, 8):
                        nv = min(8, nvt - v0)
                        E("sync", lambda h=h, v0=v0, nv=nv, sl=sl: e.dma_start(
                            out=bv[sl][:, v0 * 128:(v0 + nv) * 128].rearrange(
                                "p (n j) -> p n j", n=nv),
                            in_=ap["vscr"][v0 * 128:(v0 + nv) * 128,
            h * 128:(h + 1) * 128].rearrange(
                                "(n p) j -> p n j", p=128)), (dK[sl], 16))
                    avbank = 4 + sl
                    first_av = True
                    for kbi in range(kb):
                        for qt in range(2):
                            bnk = qt * 2 + (kbi % 2)
                            jt = 2 * g + qt
                            W("tensor", dQ)
                            W("tensor", dK[sl])
                            W("tensor", sA)
                            W("tensor", sV)
                            if first_av:
                                W("tensor", sV, avfree[sl])
                            qh_t = bqh[:, h * 256 + qt * 128: h * 256 + qt * 128 + 128]
                            ql_t = bql[:, h * 256 + qt * 128: h * 256 + qt * 128 + 128]
                            kh_t = bkh[sl][:, kbi * KB:(kbi + 1) * KB]
                            kl_t = bkl[sl][:, kbi * KB:(kbi + 1) * KB]
                            k = 0
                            for (sx, mv) in ((qh_t, kh_t), (qh_t, kl_t), (ql_t, kh_t)):
                                k += 1
                                E("tensor", lambda sx=sx, mv=mv, bnk=bnk, k=k:
                                  e.matmul(ps[:, bnk, 0:KB], sx, mv,
            start=(k == 1), stop=(k == 3)),
                                    *([(sT, 1)] if k == 3 else []))
                            if kbi == kb - 1:
                                W("vector", sT)
                                E("vector", lambda bnk=bnk, jt=jt: e.tensor_add(
                                    ps[:, bnk, 0:KB], ps[:, bnk, 0:KB],
                                    bmk[:, jt * KB:(jt + 1) * KB]), (sV, 1))
                            W("scalar", sT)
                            W("scalar", sV)
                            E("scalar", lambda bnk=bnk, qt=qt, jt=jt, kbi=kbi:
                              e.activation(
                                  bp[:, qt * KB:(qt + 1) * KB], ps[:, bnk, 0:KB],
                                  AF.Exp, bias=g_["cbias"][:, jt * H + h:jt * H + h + 1],
                                  scale=1.0,
                                  accum_out=blt[:, qt * (NG + 2) + kbi + 1:
            qt * (NG + 2) + kbi + 2]), (sA, 1))
                        tbank = 6
                        nkc = KB // 128
                        for kc in range(nkc):
                            if kc % 2 == 0:
                                W("tensor", sA)
                            for qt in range(2):
                                E("tensor", lambda kc=kc, qt=qt, tbank=tbank:
                                  e.transpose(
                                      ps[:, tbank, (kc % 2) * 256 + qt * 128:
            (kc % 2) * 256 + qt * 128 + 128],
                                      bp[:, qt * KB + kc * 128:
            qt * KB + (kc + 1) * 128],
                                      g_["id32"][:, :]),
                                    *([(sT, 1)] if qt == 1 else []))
                            if kc % 2 == 1 or kc == nkc - 1:
                                W("scalar", sT)
                                E("scalar", lambda kc=kc, tbank=tbank: e.activation(
                                    bpt[:, (kc - kc % 2) * 256:(kc + 1) * 256],
                                    ps[:, tbank, 0:(kc % 2 + 1) * 256],
                                    AF.Copy, bias=0.0, scale=1.0), (sA, 1))
                        W("tensor", sA)
                        for kc in range(nkc):
                            kt_glob = kbi * nkc + kc
                            vt = bv[sl][:, kt_glob * 128:(kt_glob + 1) * 128]
                            lastmm = (kbi == kb - 1 and kc == nkc - 1)
                            E("tensor", lambda vt=vt, kc=kc, avbank=avbank,
                              first_av=first_av, lastmm=lastmm: e.matmul(
                                  ps[:, avbank, 0:256], vt,
                                  bpt[:, kc * 256:(kc + 1) * 256],
                                  start=first_av, stop=lastmm),
                                *([(sT, 1)] if lastmm else []))
                            first_av = False
                    kvtick[sl] = sT.c
                    W("vector", sA)
                    for qt in range(2):
                        jt = 2 * g + qt
                        E("vector", lambda qt=qt, kb=kb: e.tensor_reduce(
                            blt[:, qt * (NG + 2):qt * (NG + 2) + 1],
                            blt[:, qt * (NG + 2) + 1:qt * (NG + 2) + 1 + kb],
                            axis=AX.X, op=Op.add), (sV, 1))
                        W("vector", sV)
                        E("vector", lambda qt=qt, jt=jt, h=h: e.reciprocal(
                            g_["linv"][:, jt * H + h:jt * H + h + 1],
                            blt[:, qt * (NG + 2):qt * (NG + 2) + 1]), (sV, 1))
                    W("tensor", sV)
                    W("tensor", sA)
                    for qt in range(2):
                        jt = 2 * g + qt
                        E("tensor", lambda jt=jt, h=h, qt=qt: e.transpose(
                            ps[0:1, 3, qt * 128:qt * 128 + 128],
                            g_["linv"][:, jt * H + h:jt * H + h + 1],
                            g_["id32"][:, :]), *([(sT, 1)] if qt == 1 else []))
                    W("scalar", sT)
                    E("scalar", lambda: e.activation(
                        blb[0:1, 0:256], ps[0:1, 3, 0:256], AF.Copy, bias=0.0,
                        scale=1.0), (sA, 1))
                    W("tensor", sA)
                    E("tensor", lambda: e.matmul(
                        ps[:, 3, 0:256], g_["ones1"][0:1, :], blb[0:1, 0:256],
                        start=True, stop=True), (sT, 1))
                    W("scalar", sT)
                    E("scalar", lambda: e.activation(
                        blb[:, 0:256], ps[:, 3, 0:256], AF.Copy, bias=0.0,
                        scale=1.0), (sA, 1))
                    hesl = hect % 2
                    hect += 1
                    W("vector", sA)
                    W("vector", sT)
                    W("vector", gH[hesl])
                    E("vector", lambda avbank=avbank, hesl=hesl: e.tensor_mul(
                        bhb[hesl][:, :], ps[:, avbank, 0:256], blb[:, 0:256]),
                        (sV, 1))
                    avfree[sl] = sV.c
                    W("vector", sV)
                    E("vector", lambda hesl=hesl: e.tensor_copy(
                        bhe[hesl][:, :], bhb[hesl][:, :]), (sV, 1))
                    W("vector", sV)
                    E("vector", lambda hesl=hesl: e.tensor_sub(
                        bhb[hesl][:, :], bhb[hesl][:, :], bhe[hesl][:, :]), (sV, 1))
                    W("gpsimd", sV)
                    E("gpsimd", lambda h=h, g=g, hesl=hesl: e.dma_start(
                        out=ap["hscr_h"][h][:, g * 256:(g + 1) * 256],
                        in_=bhe[hesl][:, :]), (gH[hesl], 16))
                    E("gpsimd", lambda h=h, g=g, hesl=hesl: e.dma_start(
                        out=ap["hscr_l"][h][:, g * 256:(g + 1) * 256],
                        in_=bhb[hesl][:, :]), (gH[hesl], 16))
                qtick = sT.c

    if 'B' in stages:
        run_stage(ctx, sB)

    def sC(ctx, e, eng):
        E, W, B = mk_EW(e, eng, ctx.all_sems, ctx.own_map[eng])
        B()
        W("tensor", dC)
        TCB = min(512, OT)
        NCB = OT // TCB
        TPC = TCB // 128
        if True:
            chh = ctx.balloc("c_hh", [128, H * TCB], BF16)
            chl = ctx.balloc("c_hl", [128, H * TCB], BF16)
            cwh = ctx.balloc2("c_wh", [128, 2, H * WBLK], BF16)
            cwl = ctx.balloc2("c_wl", [128, 2, H * WBLK], BF16)
            cx = ctx.balloc("c_x", [128, TPC * D], F32)
            co = ctx.balloc2("c_o", [128, 2, WBLK], F32)
            wtick = {0: 0, 1: 0}
            oct_ = 0
            W("sync", gH[0])
            W("sync", gH[1])
            for tcb in range(NCB):
                W("sync", sT)
                W("sync", sV)
                for h in range(H):
                    E("sync", lambda h=h, tcb=tcb: e.dma_start(
                        out=chh[:, h * TCB:(h + 1) * TCB],
                        in_=ap["hscr_h"][h][:, tcb * TCB:(tcb + 1) * TCB]),
                        (dX[0], 16))
                    E("sync", lambda h=h, tcb=tcb: e.dma_start(
                        out=chl[:, h * TCB:(h + 1) * TCB],
                        in_=ap["hscr_l"][h][:, tcb * TCB:(tcb + 1) * TCB]),
                        (dX[0], 16))
                for tc in range(TPC):
                    E("sync", lambda tcb=tcb, tc=tc: e.dma_start(
                        out=cx[:, tc * D:(tc + 1) * D],
                        in_=ap["xo"][(tcb * TPC + tc) * 128:
                                     (tcb * TPC + tc + 1) * 128, :]), (dX[0], 16))
                for dob in range(DBLK):
                    sl = (tcb * DBLK + dob) % 2
                    if tcb * DBLK + dob >= 2:
                        W("sync", sT, wtick[sl])
                    for part, buf in (("wot_h", cwh), ("wot_l", cwl)):
                        E("sync", lambda part=part, dob=dob, buf=buf, sl=sl:
                          e.dma_start(
                              out=buf[sl][:, :].rearrange("p (c j) -> p c j", c=H),
                              in_=ap[part][:, :, dob * WBLK:(dob + 1) * WBLK]
                              .rearrange("c p j -> p c j")), (dW[sl], 16))
                    for tc in range(TPC):
                        bank = tc % 2
                        osl = oct_ % 2
                        oct_ += 1
                        W("tensor", dW[sl])
                        W("tensor", dX[0])
                        W("tensor", sV)
                        k, n3 = 0, 3 * H
                        for dc in range(H):
                            hh_t = chh[:, dc * TCB + tc * 128: dc * TCB + tc * 128 + 128]
                            hl_t = chl[:, dc * TCB + tc * 128: dc * TCB + tc * 128 + 128]
                            wh_t = cwh[sl][:, dc * WBLK:(dc + 1) * WBLK]
                            wl_t = cwl[sl][:, dc * WBLK:(dc + 1) * WBLK]
                            for (sx, mv) in ((hh_t, wh_t), (hh_t, wl_t), (hl_t, wh_t)):
                                k += 1
                                E("tensor", lambda sx=sx, mv=mv, bank=bank, k=k,
                                  n3=n3: e.matmul(
                                      ps[:, bank, 0:WBLK], sx, mv, start=(k == 1),
                                      stop=(k == n3)),
                                    *([(sT, 1)] if k == n3 else []))
                        W("vector", sT)
                        W("vector", dX[0])
                        W("vector", gO[osl])
                        E("vector", lambda bank=bank, tc=tc, dob=dob, osl=osl:
                          e.tensor_add(
                              co[osl][:, :], ps[:, bank, 0:WBLK],
                              cx[:, tc * D + dob * WBLK: tc * D + (dob + 1) * WBLK]),
                            (sV, 1))
                        W("gpsimd", sV)
                        E("gpsimd", lambda tcb=tcb, tc=tc, dob=dob, osl=osl:
                          e.dma_start(
                              out=ap["oscr"][(tcb * TPC + tc) * 128:
            (tcb * TPC + tc + 1) * 128,
            dob * WBLK:(dob + 1) * WBLK],
                              in_=co[osl][:, :]), (gO[osl], 16))
                    wtick[sl] = sT.c

    if 'C' in stages:
        run_stage(ctx, sC)

    def sD(ctx, e, eng):
        E, W, B = mk_EW(e, eng, ctx.all_sems, ctx.own_map[eng])
        B()
        W("tensor", dC)
        OC = IC
        HQ = min(2048, max(I // 2, D))
        NQ2 = I // HQ
        LVL = [float(1 << k) for k in range(7, -1, -1)]
        CONST = []
        c = 0.0
        for t in LVL:
            CONST.append(-(t + c))
            c += t / 2.0
        CFIN = c
        if True:
            dxh = ctx.balloc("d_xh", [128, DC * FMT], BF16)
            dxl = ctx.balloc("d_xl", [128, DC * FMT], BF16)
            df1 = ctx.balloc("d_f1", [128, DC * FMT], F32)
            dsc2 = ctx.balloc("d_sc2", [128, HQ], F32)
            dtm = ctx.balloc("d_tm", [128, D], F32)
            dq1f = ctx.balloc2("d_q1f", [128, 2, max(DC * FMT, HQ)], BF16)
            dsc = ctx.balloc("d_dsc", [128, 24], F32)
            dldr = ctx.balloc("d_ldr", [128, 8], F32)
            dwg = ctx.balloc2("d_wg", [128, 2, DC * WBI], FP8)
            dwu = ctx.balloc2("d_wu", [128, 2, DC * WBI], FP8)
            dt1 = ctx.balloc("d_t1", [128, WBI], F32)
            dt2 = ctx.balloc("d_t2", [128, WBI], F32)
            dytm = ctx.balloc("d_ytm", [128, I], F32)
            dyfm = ctx.balloc("d_yfm", [128, IC * FMT], F32)
            dysh = ctx.balloc("d_ysh", [128, 2 * FMT], BF16)
            dysl = ctx.balloc("d_ysl", [128, 2 * FMT], BF16)
            dq2t = ctx.balloc("d_q2t", [128, I // 2], BF16)
            dq2f = ctx.balloc("d_q2f", [128, IC * FMT], BF16)
            dwd = ctx.balloc2("d_wd", [128, 2, (IC // 2) * WBLK], FP8)
            dos = ctx.balloc("d_os", [128, WBLK], F32)
            dou = ctx.balloc2("d_ou", [128, 1, WBLK], F32)

            for k in range(8):
                E("vector", lambda k=k: e.memset(dldr[:, k:k + 1], CONST[k]),
                  (sV, 1))

            def ladder_multi(specs, scvec):
                svt = {}
                sat = {}
                for ci, (zbuf, wid, bits, wcopy, outq) in enumerate(specs):
                    E("vector", lambda zbuf=zbuf, scvec=scvec: e.tensor_scalar(
                        zbuf, zbuf, scvec, -128.0, op0=Op.mult, op1=Op.max),
                        (sV, 1))
                    E("vector", lambda zbuf=zbuf: e.tensor_scalar(
                        zbuf, zbuf, 127.49, 128.5, op0=Op.min, op1=Op.add),
                        (sV, 1))
                    E("vector", lambda zbuf=zbuf, wcopy=wcopy: e.tensor_scalar(
                        wcopy, zbuf, 1.0, None, op0=Op.mult), (sV, 1))
                    svt[ci] = sV.c
                for k in range(8):
                    for ci, (zbuf, wid, bits, wcopy, outq) in enumerate(specs):
                        W("scalar", sV, svt[ci])
                        E("scalar", lambda zbuf=zbuf, bits=bits, k=k: e.activation(
                            bits, zbuf, AF.Sign, bias=dldr[:, k:k + 1],
                            scale=1.0), (sA, 1))
                        sat[ci] = sA.c
                        W("vector", sA, sat[ci])
                        E("vector", lambda zbuf=zbuf, bits=bits, k=k:
                          e.scalar_tensor_tensor(
                              zbuf, bits, -LVL[k] / 2.0, zbuf, op0=Op.mult,
                              op1=Op.add), (sV, 1))
                        svt[ci] = sV.c
                for ci, (zbuf, wid, bits, wcopy, outq) in enumerate(specs):
                    E("vector", lambda zbuf=zbuf, wcopy=wcopy: e.tensor_sub(
                        zbuf, wcopy, zbuf), (sV, 1))
                    E("vector", lambda zbuf=zbuf, outq=outq: e.tensor_scalar(
                        outq, zbuf, CFIN - 128.0, 127.0, op0=Op.add, op1=Op.min),
                        (sV, 1))

            xn_tick = 0
            df1_free = 0
            df1v_free = 0
            dq1f_ready = {0: 0, 1: 0}
            y_tick = {}
            bank_sV = {}

            def bankw(b):
                W("tensor", sA, y_tick.get(b, 0))
                W("tensor", sV, bank_sV.get(b, 0))
            gu_bank_tick = {}
            wg_tick = {0: 0, 1: 0}
            wd_tick = {0: 0, 1: 0}
            dq2f_tick = 0
            dos_tick = 0
            dou_tick = 0
            def EARLY(mi):
                nonlocal xn_tick, df1_free, df1v_free
                t0 = mi * FMT
                xsl = mi % 2
                db = xsl * 5
                W("sync", sT, xn_tick)
                for dc in range(DC):
                    E("sync", lambda dc=dc, t0=t0: e.dma_start(
                        out=dxh[:, dc * FMT:(dc + 1) * FMT],
                        in_=ap["xnscr_h"][dc][:, t0:t0 + FMT]), (dX[0], 16))
                    E("sync", lambda dc=dc, t0=t0: e.dma_start(
                        out=dxl[:, dc * FMT:(dc + 1) * FMT],
                        in_=ap["xnscr_l"][dc][:, t0:t0 + FMT]), (dX[0], 16))
                for dc in range(DC):
                    hb6 = dc % 2
                    W("tensor", dX[0])
                    W("tensor", sA, y_tick.get((6, 0), 0))
                    W("tensor", sA, y_tick.get((6, 1), 0))
                    W("tensor", sV, y_tick.get((6, "v"), 0))
                    E("tensor", lambda dc=dc, hb6=hb6: e.matmul(
                        ps[:, 6, hb6 * FMT:(hb6 + 1) * FMT], g_["hmat"][:, :],
                        dxh[:, dc * FMT:(dc + 1) * FMT], start=True, stop=False,
                        skip_group_check=True))
                    E("tensor", lambda dc=dc, hb6=hb6: e.matmul(
                        ps[:, 6, hb6 * FMT:(hb6 + 1) * FMT], g_["hmat"][:, :],
                        dxl[:, dc * FMT:(dc + 1) * FMT], start=False, stop=True,
                        skip_group_check=True), (sT, 1))
                    if dc == DC - 1:
                        xn_tick = sT.c
                    W("scalar", sT)
                    W("scalar", sT, df1_free)
                    W("scalar", sV, df1v_free)
                    E("scalar", lambda dc=dc, hb6=hb6: e.activation(
                        df1[:, dc * FMT:(dc + 1) * FMT],
                        ps[:, 6, hb6 * FMT:(hb6 + 1) * FMT],
                        AF.Copy, bias=0.0, scale=1.0), (sA, 1))
                    y_tick[(6, hb6)] = sA.c
                ap1 = df1[:, :].rearrange("p (c j) -> p c j", c=DC)
                np1 = DC // 2
                for lv in range(L1):
                    stp = 1 << lv
                    W("vector", sA if lv == 0 else sV)
                    E("vector", lambda stp=stp: e.tensor_copy(
                        dsc2[:, 0:np1 * FMT].rearrange("p (c j) -> p c j", c=np1),
                        _bsel(ap1, stp, 1)), (sV, 1))
                    E("vector", lambda stp=stp: e.tensor_sub(
                        _bsel(ap1, stp, 1), _bsel(ap1, stp, 0), _bsel(ap1, stp, 1)),
                        (sV, 1))
                    E("vector", lambda stp=stp: e.tensor_add(
                        _bsel(ap1, stp, 0), _bsel(ap1, stp, 0),
                        dsc2[:, 0:np1 * FMT].rearrange("p (c j) -> p c j", c=np1)),
                        (sV, 1))
                bf1_tick = sV.c
                for dc in range(DC):
                    q = dc % 4
                    if q == 0:
                        W("tensor", sV, bf1_tick)
                        W("tensor", sA, y_tick.get((6, 0), 0))
                        W("tensor", sA, y_tick.get((6, 1), 0))
                    E("tensor", lambda dc=dc, q=q: e.transpose(
                        ps[:, 6, q * 128:(q + 1) * 128],
                        df1[:, dc * FMT:(dc + 1) * FMT], g_["id32"][:, :]),
                        *([(sT, 1)] if (q == 3 or dc == DC - 1) else []))
                    if q == 3 or dc == DC - 1:
                        W("scalar", sT)
                        E("scalar", lambda dc=dc, q=q: e.activation(
                            dtm[:, (dc - q) * 128:(dc + 1) * 128],
                            ps[:, 6, 0:(q + 1) * 128], AF.Copy, bias=0.0,
                            scale=1.0), (sA, 1))
                        y_tick[(6, 0)] = sA.c
                        y_tick[(6, 1)] = sA.c
                W("vector", sA)
                E("vector", lambda db=db: e.tensor_reduce(
                    dsc[:, db + 0:db + 1], dtm[:, :], axis=AX.X, op=Op.max,
                    apply_absolute_value=True), (sV, 1))
                E("vector", lambda db=db: e.tensor_scalar(
                    dsc[:, db + 1:db + 2], dsc[:, db + 0:db + 1], C1, EPS, op0=Op.mult, op1=Op.max),
                    (sV, 1))
                E("vector", lambda db=db: e.tensor_scalar(
                    dsc[:, db + 1:db + 2], dsc[:, db + 1:db + 2], 1.0 / 127.0, None, op0=Op.mult),
                    (sV, 1))
                E("vector", lambda db=db: e.reciprocal(dsc[:, db + 2:db + 3], dsc[:, db + 1:db + 2]), (sV, 1))
                E("vector", lambda db=db: e.tensor_scalar(
                    dsc[:, db + 2:db + 3], dsc[:, db + 2:db + 3], C1, None, op0=Op.mult), (sV, 1))
                E("vector", lambda db=db: e.tensor_scalar(
                    dsc[:, db + 3:db + 4], dsc[:, db + 1:db + 2], g_["wsg"][:, 0:1], None, op0=Op.mult),
                    (sV, 1))
                E("vector", lambda db=db: e.tensor_scalar(
                    dsc[:, db + 4:db + 5], dsc[:, db + 1:db + 2], g_["wsu"][:, 0:1], None, op0=Op.mult),
                    (sV, 1))
                ladder_multi([(dtm[:, :], D, dq2t[:, 0:D], dsc2[:, 0:D],
                               df1[:, 0:D])], dsc[:, db + 2:db + 3])
                q1l_tick = sV.c
                for dc in range(DC):
                    q = dc % 4
                    if q == 0:
                        W("tensor", sV, q1l_tick)
                        W("tensor", sV, y_tick.get((6, "v"), 0))
                        W("tensor", sA, y_tick.get((6, 0), 0))
                        W("tensor", sA, y_tick.get((6, 1), 0))
                    E("tensor", lambda dc=dc, q=q: e.transpose(
                        ps[:, 6, q * 128:(q + 1) * 128],
                        df1[:, dc * 128:(dc + 1) * 128], g_["id32"][:, :]),
                        *([(sT, 1)] if (q == 3 or dc == DC - 1) else []))
                    if q == 3 or dc == DC - 1:
                        W("vector", sT)
                        E("vector", lambda dc=dc, q=q, xsl=xsl: e.tensor_copy(
                            dq1f[xsl][:, (dc - q) * FMT:(dc + 1) * FMT],
                            ps[:, 6, 0:(q + 1) * 128]), (sV, 1))
                        y_tick[(6, "v")] = sV.c
                dq1f_ready[xsl] = sV.c
                df1_free = sT.c
            def LATE(mi):
                nonlocal df1v_free, dq2f_tick, dos_tick
                t0 = mi * FMT
                xsl = mi % 2
                db = xsl * 5
                for ob in range(OBI):
                    wsl = (mi * OBI + ob) % 2
                    if mi * OBI + ob >= 2:
                        W("sync", sT, wg_tick[wsl])
                    E("sync", lambda ob=ob, wsl=wsl: e.dma_start(
                        out=dwg[wsl][:, :].rearrange("p (c j) -> p c j", c=DC),
                        in_=ap["wscr_g"][:, :, ob * WBI:(ob + 1) * WBI]
                        .rearrange("c p j -> p c j")), (dW[wsl], 16))
                    E("sync", lambda ob=ob, wsl=wsl: e.dma_start(
                        out=dwu[wsl][:, :].rearrange("p (c j) -> p c j", c=DC),
                        in_=ap["wscr_u"][:, :, ob * WBI:(ob + 1) * WBI]
                        .rearrange("c p j -> p c j")), (dW[wsl], 16))
                    gbank, ubank = 2, 3
                    for (wbuf, bank) in ((dwg, gbank), (dwu, ubank)):
                        W("tensor", dW[wsl])
                        W("tensor", sV, dq1f_ready[xsl])
                        bankw(bank)
                        k = 0
                        for dc in range(DC):
                            k += 1
                            E("tensor", lambda wbuf=wbuf, dc=dc, bank=bank, k=k,
                              wsl=wsl, xsl=xsl: e.matmul(
                                  ps[:, bank, 0:WBI],
                                  dq1f[xsl][:, dc * FMT:(dc + 1) * FMT],
                                  wbuf[wsl][:, dc * WBI:(dc + 1) * WBI],
                                  start=(k == 1), stop=(k == DC)),
                                *([(sT, 1)] if k == DC else []))
                    wg_tick[wsl] = sT.c
                    W("scalar", sT)
                    W("scalar", sV)
                    E("scalar", lambda gbank=gbank, db=db: e.activation(
                        dt1[:, :], ps[:, gbank, 0:WBI], AF.Silu,
                        bias=g_["kc0"][:, 0:1], scale=dsc[:, db + 3:db + 4]), (sA, 1))
                    E("scalar", lambda ubank=ubank, db=db: e.activation(
                        dt2[:, :], ps[:, ubank, 0:WBI], AF.Identity,
                        bias=g_["kc0"][:, 0:1], scale=dsc[:, db + 4:db + 5]), (sA, 1))
                    y_tick[gbank] = sA.c
                    y_tick[ubank] = sA.c
                    W("vector", sA)
                    if ob == 0:
                        W("vector", sT, y_tick.get("ytm", 0))
                    E("vector", lambda ob=ob: e.tensor_mul(
                        dytm[:, ob * WBI:(ob + 1) * WBI], dt1[:, :], dt2[:, :]),
                        (sV, 1))
                for ic in range(IC):
                    bank = 4 + ((ic // 4) % 2)
                    q = ic % 4
                    if q == 0:
                        W("tensor", sV)
                        bankw(bank)
                    E("tensor", lambda ic=ic, bank=bank, q=q: e.transpose(
                        ps[:, bank, q * 128:(q + 1) * 128],
                        dytm[:, ic * 128:(ic + 1) * 128], g_["id32"][:, :]),
                        *([(sT, 1)] if (q == 3 or ic == IC - 1) else []))
                    if q == 3 or ic == IC - 1:
                        if ic == IC - 1:
                            y_tick["ytm"] = sT.c
                        W("scalar", sT)
                        E("scalar", lambda ic=ic, bank=bank, q=q: e.activation(
                            dyfm[:, (ic - q) * FMT:(ic + 1) * FMT],
                            ps[:, bank, 0:(q + 1) * 128], AF.Copy, bias=0.0,
                            scale=1.0), (sA, 1))
                        y_tick[bank] = sA.c
                for c0 in range(0, IC, 2):
                    ng = min(2, IC - c0)
                    W("vector", sA)
                    E("vector", lambda c0=c0, ng=ng: e.tensor_copy(
                        dysh[:, 0:ng * FMT], dyfm[:, c0 * FMT:(c0 + ng) * FMT]),
                        (sV, 1))
                    E("vector", lambda c0=c0, ng=ng: e.tensor_sub(
                        dysl[:, 0:ng * FMT], dyfm[:, c0 * FMT:(c0 + ng) * FMT],
                        dysh[:, 0:ng * FMT]), (sV, 1))
                    for ci in range(ng):
                        ic = c0 + ci
                        bank = ic % 2
                        W("tensor", sV)
                        bankw(bank)
                        E("tensor", lambda ci=ci, bank=bank: e.matmul(
                            ps[:, bank, 0:FMT], g_["hmat"][:, :],
                            dysh[:, ci * FMT:(ci + 1) * FMT], start=True,
                            stop=False))
                        E("tensor", lambda ci=ci, bank=bank: e.matmul(
                            ps[:, bank, 0:FMT], g_["hmat"][:, :],
                            dysl[:, ci * FMT:(ci + 1) * FMT], start=False,
                            stop=True), (sT, 1))
                        W("scalar", sT)
                        E("scalar", lambda ic=ic, bank=bank: e.activation(
                            dyfm[:, ic * FMT:(ic + 1) * FMT], ps[:, bank, 0:FMT],
                            AF.Copy, bias=0.0, scale=1.0), (sA, 1))
                        y_tick[bank] = sA.c
                ap2 = dyfm[:, :].rearrange("p (c j) -> p c j", c=IC)
                np2 = IC // 2
                nh = (np2 * FMT + HQ - 1) // HQ
                per = np2 // nh
                for lv in range(L2):
                    stp = 1 << lv
                    ng2 = IC // (2 * stp)
                    if ng2 >= nh:
                        tg, tc_ = ng2 // nh, stp
                    else:
                        tg, tc_ = ng2, stp // nh

                    def tview():
                        return dsc2[:, 0:tg * tc_ * FMT].rearrange(
                            "p (g c j) -> p g c j", g=tg, c=tc_)
                    W("vector", sA if lv == 0 else sV)
                    for hb in range(nh):
                        E("vector", lambda stp=stp, hb=hb, tview=tview:
                          e.tensor_copy(tview(), _bsel2(ap2, stp, 1, hb, nh)),
                            (sV, 1))
                        E("vector", lambda stp=stp, hb=hb: e.tensor_sub(
                            _bsel2(ap2, stp, 1, hb, nh),
                            _bsel2(ap2, stp, 0, hb, nh),
                            _bsel2(ap2, stp, 1, hb, nh)), (sV, 1))
                        E("vector", lambda stp=stp, hb=hb, tview=tview:
                          e.tensor_add(_bsel2(ap2, stp, 0, hb, nh),
                                       _bsel2(ap2, stp, 0, hb, nh), tview()),
                            (sV, 1))
                for ic in range(IC):
                    bank = 4 + ((ic // 4) % 2)
                    q = ic % 4
                    if q == 0:
                        W("tensor", sV)
                        bankw(bank)
                    E("tensor", lambda ic=ic, bank=bank, q=q: e.transpose(
                        ps[:, bank, q * 128:(q + 1) * 128],
                        dyfm[:, ic * FMT:(ic + 1) * FMT], g_["id32"][:, :]),
                        *([(sT, 1)] if (q == 3 or ic == IC - 1) else []))
                    if q == 3 or ic == IC - 1:
                        W("scalar", sT)
                        W("scalar", sV)
                        E("scalar", lambda ic=ic, bank=bank, q=q: e.activation(
                            dytm[:, (ic - q) * 128:(ic + 1) * 128],
                            ps[:, bank, 0:(q + 1) * 128], AF.Copy, bias=0.0,
                            scale=1.0), (sA, 1))
                        y_tick[bank] = sA.c
                W("vector", sA)
                E("vector", lambda: e.tensor_reduce(
                    dsc[:, 12:13], dytm[:, :], axis=AX.X, op=Op.max,
                    apply_absolute_value=True), (sV, 1))
                E("vector", lambda: e.tensor_scalar(
                    dsc[:, 13:14], dsc[:, 12:13], C2, EPS, op0=Op.mult, op1=Op.max),
                    (sV, 1))
                E("vector", lambda: e.tensor_scalar(
                    dsc[:, 13:14], dsc[:, 13:14], 1.0 / 127.0, None, op0=Op.mult),
                    (sV, 1))
                E("vector", lambda: e.reciprocal(dsc[:, 12:13], dsc[:, 13:14]),
                  (sV, 1))
                E("vector", lambda: e.tensor_scalar(
                    dsc[:, 12:13], dsc[:, 12:13], C2, None, op0=Op.mult), (sV, 1))
                E("vector", lambda: e.tensor_scalar(
                    dsc[:, 13:14], dsc[:, 13:14], g_["wsd"][:, 0:1], None,
                    op0=Op.mult), (sV, 1))
                for hf in range(2):
                    nq = max(I // 2 // HQ, 1)
                    specs = []
                    for qd in range(nq):
                        o0 = hf * (I // 2) + qd * HQ
                        bslot = dq2t[:, ((qd + 1) % max(nq, 2)) * HQ:
                                     ((qd + 1) % max(nq, 2) + 1) * HQ] \
                            if nq > 1 else dq2t[:, 0:HQ]
                        specs.append((dytm[:, o0:o0 + HQ], HQ, bslot,
                                      dsc2[:, 0:HQ] if qd % 2 == 0
                                      else dyfm[:, 0:HQ],
                                      dq2t[:, qd * HQ:(qd + 1) * HQ]))
                    for p0 in range(0, nq, 2):
                        ladder_multi(specs[p0:p0 + 2], dsc[:, 12:13])
                    for ici in range(IC // 2):
                        ic = hf * (IC // 2) + ici
                        half = (ici // 4) % 2
                        q = ici % 4
                        if q == 0:
                            W("tensor", sV)
                            W("tensor", sA, gu_bank_tick.get(6, 0))
                        W("tensor", sA, gu_bank_tick.get(7, 0))
                        E("tensor", lambda ici=ici, half=half, q=q: e.transpose(
                            psb[:, half * 512 + q * 128: half * 512 + (q + 1) * 128],
                            dq2t[:, ici * 128:(ici + 1) * 128], g_["idbf"][:, :]),
                            *([(sT, 1)] if (q == 3 or ici == IC // 2 - 1) else []))
                        if q == 3 or ici == IC // 2 - 1:
                            W("scalar", sT)
                            W("scalar", sT, dq2f_tick)
                            for qq in range(q + 1):
                                icq = ic - q + qq
                                E("scalar", lambda icq=icq, half=half, qq=qq:
                                  e.activation(
                                      dq2f[:, icq * FMT:(icq + 1) * FMT],
                                      psb[:, half * 512 + qq * 128:
                                          half * 512 + (qq + 1) * 128], AF.Copy,
                                      bias=0.0, scale=1.0),
                                    *([(sA, 1)] if qq == q else []))
                            gu_bank_tick[6 + half] = sA.c
                for dob in range(DBLK):
                    W("sync", sV, dos_tick)
                    E("sync", lambda mi=mi, dob=dob: e.dma_start(
                        out=dos[:, :],
                        in_=ap["oscr"][mi * 128:(mi + 1) * 128,
                                       dob * WBLK:(dob + 1) * WBLK]), (dX[1], 16))
                    bank = dob % 2
                    for hw_ in range(2):
                        W("sync", sT, wd_tick[hw_])
                        E("sync", lambda dob=dob, hw_=hw_: e.dma_start(
                            out=dwd[hw_][:, :].rearrange(
                                "p (c j) -> p c j", c=IC // 2),
                            in_=ap["wscr_d"][hw_ * (IC // 2):(hw_ + 1) * (IC // 2),
                                             :, dob * WBLK:(dob + 1) * WBLK]
                            .rearrange("c p j -> p c j")), (dK[hw_], 16))
                    for hw_ in range(2):
                        W("tensor", dK[hw_])
                        if hw_ == 0:
                            W("tensor", sA)
                            bankw(bank)
                        k = 0
                        for ici in range(IC // 2):
                            ic = hw_ * (IC // 2) + ici
                            k += 1
                            E("tensor", lambda ic=ic, ici=ici, bank=bank, k=k,
                              hw_=hw_, xsl=xsl: e.matmul(
                                  ps[:, bank, 0:WBLK],
                                  dq2f[:, ic * FMT:(ic + 1) * FMT],
                                  dwd[hw_][:, ici * WBLK:(ici + 1) * WBLK],
                                  start=(hw_ == 0 and k == 1),
                                  stop=(hw_ == 1 and k == IC // 2)),
                                *([(sT, 1)] if (hw_ == 1 and k == IC // 2)
                                  else []))
                        wd_tick[hw_] = sT.c
                    if dob == DBLK - 1:
                        dq2f_tick = sT.c
                    W("vector", sT)
                    W("vector", dX[1])
                    W("vector", gU[0])
                    E("vector", lambda bank=bank: e.scalar_tensor_tensor(
                        dou[0][:, :], ps[:, bank, 0:WBLK], dsc[:, 13:14],
                        dos[:, :], op0=Op.mult, op1=Op.add), (sV, 1))
                    dos_tick = sV.c
                    bank_sV[bank] = sV.c
                    W("gpsimd", sV)
                    E("gpsimd", lambda mi=mi, dob=dob: e.dma_start(
                        out=ap["out"][mi * 128:(mi + 1) * 128,
                                      dob * WBLK:(dob + 1) * WBLK],
                        in_=dou[0][:, :]), (gU[0], 16))

            for step in range(NM + 1):
                if step < NM:
                    EARLY(step)
                if step > 0:
                    LATE(step - 1)
            W("gpsimd", gU[0])

    if 'D' in stages:
        run_stage(ctx, sD)

    stk.close()
    return nc



_NC_CACHE = {}


def kernel(**inputs):
    cfg = default_cfg()
    cd = derived(cfg)
    if "nc" not in _NC_CACHE:
        _NC_CACHE["nc"] = build_nc(cfg)
    nc = _NC_CACHE["nc"]
    in_maps = [host_core_inputs(cfg, inputs, c) for c in range(cfg["n_cores"])]
    res = run_bass_kernel_spmd(nc, in_maps, core_ids=list(range(cfg["n_cores"])))
    B, T, D = cfg["B"], cfg["T"], cfg["D"]
    out = np.empty((B, T, D), dtype=np.float32)
    for c in range(cfg["n_cores"]):
        b, p = c // cd["CPB"], c % cd["CPB"]
        o = np.asarray(res.results[c]["out"])
        for jt in range(cd["NT"]):
            out[b, (2 * jt + p) * 128:(2 * jt + p + 1) * 128, :] = \
                o[jt * 128:(jt + 1) * 128, :]
    return out


# version 12
# speedup vs baseline: 1.4732x; 1.1578x over previous
import contextlib
import numpy as np
import ml_dtypes
import concourse.bass as bass
import concourse.mybir as mybir
from concourse.alu_op_type import AluOpType as Op
from concourse.bass_utils import run_bass_kernel_spmd
from concourse import library_config

F32 = mybir.dt.float32
BF16 = mybir.dt.bfloat16
FP8 = mybir.dt.float8e4
AF = mybir.ActivationFunctionType
AX = mybir.AxisListType

NEG = np.float32(-3.0e38)
EPS = 1e-5


def default_cfg():
    return dict(B=4, T=4096, D=2048, H=16, I=8192, n_cores=8)


def derived(cfg):
    d = dict(cfg)
    T, D, I = cfg["T"], cfg["D"], cfg["I"]
    d["DC"] = D // 128
    d["IC"] = I // 128
    d["OT"] = T // 2
    d["NT"] = d["OT"] // 128
    d["TT"] = T // 128
    d["NG"] = max(d["NT"] // 2, 1)
    d["KB"] = min(512, T)
    d["WBLK"] = min(512, D)
    d["DBLK"] = D // d["WBLK"]
    d["WBI"] = min(512, I)
    d["OBI"] = I // d["WBI"]
    def _div(cap, n):
        a = min(cap, n)
        while n % a:
            a -= 128
        return a
    d["AIT"] = _div(1024, T)
    d["AIT2"] = _div(1024, d["OT"])
    d["FMT"] = 128
    d["NM"] = d["OT"] // d["FMT"]
    d["CPB"] = cfg["n_cores"] // cfg["B"]
    return d


def kb_of_group(g):
    return g + 1



def _hadamard(n):
    h = np.array([[1.0]], dtype=np.float64)
    while h.shape[0] < n:
        h = np.block([[h, h], [h, -h]])
    return h


def _rope_tables(positions, scale):
    DH = 128
    j = np.arange(DH // 2, dtype=np.float64)
    theta = 10000.0 ** (-2.0 * j / DH)
    ang = positions.astype(np.float64)[:, None] * theta[None, :]
    out = []
    for t in (np.cos(ang) * scale, np.sin(ang) * scale):
        hi = t.astype(np.float32)
        lo = (t - hi).astype(np.float32)
        out += [hi, lo]
    return out


def _tile_major(a):
    T = a.shape[0]
    return np.ascontiguousarray(
        a.reshape(T // 128, 128, 64).transpose(1, 0, 2).reshape(128, -1))


def _deint_perm(D):
    DH = 128
    perm = []
    for h0 in range(0, D, DH):
        perm += [h0 + 2 * j for j in range(DH // 2)]
        perm += [h0 + 2 * j + 1 for j in range(DH // 2)]
    return np.array(perm)


def host_core_inputs(cfg, inputs, core, names=None):
    cd = derived(cfg)
    T, D = cfg["T"], cfg["D"]
    b, p = core // cd["CPB"], core % cd["CPB"]
    nm = names or {}
    X = np.asarray(inputs[nm.get("X", "L0_attn_X")])
    Wq = np.asarray(inputs[nm.get("Wq", "L0_attn_Wq")])
    Wk = np.asarray(inputs[nm.get("Wk", "L0_attn_Wk")])
    Wv = np.asarray(inputs[nm.get("Wv", "L0_attn_Wv")])
    Wo = np.asarray(inputs[nm.get("Wo", "L0_attn_Wo")])
    Wg = np.asarray(inputs[nm.get("Wg", "L0_ffn_w_gate")])
    Wu = np.asarray(inputs[nm.get("Wu", "L0_ffn_w_up")])
    Wd = np.asarray(inputs[nm.get("Wd", "L0_ffn_w_down")])
    own_rows = np.concatenate(
        [np.arange((2 * jt + p) * 128, (2 * jt + p + 1) * 128)
         for jt in range(cd["NT"])])
    perm = _deint_perm(D)

    m = {}
    m["xf"] = np.ascontiguousarray(X[b], dtype=np.float32)
    m["xo"] = np.ascontiguousarray(X[b][own_rows], dtype=np.float32)
    m["wqT"] = np.ascontiguousarray(Wq[perm].T, dtype=np.float32)
    m["wkT"] = np.ascontiguousarray(Wk[perm].T, dtype=np.float32)
    m["wvT"] = np.ascontiguousarray(Wv.T, dtype=np.float32)
    m["woT"] = np.ascontiguousarray(Wo.T, dtype=np.float32)
    m["wgT"] = np.ascontiguousarray(Wg.T, dtype=np.float32)
    m["wuT"] = np.ascontiguousarray(Wu.T, dtype=np.float32)
    m["wdT"] = np.ascontiguousarray(Wd.T, dtype=np.float32)

    kt = _rope_tables(np.arange(T), 1.0)
    m["tabk"] = np.stack([_tile_major(x) for x in kt], 0)
    qt = _rope_tables(own_rows, 128.0 ** -0.5)
    m["tabq"] = np.stack([_tile_major(x) for x in qt], 0)

    masks = np.zeros((cd["NT"], 128, cd["KB"]), dtype=np.float32)
    for jt in range(cd["NT"]):
        gq, g = 2 * jt + p, jt // 2
        k0 = (kb_of_group(g) - 1) * cd["KB"]
        i = np.arange(128)[:, None]
        jj = np.arange(cd["KB"])[None, :]
        masks[jt] = np.where(k0 + jj > gq * 128 + i, NEG, 0.0)
    m["masks"] = masks.astype(ml_dtypes.bfloat16)

    m["hmat"] = _hadamard(128).astype(ml_dtypes.bfloat16)
    m["id32"] = np.eye(128, dtype=np.float32)
    m["idbf"] = np.eye(128).astype(ml_dtypes.bfloat16)
    return m


def host_input_specs(cfg):
    cd = derived(cfg)
    T, D, I = cfg["T"], cfg["D"], cfg["I"]
    OT, NT, TT, KB = cd["OT"], cd["NT"], cd["TT"], cd["KB"]
    return [
        ("xf", [T, D], F32), ("xo", [OT, D], F32),
        ("wqT", [D, D], F32), ("wkT", [D, D], F32),
        ("wvT", [D, D], F32), ("woT", [D, D], F32),
        ("wgT", [D, I], F32), ("wuT", [D, I], F32), ("wdT", [I, D], F32),
        ("tabk", [4, 128, TT * 64], F32), ("tabq", [4, 128, NT * 64], F32),
        ("masks", [NT, 128, KB], BF16),
        ("hmat", [128, 128], BF16), ("id32", [128, 128], F32),
        ("idbf", [128, 128], BF16),
    ]



class Sm:
    __slots__ = ("h", "c")

    def __init__(self, h):
        self.h = h
        self.c = 0


ENGS = ("sync", "tensor", "vector", "scalar", "gpsimd")


class Ctx:
    def balloc(self, name, shape, dt):
        if name not in self.stage_cache:
            self.stage_cache[name] = self.stage_stack.enter_context(
                self.nc.sbuf_tensor("s%d_%s" % (self.stage_id, name), shape, dt))
        return self.stage_cache[name]

    def balloc2(self, name, shape, dt):
        if name not in self.stage_cache:
            self.stage_cache[name] = DBuf(self.stage_stack.enter_context(
                self.nc.sbuf_tensor("s%d_%s" % (self.stage_id, name), shape, dt)))
        return self.stage_cache[name]


def run_stage(ctx, fn):
    snap = [(s, s.c) for s in ctx.all_sems]
    ctx.stage_cache = {}
    ctx.stage_id = getattr(ctx, "stage_id", -1) + 1
    ctx.stage_stack = contextlib.ExitStack()
    finals = None
    with ctx.nc.Block() as blk:
        for eng in ENGS:
            for s, v in snap:
                s.c = v
            getattr(blk, eng)(_section(fn, ctx, eng))
            cur = [s.c for s, _ in snap]
            if finals is None:
                finals = cur
            elif finals != cur:
                raise RuntimeError(
                    f"stage {ctx.stage_id} counter drift on pass {eng}: "
                    f"{[(i, a, b) for i, (a, b) in enumerate(zip(finals, cur)) if a != b]}")
    ctx.stage_stack.close()
    ctx.stage_cache = None


def _section(fn, ctx, eng):
    def sec(e):
        fn(ctx, e, eng)
    return sec


def mk_EW(e, eng, sems_all=None, own=None):
    last = [0]

    def E(tgt, make, *incs):
        if eng == tgt:
            if own is not None and own.c > last[0]:
                e.wait_ge(own.h, own.c)
                last[0] = own.c
            r = make()
            for (s, n) in incs:
                r = r.then_inc(s.h, n)
        for (s, n) in incs:
            s.c += n

    def W(tgt, s, v=None):
        if eng == tgt:
            e.wait_ge(s.h, s.c if v is None else v)

    def B():
        for s in (sems_all or ()):
            if s.c:
                e.wait_ge(s.h, s.c)
    return E, W, B



class DBuf:

    def __init__(self, t):
        self.t = t

    def __getitem__(self, sl):
        return _Slot(self.t, sl)


class _Slot:
    def __init__(self, t, sl):
        self.t = t
        self.sl = sl

    def __getitem__(self, idx):
        p, f = idx
        return self.t[p, self.sl, f]


def _bsel2(apat, stp, which, hb, nh):
    nch = apat.shape[1]
    ng = nch // (2 * stp)
    r = apat.rearrange("p (g t c) j -> p g t c j", g=ng, t=2, c=stp)
    sel = r[:, :, which, :, :]
    if ng >= nh:
        gp = ng // nh
        return sel[:, hb * gp:(hb + 1) * gp, :, :]
    cp = stp // nh
    return sel[:, :, hb * cp:(hb + 1) * cp, :]


def _bsel(apat, stp, which):
    nch = apat.shape[1]
    ng = nch // (2 * stp)
    r = apat.rearrange("p (g t c) j -> p g t c j", g=ng, t=2, c=stp)
    return r[:, :, which, :, :]



def build_nc(cfg, stages=('0', 'A1', 'A2', 'B', 'C', 'D')):
    cd = derived(cfg)
    T, D, H, I = cfg["T"], cfg["D"], cfg["H"], cfg["I"]
    DC, IC, OT, NT, TT, NG, KB = (cd["DC"], cd["IC"], cd["OT"], cd["NT"],
                                  cd["TT"], cd["NG"], cd["KB"])
    WBLK, DBLK, WBI, OBI = cd["WBLK"], cd["DBLK"], cd["WBI"], cd["OBI"]
    AIT, AIT2, FMT, NM = cd["AIT"], cd["AIT2"], cd["FMT"], cd["NM"]
    HPB = WBLK // 128
    C1 = float(1.0 / np.sqrt(D))
    C2 = float(1.0 / np.sqrt(I))
    L1 = int(np.log2(DC))
    L2 = int(np.log2(IC))

    nc = bass.Bass("TRN2", target_bir_lowering=False)
    ctx = Ctx()
    ctx.nc = nc
    stk = contextlib.ExitStack()

    ap = {}
    for name, shape, dt in host_input_specs(cfg):
        ap[name] = nc.dram_tensor(name, shape, dt, kind="ExternalInput").ap()
    ap["out"] = nc.dram_tensor("out", [OT, D], F32, kind="ExternalOutput").ap()
    for name, shape, dt in [
            ("kscr_h", [T, D], BF16), ("kscr_l", [T, D], BF16),
            ("vscr", [T, D], F32),
            ("qscr_h", [OT, D], BF16), ("qscr_l", [OT, D], BF16),
            ("xnscr_h", [DC, 128, OT], BF16), ("xnscr_l", [DC, 128, OT], BF16),
            ("wt_h", [3, DC, 128, D], BF16), ("wt_l", [3, DC, 128, D], BF16),
            ("wot_h", [DC, 128, D], BF16), ("wot_l", [DC, 128, D], BF16),
            ("wscr_g", [DC, 128, I], FP8), ("wscr_u", [DC, 128, I], FP8),
            ("wscr_d", [IC, 128, D], FP8),
            ("hscr_h", [H, 128, OT], BF16), ("hscr_l", [H, 128, OT], BF16),
            ("oscr", [OT, D], F32)]:
        ap[name] = nc.dram_tensor(name, shape, dt, kind="Internal").ap()

    sems = {}
    for name in ("sT", "sV", "sA", "sP",
                 "dC", "dT", "dT2", "dW0", "dW1", "dX0", "dX1",
                 "dK0", "dK1", "dQ", "dD",
                 "gW0", "gW1", "gK0", "gK1", "gV0", "gV1", "gX",
                 "gH0", "gH1", "gO0", "gO1", "gU0", "gU1"):
        sems[name] = Sm(stk.enter_context(nc.semaphore(name)))
    ctx.all_sems = list(sems.values())
    ctx.own_map = {"tensor": sems["sT"], "vector": sems["sV"],
                   "scalar": sems["sA"], "gpsimd": sems["sP"], "sync": None}
    sT, sV, sA, sP = (sems[k] for k in ("sT", "sV", "sA", "sP"))
    dC, dT, dT2, dQ, dD, gX = (sems[k] for k in ("dC", "dT", "dT2", "dQ", "dD", "gX"))
    dW = [sems["dW0"], sems["dW1"]]
    dX = [sems["dX0"], sems["dX1"]]
    dK = [sems["dK0"], sems["dK1"]]
    gW = [sems["gW0"], sems["gW1"]]
    gK = [sems["gK0"], sems["gK1"]]
    gV = [sems["gV0"], sems["gV1"]]
    gH = [sems["gH0"], sems["gH1"]]
    gO = [sems["gO0"], sems["gO1"]]
    gU = [sems["gU0"], sems["gU1"]]

    g_ = {}
    for name, shape, dt in [
            ("hmat", [128, 128], BF16), ("id32", [128, 128], F32),
            ("idbf", [128, 128], BF16), ("cbias", [128, NT * H], F32),
            ("linv", [128, NT * H], F32), ("qn2", [128, NT * H], F32),
            ("kn2", [128, H * TT], F32), ("kmax", [128, H], F32),
            ("wsg", [128, 1], F32), ("wsu", [128, 1], F32),
            ("wsd", [128, 1], F32), ("kc0", [128, 2], F32), ("ones1", [1, 128], F32)]:
        g_[name] = stk.enter_context(nc.sbuf_tensor("g_" + name, shape, dt))

    ps = stk.enter_context(nc.psum_tensor("ps", [128, 7, 512], F32))
    psb = stk.enter_context(nc.psum_tensor("psb", [128, 1024], BF16))

    def s0(ctx, e, eng):
        E, W, B = mk_EW(e, eng, ctx.all_sems, ctx.own_map[eng])
        B()
        for n in ("hmat", "id32", "idbf"):
            E("sync", lambda n=n: e.dma_start(out=g_[n][:, :], in_=ap[n][:, :]),
              (dC, 16))
        E("vector", lambda: e.memset(g_["ones1"][0:1, :], 1.0), (sV, 1))
        E("vector", lambda: e.memset(g_["kc0"][:, 0:1], 0.0), (sV, 1))
        E("vector", lambda: e.memset(g_["kc0"][:, 1:2], float(EPS)), (sV, 1))
        if True:
            win = ctx.balloc2("w0_in", [128, 2, max(I, D)], F32)
            whi = ctx.balloc2("w0_hi", [128, 2, D], BF16)
            wlo = ctx.balloc2("w0_lo", [128, 2, D], BF16)
            wt8 = ctx.balloc2("w0_t8", [128, 2, max(I, D)], FP8)
            wle = ctx.balloc("w0_le", [128, max(I, D)], F32)
            wms = ctx.balloc("w0_ms", [128, max(DC, IC)], F32)
            wsc = ctx.balloc("w0_sc", [128, 8], F32)
            whn = ctx.balloc("w0_hn", [128, 2], F32)
            srcs = [("wqT", "wt_h", "wt_l", 0), ("wkT", "wt_h", "wt_l", 1),
                    ("wvT", "wt_h", "wt_l", 2), ("woT", "wot_h", "wot_l", None)]
            winck = {0: 0, 1: 0}
            for wi, (src, dh_, dl_, idx) in enumerate(srcs):
                for rc in range(DC):
                    it = wi * DC + rc
                    sl = it % 2
                    W("sync", sV, winck[sl])
                    E("sync", lambda src=src, rc=rc, sl=sl: e.dma_start(
                        out=win[sl][:, 0:D],
                        in_=ap[src][rc * 128:(rc + 1) * 128, :]), (dW[sl], 16))
                    W("vector", dW[sl])
                    W("vector", gW[sl])
                    E("vector", lambda sl=sl: e.tensor_copy(
                        whi[sl][:, :], win[sl][:, 0:D]), (sV, 1))
                    W("vector", sV)
                    E("vector", lambda sl=sl: e.tensor_sub(
                        wlo[sl][:, :], win[sl][:, 0:D], whi[sl][:, :]), (sV, 1))
                    winck[sl] = sV.c
                    W("gpsimd", sV)
                    if idx is None:
                        dsth, dstl = ap[dh_][rc], ap[dl_][rc]
                    else:
                        dsth, dstl = ap[dh_][idx, rc], ap[dl_][idx, rc]
                    E("gpsimd", lambda d=dsth, sl=sl: e.dma_start(
                        out=d[:, :], in_=whi[sl][:, :]), (gW[sl], 16))
                    E("gpsimd", lambda d=dstl, sl=sl: e.dma_start(
                        out=d[:, :], in_=wlo[sl][:, :]), (gW[sl], 16))

            for (src, dst, wsn, NR, NW) in [
                    ("wgT", "wscr_g", "wsg", DC, I),
                    ("wuT", "wscr_u", "wsu", DC, I),
                    ("wdT", "wscr_d", "wsd", IC, D)]:
                for rc in range(NR):
                    sl = rc % 2
                    W("sync", sV, winck[sl])
                    E("sync", lambda src=src, rc=rc, sl=sl, NW=NW: e.dma_start(
                        out=win[sl][:, 0:NW],
                        in_=ap[src][rc * 128:(rc + 1) * 128, :]), (dW[sl], 16))
                    W("vector", dW[sl])
                    E("vector", lambda rc=rc, sl=sl, NW=NW: e.tensor_reduce(
                        wms[:, rc:rc + 1], win[sl][:, 0:NW], axis=AX.X,
                        op=Op.add, apply_absolute_value=True), (sV, 1))
                    winck[sl] = sV.c
                W("vector", sV)
                E("vector", lambda NR=NR: e.tensor_reduce(
                    wsc[:, 0:1], wms[:, 0:NR], axis=AX.X, op=Op.add), (sV, 1))
                W("tensor", sV)
                W("tensor", dC)
                E("tensor", lambda: e.transpose(
                    ps[0:1, 6, 0:128], wsc[:, 0:1], g_["id32"][:, :]), (sT, 1))
                W("vector", sT)
                E("vector", lambda: e.tensor_reduce(
                    wsc[0:1, 1:2], ps[0:1, 6, 0:128], axis=AX.X, op=Op.add), (sV, 1))
                W("vector", sV)
                E("vector", lambda NR=NR, NW=NW: e.tensor_scalar(
                    wsc[0:1, 2:3], wsc[0:1, 1:2], 1.0 / (NR * 128 * NW), EPS,
                    op0=Op.mult, op1=Op.max), (sV, 1))
                W("tensor", sV)
                E("tensor", lambda: e.matmul(
                    ps[:, 6, 0:1], g_["ones1"][0:1, :], wsc[0:1, 2:3],
                    start=True, stop=True), (sT, 1))
                W("scalar", sT)
                E("scalar", lambda wsn=wsn: e.activation(
                    g_[wsn][:, 0:1], ps[:, 6, 0:1], AF.Copy, bias=0.0,
                    scale=1.0), (sA, 1))
                W("vector", sA)
                E("vector", lambda wsn=wsn: e.tensor_scalar(
                    whn[:, 0:1], g_[wsn][:, 0:1], 0.5, None, op0=Op.mult), (sV, 1))
                W("vector", sV)
                E("vector", lambda: e.tensor_scalar(
                    whn[:, 1:2], whn[:, 0:1], -1.0, None, op0=Op.mult), (sV, 1))
                for rc in range(NR):
                    sl = rc % 2
                    W("sync", sV, winck[sl])
                    E("sync", lambda src=src, rc=rc, sl=sl, NW=NW: e.dma_start(
                        out=win[sl][:, 0:NW],
                        in_=ap[src][rc * 128:(rc + 1) * 128, :]), (dW[sl], 16))
                    W("vector", dW[sl])
                    W("vector", sV)
                    E("vector", lambda sl=sl, NW=NW: e.tensor_scalar(
                        wle[:, 0:NW], win[sl][:, 0:NW], whn[:, 1:2], None,
                        op0=Op.is_le), (sV, 1))
                    W("vector", sV)
                    W("vector", gW[sl])
                    E("vector", lambda sl=sl, NW=NW: e.scalar_tensor_tensor(
                        wt8[sl][:, 0:NW], win[sl][:, 0:NW], whn[:, 0:1],
                        wle[:, 0:NW], op0=Op.is_ge, op1=Op.subtract), (sV, 1))
                    winck[sl] = sV.c
                    W("gpsimd", sV)
                    E("gpsimd", lambda dst=dst, rc=rc, sl=sl, NW=NW: e.dma_start(
                        out=ap[dst][rc], in_=wt8[sl][:, 0:NW]), (gW[sl], 16))

    if '0' in stages:
        run_stage(ctx, s0)

    def make_sA(own):
        xsrc = "xo" if own else "xf"
        tab = "tabq" if own else "tabk"
        ait = AIT2 if own else AIT
        niter = (OT if own else T) // ait
        tpi = ait // 128

        def sA_(ctx, e, eng):
            E, W, B = mk_EW(e, eng, ctx.all_sems, ctx.own_map[eng])
            B()
            W("tensor", dC)
            if True:
                ax = ctx.balloc2("a_x", [128, 2, D], F32)
                axn = ctx.balloc("a_xn", [128, D], F32)
                ast = ctx.balloc("a_st", [128, 8], F32)
                axh = ctx.balloc("a_xh", [128, DC * ait], BF16)
                axl = ctx.balloc("a_xl", [128, DC * ait], BF16)
                awh = ctx.balloc2("a_wh", [128, 2, DC * WBLK], BF16)
                awl = ctx.balloc2("a_wl", [128, 2, DC * WBLK], BF16)
                act_ = ctx.balloc("a_ct", [128, tpi * 64], F32)
                astb = ctx.balloc("a_stb", [128, tpi * 64], F32)
                ajnk = ctx.balloc("a_jnk", [128, max(D, tpi * 64)], F32)
                aro = ctx.balloc("a_ro", [128, 128], F32)
                arh = ctx.balloc("a_rh", [128, 128], BF16)
                arl = ctx.balloc("a_rl", [128, 128], BF16)
                ake = ctx.balloc2("a_ke", [128, 2, 256], BF16)
                ave = ctx.balloc2("a_ve", [128, 2, WBLK], F32)
                anrm = ctx.balloc("a_nrm", [128, tpi * H], F32)
                wtick = {0: 0, 1: 0}
                pb_tick = {}
                ketick = {0: 0, 1: 0}
                vetick = {0: 0, 1: 0}
                abct = {"ke": 0, "ve": 0}

                def rope_head(bank, tc, gt, hglob):
                    x0 = ps[:, bank, (hglob % HPB) * 128:(hglob % HPB) * 128 + 64]
                    x1 = ps[:, bank, (hglob % HPB) * 128 + 64:(hglob % HPB) * 128 + 128]
                    ct = act_[:, tc * 64:(tc + 1) * 64]
                    st_ = astb[:, tc * 64:(tc + 1) * 64]
                    W("vector", sT)
                    E("vector", lambda: e.tensor_mul(ajnk[:, 0:64], x0, ct), (sV, 1))
                    E("vector", lambda: e.tensor_mul(ajnk[:, 64:128], x1, st_), (sV, 1))
                    E("vector", lambda: e.tensor_mul(ajnk[:, 128:192], x0, st_), (sV, 1))
                    E("vector", lambda: e.tensor_mul(ajnk[:, 192:256], x1, ct), (sV, 1))
                    W("vector", sV)
                    E("vector", lambda: e.tensor_sub(
                        aro[:, 0:64], ajnk[:, 0:64], ajnk[:, 64:128]), (sV, 1))
                    E("vector", lambda: e.tensor_add(
                        aro[:, 64:128], ajnk[:, 128:192], ajnk[:, 192:256]), (sV, 1))
                    W("vector", sV)
                    if own:
                        ncol = g_["qn2"][:, gt * H + hglob: gt * H + hglob + 1]
                    else:
                        ncol = g_["kn2"][:, hglob * TT + gt: hglob * TT + gt + 1]
                    E("vector", lambda ncol=ncol: e.scalar_tensor_tensor(
                        ajnk[:, 0:128], aro[:, :], 1.0, aro[:, :],
                        op0=Op.mult, op1=Op.mult, accum_out=ncol), (sV, 1))
                    kesl = abct["ke"] % 2
                    abct["ke"] += 1
                    W("vector", gK[kesl])
                    E("vector", lambda: e.tensor_copy(
                        ake[kesl][:, 0:128], aro[:, :]), (sV, 1))
                    W("vector", sV)
                    E("vector", lambda: e.tensor_sub(
                        ake[kesl][:, 128:256], aro[:, :], ake[kesl][:, 0:128]),
                        (sV, 1))
                    W("gpsimd", sV)
                    kdst = "qscr" if own else "kscr"
                    E("gpsimd", lambda: e.dma_start(
                        out=ap[kdst + "_h"][gt * 128:(gt + 1) * 128,
                                            hglob * 128:(hglob + 1) * 128],
                        in_=ake[kesl][:, 0:128]), (gK[kesl], 16))
                    E("gpsimd", lambda: e.dma_start(
                        out=ap[kdst + "_l"][gt * 128:(gt + 1) * 128,
                                            hglob * 128:(hglob + 1) * 128],
                        in_=ake[kesl][:, 128:256]), (gK[kesl], 16))

                def proj_tile(it, tc, blk, sl, isrope, xtick):
                    bank = 4 + (tc % 2)
                    gt = it * tpi + tc
                    W("tensor", dW[sl])
                    W("tensor", sV, xtick[tc])
                    W("tensor", sV, pb_tick.get((bank, "v"), 0))
                    W("tensor", sA, pb_tick.get((bank, "a"), 0))
                    k, n3 = 0, 3 * DC
                    for dc in range(DC):
                        c0 = dc * ait + tc * 128
                        xh_t = axh[:, c0:c0 + 128]
                        xl_t = axl[:, c0:c0 + 128]
                        wh_t = awh[sl][:, dc * WBLK:(dc + 1) * WBLK]
                        wl_t = awl[sl][:, dc * WBLK:(dc + 1) * WBLK]
                        for (sx, mv) in ((xh_t, wh_t), (xh_t, wl_t), (xl_t, wh_t)):
                            k += 1
                            E("tensor", lambda sx=sx, mv=mv, bank=bank, k=k, n3=n3:
                              e.matmul(ps[:, bank, 0:WBLK], sx, mv, start=(k == 1),
                                       stop=(k == n3)),
                                *([(sT, 1)] if k == n3 else []))
                    if isrope:
                        for hh in range(HPB):
                            rope_head(bank, tc, gt, blk * HPB + hh)
                        pb_tick[(bank, "v")] = sV.c
                    else:
                        vsl = abct["ve"] % 2
                        abct["ve"] += 1
                        W("scalar", sT)
                        W("scalar", gV[vsl])
                        E("scalar", lambda: e.activation(
                            ave[vsl][:, :], ps[:, bank, 0:WBLK], AF.Copy, bias=0.0,
                            scale=1.0), (sA, 1))
                        pb_tick[(bank, "a")] = sA.c
                        W("gpsimd", sA)
                        E("gpsimd", lambda: e.dma_start(
                            out=ap["vscr"][gt * 128:(gt + 1) * 128,
            blk * WBLK:(blk + 1) * WBLK],
                            in_=ave[vsl][:, :]), (gV[vsl], 16))

                for it in range(niter):
                    W("vector", sT)
                    W("vector", gX)
                    W("scalar", sT)
                    W("scalar", gX)
                    W("sync", sV)
                    E("sync", lambda it=it: e.dma_start(
                        out=act_[:, :],
                        in_=ap[tab][0][:, it * tpi * 64:(it + 1) * tpi * 64]),
                        (dT, 16))
                    E("sync", lambda it=it: e.dma_start(
                        out=ajnk[:, 0:tpi * 64],
                        in_=ap[tab][1][:, it * tpi * 64:(it + 1) * tpi * 64]),
                        (dT, 16))
                    W("vector", dT)
                    E("vector", lambda: e.tensor_add(
                        act_[:, :], act_[:, :], ajnk[:, 0:tpi * 64]), (sV, 1))
                    E("sync", lambda it=it: e.dma_start(
                        out=astb[:, :],
                        in_=ap[tab][2][:, it * tpi * 64:(it + 1) * tpi * 64]),
                        (dT2, 16))
                    W("sync", sV)
                    E("sync", lambda it=it: e.dma_start(
                        out=ajnk[:, 0:tpi * 64],
                        in_=ap[tab][3][:, it * tpi * 64:(it + 1) * tpi * 64]),
                        (dT2, 16))
                    W("vector", dT2)
                    E("vector", lambda: e.tensor_add(
                        astb[:, :], astb[:, :], ajnk[:, 0:tpi * 64]), (sV, 1))

                    import os as _os
                    _tpr = int(_os.environ.get("KLN", tpi) if _os.environ.get("KLN", "") != "off" else 0)
                    xtick = []
                    for tc in range(_tpr):
                        sl = tc % 2
                        if tc >= 2:
                            W("sync", sV, xtick[tc - 2])
                        E("sync", lambda it=it, tc=tc, sl=sl: e.dma_start(
                            out=ax[sl][:, :],
                            in_=ap[xsrc][(it * tpi + tc) * 128:
                                         (it * tpi + tc + 1) * 128, :]),
                            (dX[sl], 16))
                        _kcut = int(_os.environ.get("KCUT", 5))
                        if _kcut < 2:
                            xtick.append(sV.c)
                            continue
                        W("vector", dX[sl])
                        E("vector", lambda sl=sl: e.tensor_scalar(
                            ajnk[:, 0:D], ax[sl][:, :], 1.0, 0.0, op0=Op.mult,
                            op1=Op.add, accum_out=ast[:, 0:1]), (sV, 1))
                        W("vector", sV)
                        E("vector", lambda sl=sl: e.scalar_tensor_tensor(
                            ajnk[:, 0:D], ax[sl][:, :], 1.0, ax[sl][:, :],
                            op0=Op.mult, op1=Op.mult,
                            accum_out=ast[:, 1:2]), (sV, 1))
                        if _kcut < 3:
                            xtick.append(sV.c)
                            continue
                        W("vector", sV)
                        E("vector", lambda: e.tensor_scalar(
                            ast[:, 2:3], ast[:, 0:1], 1.0 / D, None, op0=Op.mult),
                            (sV, 1))
                        W("vector", sV)
                        E("vector", lambda: e.tensor_scalar(
                            ast[:, 3:4], ast[:, 1:2], 1.0 / D, None, op0=Op.mult),
                            (sV, 1))
                        W("vector", sV)
                        E("vector", lambda: e.scalar_tensor_tensor(
                            ast[:, 4:5], ast[:, 2:3], ast[:, 2:3], ast[:, 3:4],
                            op0=Op.mult, op1=Op.subtract), (sV, 1))
                        W("scalar", sV)
                        E("scalar", lambda: e.activation(
                            ast[:, 5:6], ast[:, 4:5], AF.Sqrt,
                            bias=g_["kc0"][:, 1:2], scale=-1.0), (sA, 1))
                        W("vector", sA)
                        E("vector", lambda: e.reciprocal(ast[:, 6:7], ast[:, 5:6]),
                          (sV, 1))
                        W("vector", sV)
                        E("vector", lambda sl=sl: e.tensor_scalar(
                            axn[:, :], ax[sl][:, :], ast[:, 2:3], ast[:, 6:7],
                            op0=Op.subtract, op1=Op.mult), (sV, 1))
                        if _kcut < 4:
                            xtick.append(sV.c)
                            continue
                        nb = (DC + 3) // 4
                        for bk in range(nb):
                            cnt = min(4, DC - bk * 4)
                            W("tensor", sV)
                            W("tensor", sA)
                            for q in range(cnt):
                                dc = bk * 4 + q
                                E("tensor", lambda dc=dc, bk=bk, q=q: e.transpose(
                                    ps[:, bk % 4, q * 128:(q + 1) * 128],
                                    axn[:, dc * 128:(dc + 1) * 128],
                                    g_["id32"][:, :]),
                                    *([(sT, 1)] if q == cnt - 1 else []))
                            for q in range(cnt):
                                if _kcut < 5:
                                    continue
                                dc = bk * 4 + q
                                col = dc * ait + tc * 128
                                if _os.environ.get("KEVAC", "dve") == "dve":
                                    W("vector", sT)
                                    E("vector", lambda bk=bk, q=q, col=col:
                                      e.tensor_copy(
                                          axh[:, col:col + 128],
                                          ps[:, bk % 4, q * 128:(q + 1) * 128]),
                                        (sA, 1))
                                else:
                                    W("scalar", sT)
                                    E("scalar", lambda bk=bk, q=q, col=col: e.activation(
                                        axh[:, col:col + 128],
                                        ps[:, bk % 4, q * 128:(q + 1) * 128],
                                        AF.Copy, bias=0.0, scale=1.0), (sA, 1))
                                W("vector", sA)
                                E("vector", lambda bk=bk, q=q, col=col:
                                  e.scalar_tensor_tensor(
                                      axl[:, col:col + 128],
                                      ps[:, bk % 4, q * 128:(q + 1) * 128], 0.0,
                                      axh[:, col:col + 128], op0=Op.add,
                                      op1=Op.subtract), (sV, 1))
                        xtick.append(sV.c)
                    if own:
                        W("gpsimd", sV)
                        for dc in range(DC):
                            E("gpsimd", lambda dc=dc, it=it: e.dma_start(
                                out=ap["xnscr_h"][dc][:, it * ait:(it + 1) * ait],
                                in_=axh[:, dc * ait:(dc + 1) * ait]), (gX, 16))
                            E("gpsimd", lambda dc=dc, it=it: e.dma_start(
                                out=ap["xnscr_l"][dc][:, it * ait:(it + 1) * ait],
                                in_=axl[:, dc * ait:(dc + 1) * ait]), (gX, 16))

                    projs = [(0, True)] if own else [(1, True), (2, False)]
                    import os
                    _pj = os.environ.get("KPROJ", "")
                    if _pj == "none":
                        projs = []
                    elif _pj == "k" and not own:
                        projs = [(1, True)]
                    elif _pj == "v" and not own:
                        projs = [(2, False)]
                    for (widx, isrope) in projs:
                        for blk in range(DBLK):
                            gwi = (it * 3 + widx) * DBLK + blk
                            sl = gwi % 2
                            W("sync", sT, wtick[sl])
                            for part, buf in (("wt_h", awh), ("wt_l", awl)):
                                E("sync", lambda part=part, widx=widx, blk=blk,
                                  buf=buf, sl=sl: e.dma_start(
                                      out=buf[sl][:, :].rearrange(
            "p (c j) -> p c j", c=DC),
                                      in_=ap[part][widx][:, :,
            blk * WBLK:(blk + 1) * WBLK]
                                      .rearrange("c p j -> p c j")), (dW[sl], 16))
                            for tc in range(tpi):
                                proj_tile(it, tc, blk, sl, isrope, xtick)
                            wtick[sl] = sT.c
        return sA_

    if 'A1' in stages:
        run_stage(ctx, make_sA(False))
    if 'A2' in stages:
        run_stage(ctx, make_sA(True))

    def sB(ctx, e, eng):
        E, W, B = mk_EW(e, eng, ctx.all_sems, ctx.own_map[eng])
        B()
        W("tensor", dC)
        NGKB = NG * KB
        if True:
            bmk = ctx.balloc("b_mk", [128, NT * KB], BF16)
            bqh = ctx.balloc("b_qh", [128, H * 256], BF16)
            bql = ctx.balloc("b_ql", [128, H * 256], BF16)
            bkh = ctx.balloc2("b_kh", [128, 2, NGKB], BF16)
            bkl = ctx.balloc2("b_kl", [128, 2, NGKB], BF16)
            bv = ctx.balloc2("b_v", [128, 2, NGKB], F32)
            bp = ctx.balloc("b_p", [128, 4 * KB], F32)
            bpt = ctx.balloc("b_pt", [128, 4 * KB], F32)
            blt = ctx.balloc("b_lt", [128, 2 * (NG + 2)], F32)
            bsc = ctx.balloc("b_sc", [128, 4], F32)
            blb = ctx.balloc("b_lb", [128, 256], F32)
            bhb = ctx.balloc2("b_hb", [128, 2, 256], F32)
            bhe = ctx.balloc2("b_he", [128, 2, 256], BF16)
            for jt in range(NT):
                E("sync", lambda jt=jt: e.dma_start(
                    out=bmk[:, jt * KB:(jt + 1) * KB], in_=ap["masks"][jt]),
                    (dC, 16))
            W("vector", dC)
            W("sync", gK[0])
            W("sync", gK[1])
            W("sync", gV[0])
            W("sync", gV[1])
            W("sync", gX)
            for h in range(H):
                E("vector", lambda h=h: e.tensor_reduce(
                    g_["kmax"][:, h:h + 1],
                    g_["kn2"][:, h * TT:(h + 1) * TT], axis=AX.X, op=Op.max),
                    (sV, 1))
            W("tensor", sV)
            E("tensor", lambda: e.transpose(
                ps[0:H, 6, 0:128], g_["kmax"][:, 0:H], g_["id32"][:, :]), (sT, 1))
            W("vector", sT)
            E("vector", lambda: e.tensor_reduce(
                bsc[0:H, 0:1], ps[0:H, 6, 0:128], axis=AX.X, op=Op.max), (sV, 1))
            W("scalar", sV)
            E("scalar", lambda: e.activation(
                bsc[0:H, 1:2], bsc[0:H, 0:1], AF.Sqrt, bias=g_["kc0"][0:H, 0:1],
                scale=1.0), (sA, 1))
            W("vector", sA)
            E("vector", lambda: e.tensor_scalar(
                bsc[0:H, 2:3], bsc[0:H, 1:2], -1.02, None, op0=Op.mult), (sV, 1))
            W("tensor", sV)
            E("tensor", lambda: e.transpose(
                ps[0:1, 6, 0:H], bsc[0:H, 2:3], g_["id32"][0:H, 0:H]), (sT, 1))
            W("scalar", sT)
            E("scalar", lambda: e.activation(
                bsc[0:1, 0:H] if H <= 4 else blb[0:1, 0:H],
                ps[0:1, 6, 0:H], AF.Copy, bias=0.0, scale=1.0), (sA, 1))
            W("tensor", sA)
            E("tensor", lambda: e.matmul(
                ps[:, 6, 0:H], g_["ones1"][0:1, :],
                bsc[0:1, 0:H] if H <= 4 else blb[0:1, 0:H],
                start=True, stop=True), (sT, 1))
            W("scalar", sT)
            E("scalar", lambda: e.activation(
                g_["kmax"][:, 0:H], ps[:, 6, 0:H], AF.Copy, bias=0.0,
                scale=1.0), (sA, 1))
            E("scalar", lambda: e.activation(
                g_["cbias"][:, 0:NT * H], g_["qn2"][:, 0:NT * H], AF.Sqrt,
                bias=g_["kc0"][:, 0:1], scale=1.0), (sA, 1))
            W("vector", sA)
            for jt in range(NT):
                E("vector", lambda jt=jt: e.tensor_mul(
                    g_["cbias"][:, jt * H:(jt + 1) * H],
                    g_["cbias"][:, jt * H:(jt + 1) * H], g_["kmax"][:, 0:H]),
                    (sV, 1))

            qtick = 0
            kvtick = {0: 0, 1: 0}
            avfree = 0
            pt_tick = {5: 0, 6: 0}
            ep_tick = 0
            hetick = {0: 0, 1: 0}
            ghtick = {0: 0, 1: 0}
            hect = 0
            hprev = 0
            for g in range(NG):
                kb = kb_of_group(g)
                kbK = kb * KB
                nkc = KB // 128
                W("sync", sT, qtick)
                for h in range(H):
                    E("sync", lambda g=g, h=h: e.dma_start(
                        out=bqh[:, h * 256:(h + 1) * 256],
                        in_=ap["qscr_h"][g * 256:(g + 1) * 256,
                                         h * 128:(h + 1) * 128],
                        transpose=True), (dQ, 16))
                    E("sync", lambda g=g, h=h: e.dma_start(
                        out=bql[:, h * 256:(h + 1) * 256],
                        in_=ap["qscr_l"][g * 256:(g + 1) * 256,
                                         h * 128:(h + 1) * 128],
                        transpose=True), (dQ, 16))
                for h in range(H):
                    sl = h % 2
                    if g * H + h >= 2:
                        W("sync", sT, kvtick[sl])
                    E("sync", lambda h=h, kbK=kbK, sl=sl: e.dma_start(
                        out=bkh[sl][:, 0:kbK],
                        in_=ap["kscr_h"][0:kbK, h * 128:(h + 1) * 128],
                        transpose=True), (dK[sl], 16))
                    E("sync", lambda h=h, kbK=kbK, sl=sl: e.dma_start(
                        out=bkl[sl][:, 0:kbK],
                        in_=ap["kscr_l"][0:kbK, h * 128:(h + 1) * 128],
                        transpose=True), (dK[sl], 16))
                    nvt = kbK // 128
                    for v0 in range(0, nvt, 8):
                        nv = min(8, nvt - v0)
                        E("sync", lambda h=h, v0=v0, nv=nv, sl=sl: e.dma_start(
                            out=bv[sl][:, v0 * 128:(v0 + nv) * 128].rearrange(
                                "p (n j) -> p n j", n=nv),
                            in_=ap["vscr"][v0 * 128:(v0 + nv) * 128,
                                           h * 128:(h + 1) * 128].rearrange(
                                "(n p) j -> p n j", p=128)), (dK[sl], 16))
                    qld = (dQ.c, dK[sl].c)
                    exp_tick = {}
                    ptd = {}

                    def S(kbi, h=h, sl=sl, g=g, qld=qld, kb=kb, hp=hprev):
                        for qt in range(2):
                            bnk = qt * 2 + (kbi % 2)
                            jt = 2 * g + qt
                            W("tensor", dQ, qld[0])
                            W("tensor", dK[sl], qld[1])
                            W("tensor", sA,
                              exp_tick[kbi - 2] if kbi >= 2 else hp)
                            W("tensor", sV)
                            qh_t = bqh[:, h * 256 + qt * 128:
                                       h * 256 + qt * 128 + 128]
                            ql_t = bql[:, h * 256 + qt * 128:
                                       h * 256 + qt * 128 + 128]
                            kh_t = bkh[sl][:, kbi * KB:(kbi + 1) * KB]
                            kl_t = bkl[sl][:, kbi * KB:(kbi + 1) * KB]
                            k = 0
                            for (sx, mv) in ((qh_t, kh_t), (qh_t, kl_t),
                                             (ql_t, kh_t)):
                                k += 1
                                E("tensor", lambda sx=sx, mv=mv, bnk=bnk, k=k:
                                  e.matmul(ps[:, bnk, 0:KB], sx, mv,
                                           start=(k == 1), stop=(k == 3)),
                                    *([(sT, 1)] if k == 3 else []))
                            if kbi == kb - 1:
                                W("vector", sT)
                                E("vector", lambda bnk=bnk, jt=jt: e.tensor_add(
                                    ps[:, bnk, 0:KB], ps[:, bnk, 0:KB],
                                    bmk[:, jt * KB:(jt + 1) * KB]), (sV, 1))
                            W("scalar", sT)
                            W("scalar", sV)
                            E("scalar", lambda bnk=bnk, qt=qt, jt=jt, kbi=kbi:
                              e.activation(
                                  bp[:, kbi % 2 * 2 * KB + qt * KB:
                                     kbi % 2 * 2 * KB + (qt + 1) * KB],
                                  ps[:, bnk, 0:KB],
                                  AF.Exp,
                                  bias=g_["cbias"][:, jt * H + h:jt * H + h + 1],
                                  scale=1.0,
                                  accum_out=blt[:, qt * (NG + 2) + kbi + 1:
                                                qt * (NG + 2) + kbi + 2]),
                                (sA, 1))
                        exp_tick[kbi] = sA.c

                    def T(kbi, h=h):
                        for kc in range(nkc):
                            tbank = 5 + (kc // 2) % 2
                            if kc % 2 == 0:
                                W("tensor", sA, pt_tick[tbank])
                                if kc == 0:
                                    W("tensor", sA, exp_tick[kbi])
                            for qt in range(2):
                                E("tensor", lambda kc=kc, qt=qt, tbank=tbank:
                                  e.transpose(
                                      ps[:, tbank, (kc % 2) * 256 + qt * 128:
                                         (kc % 2) * 256 + qt * 128 + 128],
                                      bp[:, kbi % 2 * 2 * KB + qt * KB
                                         + kc * 128:
                                         kbi % 2 * 2 * KB + qt * KB
                                         + (kc + 1) * 128],
                                      g_["id32"][:, :]),
                                    *([(sT, 1)] if qt == 1 else []))
                            if kc % 2 == 1 or kc == nkc - 1:
                                W("scalar", sT)
                                E("scalar", lambda kc=kc, kbi=kbi, tbank=tbank:
                                  e.activation(
                                      bpt[:, kbi % 2 * 2 * KB
                                          + (kc - kc % 2) * 256:
                                          kbi % 2 * 2 * KB + (kc + 1) * 256],
                                      ps[:, tbank, 0:(kc % 2 + 1) * 256],
                                      AF.Copy, bias=0.0, scale=1.0), (sA, 1))
                                pt_tick[tbank] = sA.c
                        ptd[kbi] = sA.c

                    def AV(kbi, h=h, sl=sl, kb=kb):
                        W("tensor", sA, ptd[kbi])
                        if kbi == 0:
                            W("tensor", sA, avfree)
                        for kc in range(nkc):
                            kt_glob = kbi * nkc + kc
                            vt = bv[sl][:, kt_glob * 128:(kt_glob + 1) * 128]
                            lastmm = (kbi == kb - 1 and kc == nkc - 1)
                            E("tensor", lambda vt=vt, kc=kc, kbi=kbi,
                              lastmm=lastmm: e.matmul(
                                  ps[:, 4, 0:256], vt,
                                  bpt[:, kbi % 2 * 2 * KB + kc * 256:
                                      kbi % 2 * 2 * KB + (kc + 1) * 256],
                                  start=(kbi == 0 and kc == 0), stop=lastmm),
                                *([(sT, 1)] if lastmm else []))

                    S(0)
                    if kb > 1:
                        S(1)
                    for kbi in range(kb):
                        T(kbi)
                        if kbi + 2 < kb:
                            S(kbi + 2)
                        if kbi > 0:
                            AV(kbi - 1)
                    AV(kb - 1)
                    kvtick[sl] = sT.c
                    W("vector", sA)
                    for qt in range(2):
                        jt = 2 * g + qt
                        E("vector", lambda qt=qt, kb=kb: e.tensor_reduce(
                            blt[:, qt * (NG + 2):qt * (NG + 2) + 1],
                            blt[:, qt * (NG + 2) + 1:qt * (NG + 2) + 1 + kb],
                            axis=AX.X, op=Op.add), (sV, 1))
                        W("vector", sV)
                        E("vector", lambda qt=qt, jt=jt, h=h: e.reciprocal(
                            g_["linv"][:, jt * H + h:jt * H + h + 1],
                            blt[:, qt * (NG + 2):qt * (NG + 2) + 1]), (sV, 1))
                    hesl = hect % 2
                    hect += 1
                    W("scalar", sT)
                    W("scalar", sV, hetick[hesl])
                    W("scalar", gH[hesl], ghtick[hesl])
                    E("scalar", lambda hesl=hesl: e.activation(
                        bhb[hesl][:, :], ps[:, 4, 0:256], AF.Copy, bias=0.0,
                        scale=1.0), (sA, 1))
                    avfree = sA.c
                    W("tensor", sV)
                    W("tensor", sA, avfree)
                    for qt in range(2):
                        E("tensor", lambda g=g, h=h, qt=qt: e.transpose(
                            ps[0:1, 4, qt * 128:qt * 128 + 128],
                            g_["linv"][:, :].rearrange(
                                "p (j hh) -> p j hh", hh=H)[
                                    :, 2 * g + qt:2 * g + qt + 1, h],
                            g_["id32"][:, :]), (sT, 1))
                    W("scalar", sT)
                    E("scalar", lambda: e.activation(
                        blb[0:1, 0:256], ps[0:1, 4, 0:256], AF.Copy, bias=0.0,
                        scale=1.0), (sA, 1))
                    W("tensor", sA)
                    E("tensor", lambda: e.matmul(
                        ps[:, 4, 0:256], g_["ones1"][0:1, :], blb[0:1, 0:256],
                        start=True, stop=True), (sT, 1))
                    W("scalar", sT)
                    E("scalar", lambda: e.activation(
                        blb[:, 0:256], ps[:, 4, 0:256], AF.Copy, bias=0.0,
                        scale=1.0), (sA, 1))
                    ep_tick = sA.c
                    avfree = sA.c
                    W("vector", sA, ep_tick)
                    E("vector", lambda hesl=hesl: e.tensor_mul(
                        bhb[hesl][:, :], bhb[hesl][:, :], blb[:, 0:256]), (sV, 1))
                    W("vector", sV)
                    W("vector", gH[hesl])
                    E("vector", lambda hesl=hesl: e.tensor_copy(
                        bhe[hesl][:, :], bhb[hesl][:, :]), (sV, 1))
                    W("vector", sV)
                    E("vector", lambda hesl=hesl: e.tensor_sub(
                        bhb[hesl][:, :], bhb[hesl][:, :], bhe[hesl][:, :]),
                        (sV, 1))
                    hetick[hesl] = sV.c
                    W("gpsimd", sV)
                    E("gpsimd", lambda h=h, g=g, hesl=hesl: e.dma_start(
                        out=ap["hscr_h"][h][:, g * 256:(g + 1) * 256],
                        in_=bhe[hesl][:, :]), (gH[hesl], 16))
                    E("gpsimd", lambda h=h, g=g, hesl=hesl: e.dma_start(
                        out=ap["hscr_l"][h][:, g * 256:(g + 1) * 256],
                        in_=bhb[hesl][:, :]), (gH[hesl], 16))
                    ghtick[hesl] = gH[hesl].c
                    hprev = sA.c
                qtick = sT.c

    if 'B' in stages:
        run_stage(ctx, sB)

    def sC(ctx, e, eng):
        E, W, B = mk_EW(e, eng, ctx.all_sems, ctx.own_map[eng])
        B()
        W("tensor", dC)
        TCB = min(512, OT)
        while OT % TCB:
            TCB -= 128
        NCB = OT // TCB
        TPC = TCB // 128
        if True:
            chh = ctx.balloc("c_hh", [128, H * TCB], BF16)
            chl = ctx.balloc("c_hl", [128, H * TCB], BF16)
            cwh = ctx.balloc2("c_wh", [128, 2, H * WBLK], BF16)
            cwl = ctx.balloc2("c_wl", [128, 2, H * WBLK], BF16)
            cx = ctx.balloc("c_x", [128, TPC * D], F32)
            co = ctx.balloc2("c_o", [128, 2, WBLK], F32)
            wtick = {0: 0, 1: 0}
            oct_ = 0
            W("sync", gH[0])
            W("sync", gH[1])
            for tcb in range(NCB):
                W("sync", sT)
                W("sync", sV)
                for h in range(H):
                    E("sync", lambda h=h, tcb=tcb: e.dma_start(
                        out=chh[:, h * TCB:(h + 1) * TCB],
                        in_=ap["hscr_h"][h][:, tcb * TCB:(tcb + 1) * TCB]),
                        (dX[0], 16))
                    E("sync", lambda h=h, tcb=tcb: e.dma_start(
                        out=chl[:, h * TCB:(h + 1) * TCB],
                        in_=ap["hscr_l"][h][:, tcb * TCB:(tcb + 1) * TCB]),
                        (dX[0], 16))
                for tc in range(TPC):
                    E("sync", lambda tcb=tcb, tc=tc: e.dma_start(
                        out=cx[:, tc * D:(tc + 1) * D],
                        in_=ap["xo"][(tcb * TPC + tc) * 128:
                                     (tcb * TPC + tc + 1) * 128, :]), (dX[0], 16))
                for dob in range(DBLK):
                    sl = (tcb * DBLK + dob) % 2
                    if tcb * DBLK + dob >= 2:
                        W("sync", sT, wtick[sl])
                    for part, buf in (("wot_h", cwh), ("wot_l", cwl)):
                        E("sync", lambda part=part, dob=dob, buf=buf, sl=sl:
                          e.dma_start(
                              out=buf[sl][:, :].rearrange("p (c j) -> p c j", c=H),
                              in_=ap[part][:, :, dob * WBLK:(dob + 1) * WBLK]
                              .rearrange("c p j -> p c j")), (dW[sl], 16))
                    for tc in range(TPC):
                        bank = tc % 2
                        osl = oct_ % 2
                        oct_ += 1
                        W("tensor", dW[sl])
                        W("tensor", dX[0])
                        W("tensor", sV)
                        k, n3 = 0, 3 * H
                        for dc in range(H):
                            hh_t = chh[:, dc * TCB + tc * 128: dc * TCB + tc * 128 + 128]
                            hl_t = chl[:, dc * TCB + tc * 128: dc * TCB + tc * 128 + 128]
                            wh_t = cwh[sl][:, dc * WBLK:(dc + 1) * WBLK]
                            wl_t = cwl[sl][:, dc * WBLK:(dc + 1) * WBLK]
                            for (sx, mv) in ((hh_t, wh_t), (hh_t, wl_t), (hl_t, wh_t)):
                                k += 1
                                E("tensor", lambda sx=sx, mv=mv, bank=bank, k=k,
                                  n3=n3: e.matmul(
                                      ps[:, bank, 0:WBLK], sx, mv, start=(k == 1),
                                      stop=(k == n3)),
                                    *([(sT, 1)] if k == n3 else []))
                        W("vector", sT)
                        W("vector", dX[0])
                        W("vector", gO[osl])
                        E("vector", lambda bank=bank, tc=tc, dob=dob, osl=osl:
                          e.tensor_add(
                              co[osl][:, :], ps[:, bank, 0:WBLK],
                              cx[:, tc * D + dob * WBLK: tc * D + (dob + 1) * WBLK]),
                            (sV, 1))
                        W("gpsimd", sV)
                        E("gpsimd", lambda tcb=tcb, tc=tc, dob=dob, osl=osl:
                          e.dma_start(
                              out=ap["oscr"][(tcb * TPC + tc) * 128:
            (tcb * TPC + tc + 1) * 128,
            dob * WBLK:(dob + 1) * WBLK],
                              in_=co[osl][:, :]), (gO[osl], 16))
                    wtick[sl] = sT.c

    if 'C' in stages:
        run_stage(ctx, sC)

    def sD(ctx, e, eng):
        E, W, B = mk_EW(e, eng, ctx.all_sems, ctx.own_map[eng])
        B()
        W("tensor", dC)
        OC = IC
        HQ = min(2048, max(I // 2, D))
        NQ2 = I // HQ
        LVL = [float(1 << k) for k in range(7, -1, -1)]
        CONST = []
        c = 0.0
        for t in LVL:
            CONST.append(-(t + c))
            c += t / 2.0
        CFIN = c
        if True:
            dxh = ctx.balloc("d_xh", [128, DC * FMT], BF16)
            dxl = ctx.balloc("d_xl", [128, DC * FMT], BF16)
            df1 = ctx.balloc("d_f1", [128, DC * FMT], F32)
            dsc2 = ctx.balloc("d_sc2", [128, HQ], F32)
            dtm = ctx.balloc("d_tm", [128, D], F32)
            dq1f = ctx.balloc2("d_q1f", [128, 2, max(DC * FMT, HQ)], BF16)
            dsc = ctx.balloc("d_dsc", [128, 24], F32)
            dldr = ctx.balloc("d_ldr", [128, 8], F32)
            dwg = ctx.balloc2("d_wg", [128, 2, DC * WBI], FP8)
            dwu = ctx.balloc2("d_wu", [128, 2, DC * WBI], FP8)
            dt1 = ctx.balloc("d_t1", [128, WBI], F32)
            dt2 = ctx.balloc("d_t2", [128, WBI], F32)
            dytm = ctx.balloc("d_ytm", [128, I], F32)
            dyfm = ctx.balloc("d_yfm", [128, IC * FMT], F32)
            dysh = ctx.balloc("d_ysh", [128, 2 * FMT], BF16)
            dysl = ctx.balloc("d_ysl", [128, 2 * FMT], BF16)
            dq2t = ctx.balloc("d_q2t", [128, I // 2], BF16)
            dq2f = ctx.balloc("d_q2f", [128, IC * FMT], BF16)
            dwd = ctx.balloc2("d_wd", [128, 2, (IC // 2) * WBLK], FP8)
            dos = ctx.balloc("d_os", [128, WBLK], F32)
            dou = ctx.balloc2("d_ou", [128, 1, WBLK], F32)

            for k in range(8):
                E("vector", lambda k=k: e.memset(dldr[:, k:k + 1], CONST[k]),
                  (sV, 1))

            def ladder_multi(specs, scvec):
                svt = {}
                sat = {}
                for ci, (zbuf, wid, bits, wcopy, outq) in enumerate(specs):
                    E("vector", lambda zbuf=zbuf, scvec=scvec: e.tensor_scalar(
                        zbuf, zbuf, scvec, -128.0, op0=Op.mult, op1=Op.max),
                        (sV, 1))
                    E("vector", lambda zbuf=zbuf: e.tensor_scalar(
                        zbuf, zbuf, 127.49, 128.5, op0=Op.min, op1=Op.add),
                        (sV, 1))
                    E("vector", lambda zbuf=zbuf, wcopy=wcopy: e.tensor_scalar(
                        wcopy, zbuf, 1.0, None, op0=Op.mult), (sV, 1))
                    svt[ci] = sV.c
                for k in range(8):
                    for ci, (zbuf, wid, bits, wcopy, outq) in enumerate(specs):
                        W("scalar", sV, svt[ci])
                        E("scalar", lambda zbuf=zbuf, bits=bits, k=k: e.activation(
                            bits, zbuf, AF.Sign, bias=dldr[:, k:k + 1],
                            scale=1.0), (sA, 1))
                        sat[ci] = sA.c
                        W("vector", sA, sat[ci])
                        E("vector", lambda zbuf=zbuf, bits=bits, k=k:
                          e.scalar_tensor_tensor(
                              zbuf, bits, -LVL[k] / 2.0, zbuf, op0=Op.mult,
                              op1=Op.add), (sV, 1))
                        svt[ci] = sV.c
                for ci, (zbuf, wid, bits, wcopy, outq) in enumerate(specs):
                    E("vector", lambda zbuf=zbuf, wcopy=wcopy: e.tensor_sub(
                        zbuf, wcopy, zbuf), (sV, 1))
                    E("vector", lambda zbuf=zbuf, outq=outq: e.tensor_scalar(
                        outq, zbuf, CFIN - 128.0, 127.0, op0=Op.add, op1=Op.min),
                        (sV, 1))

            xn_tick = 0
            df1_free = 0
            df1v_free = 0
            dq1f_ready = {0: 0, 1: 0}
            y_tick = {}
            bank_sV = {}

            def bankw(b):
                W("tensor", sA, y_tick.get(b, 0))
                W("tensor", sV, bank_sV.get(b, 0))
            gu_bank_tick = {}
            wg_tick = {0: 0, 1: 0}
            wd_tick = {0: 0, 1: 0}
            dq2f_tick = 0
            dos_tick = 0
            dou_tick = 0
            def EARLY(mi):
                nonlocal xn_tick, df1_free, df1v_free
                t0 = mi * FMT
                xsl = mi % 2
                db = xsl * 5
                W("sync", sT, xn_tick)
                for dc in range(DC):
                    E("sync", lambda dc=dc, t0=t0: e.dma_start(
                        out=dxh[:, dc * FMT:(dc + 1) * FMT],
                        in_=ap["xnscr_h"][dc][:, t0:t0 + FMT]), (dX[0], 16))
                    E("sync", lambda dc=dc, t0=t0: e.dma_start(
                        out=dxl[:, dc * FMT:(dc + 1) * FMT],
                        in_=ap["xnscr_l"][dc][:, t0:t0 + FMT]), (dX[0], 16))
                for dc in range(DC):
                    hb6 = dc % 2
                    W("tensor", dX[0])
                    W("tensor", sA, y_tick.get((6, 0), 0))
                    W("tensor", sA, y_tick.get((6, 1), 0))
                    W("tensor", sV, y_tick.get((6, "v"), 0))
                    E("tensor", lambda dc=dc, hb6=hb6: e.matmul(
                        ps[:, 6, hb6 * FMT:(hb6 + 1) * FMT], g_["hmat"][:, :],
                        dxh[:, dc * FMT:(dc + 1) * FMT], start=True, stop=False,
                        skip_group_check=True))
                    E("tensor", lambda dc=dc, hb6=hb6: e.matmul(
                        ps[:, 6, hb6 * FMT:(hb6 + 1) * FMT], g_["hmat"][:, :],
                        dxl[:, dc * FMT:(dc + 1) * FMT], start=False, stop=True,
                        skip_group_check=True), (sT, 1))
                    if dc == DC - 1:
                        xn_tick = sT.c
                    W("scalar", sT)
                    W("scalar", sT, df1_free)
                    W("scalar", sV, df1v_free)
                    E("scalar", lambda dc=dc, hb6=hb6: e.activation(
                        df1[:, dc * FMT:(dc + 1) * FMT],
                        ps[:, 6, hb6 * FMT:(hb6 + 1) * FMT],
                        AF.Copy, bias=0.0, scale=1.0), (sA, 1))
                    y_tick[(6, hb6)] = sA.c
                ap1 = df1[:, :].rearrange("p (c j) -> p c j", c=DC)
                np1 = DC // 2
                for lv in range(L1):
                    stp = 1 << lv
                    W("vector", sA if lv == 0 else sV)
                    E("vector", lambda stp=stp: e.tensor_copy(
                        dsc2[:, 0:np1 * FMT].rearrange("p (c j) -> p c j", c=np1),
                        _bsel(ap1, stp, 1)), (sV, 1))
                    E("vector", lambda stp=stp: e.tensor_sub(
                        _bsel(ap1, stp, 1), _bsel(ap1, stp, 0), _bsel(ap1, stp, 1)),
                        (sV, 1))
                    E("vector", lambda stp=stp: e.tensor_add(
                        _bsel(ap1, stp, 0), _bsel(ap1, stp, 0),
                        dsc2[:, 0:np1 * FMT].rearrange("p (c j) -> p c j", c=np1)),
                        (sV, 1))
                bf1_tick = sV.c
                for dc in range(DC):
                    q = dc % 4
                    if q == 0:
                        W("tensor", sV, bf1_tick)
                        W("tensor", sA, y_tick.get((6, 0), 0))
                        W("tensor", sA, y_tick.get((6, 1), 0))
                    E("tensor", lambda dc=dc, q=q: e.transpose(
                        ps[:, 6, q * 128:(q + 1) * 128],
                        df1[:, dc * FMT:(dc + 1) * FMT], g_["id32"][:, :]),
                        *([(sT, 1)] if (q == 3 or dc == DC - 1) else []))
                    if q == 3 or dc == DC - 1:
                        W("scalar", sT)
                        E("scalar", lambda dc=dc, q=q: e.activation(
                            dtm[:, (dc - q) * 128:(dc + 1) * 128],
                            ps[:, 6, 0:(q + 1) * 128], AF.Copy, bias=0.0,
                            scale=1.0), (sA, 1))
                        y_tick[(6, 0)] = sA.c
                        y_tick[(6, 1)] = sA.c
                W("vector", sA)
                E("vector", lambda db=db: e.tensor_reduce(
                    dsc[:, db + 0:db + 1], dtm[:, :], axis=AX.X, op=Op.max,
                    apply_absolute_value=True), (sV, 1))
                E("vector", lambda db=db: e.tensor_scalar(
                    dsc[:, db + 1:db + 2], dsc[:, db + 0:db + 1], C1, EPS, op0=Op.mult, op1=Op.max),
                    (sV, 1))
                E("vector", lambda db=db: e.tensor_scalar(
                    dsc[:, db + 1:db + 2], dsc[:, db + 1:db + 2], 1.0 / 127.0, None, op0=Op.mult),
                    (sV, 1))
                E("vector", lambda db=db: e.reciprocal(dsc[:, db + 2:db + 3], dsc[:, db + 1:db + 2]), (sV, 1))
                E("vector", lambda db=db: e.tensor_scalar(
                    dsc[:, db + 2:db + 3], dsc[:, db + 2:db + 3], C1, None, op0=Op.mult), (sV, 1))
                E("vector", lambda db=db: e.tensor_scalar(
                    dsc[:, db + 3:db + 4], dsc[:, db + 1:db + 2], g_["wsg"][:, 0:1], None, op0=Op.mult),
                    (sV, 1))
                E("vector", lambda db=db: e.tensor_scalar(
                    dsc[:, db + 4:db + 5], dsc[:, db + 1:db + 2], g_["wsu"][:, 0:1], None, op0=Op.mult),
                    (sV, 1))
                ladder_multi([(dtm[:, :], D, dq2t[:, 0:D], dsc2[:, 0:D],
                               df1[:, 0:D])], dsc[:, db + 2:db + 3])
                q1l_tick = sV.c
                for dc in range(DC):
                    q = dc % 4
                    if q == 0:
                        W("tensor", sV, q1l_tick)
                        W("tensor", sV, y_tick.get((6, "v"), 0))
                        W("tensor", sA, y_tick.get((6, 0), 0))
                        W("tensor", sA, y_tick.get((6, 1), 0))
                    E("tensor", lambda dc=dc, q=q: e.transpose(
                        ps[:, 6, q * 128:(q + 1) * 128],
                        df1[:, dc * 128:(dc + 1) * 128], g_["id32"][:, :]),
                        *([(sT, 1)] if (q == 3 or dc == DC - 1) else []))
                    if q == 3 or dc == DC - 1:
                        W("vector", sT)
                        E("vector", lambda dc=dc, q=q, xsl=xsl: e.tensor_copy(
                            dq1f[xsl][:, (dc - q) * FMT:(dc + 1) * FMT],
                            ps[:, 6, 0:(q + 1) * 128]), (sV, 1))
                        y_tick[(6, "v")] = sV.c
                dq1f_ready[xsl] = sV.c
                df1_free = sT.c
            def LATE(mi):
                nonlocal df1v_free, dq2f_tick, dos_tick
                t0 = mi * FMT
                xsl = mi % 2
                db = xsl * 5
                for ob in range(OBI):
                    wsl = (mi * OBI + ob) % 2
                    if mi * OBI + ob >= 2:
                        W("sync", sT, wg_tick[wsl])
                    E("sync", lambda ob=ob, wsl=wsl: e.dma_start(
                        out=dwg[wsl][:, :].rearrange("p (c j) -> p c j", c=DC),
                        in_=ap["wscr_g"][:, :, ob * WBI:(ob + 1) * WBI]
                        .rearrange("c p j -> p c j")), (dW[wsl], 16))
                    E("sync", lambda ob=ob, wsl=wsl: e.dma_start(
                        out=dwu[wsl][:, :].rearrange("p (c j) -> p c j", c=DC),
                        in_=ap["wscr_u"][:, :, ob * WBI:(ob + 1) * WBI]
                        .rearrange("c p j -> p c j")), (dW[wsl], 16))
                    gbank, ubank = 2, 3
                    for (wbuf, bank) in ((dwg, gbank), (dwu, ubank)):
                        W("tensor", dW[wsl])
                        W("tensor", sV, dq1f_ready[xsl])
                        bankw(bank)
                        k = 0
                        for dc in range(DC):
                            k += 1
                            E("tensor", lambda wbuf=wbuf, dc=dc, bank=bank, k=k,
                              wsl=wsl, xsl=xsl: e.matmul(
                                  ps[:, bank, 0:WBI],
                                  dq1f[xsl][:, dc * FMT:(dc + 1) * FMT],
                                  wbuf[wsl][:, dc * WBI:(dc + 1) * WBI],
                                  start=(k == 1), stop=(k == DC)),
                                *([(sT, 1)] if k == DC else []))
                    wg_tick[wsl] = sT.c
                    W("scalar", sT)
                    W("scalar", sV)
                    E("scalar", lambda gbank=gbank, db=db: e.activation(
                        dt1[:, :], ps[:, gbank, 0:WBI], AF.Silu,
                        bias=g_["kc0"][:, 0:1], scale=dsc[:, db + 3:db + 4]), (sA, 1))
                    E("scalar", lambda ubank=ubank, db=db: e.activation(
                        dt2[:, :], ps[:, ubank, 0:WBI], AF.Identity,
                        bias=g_["kc0"][:, 0:1], scale=dsc[:, db + 4:db + 5]), (sA, 1))
                    y_tick[gbank] = sA.c
                    y_tick[ubank] = sA.c
                    W("vector", sA)
                    if ob == 0:
                        W("vector", sT, y_tick.get("ytm", 0))
                    E("vector", lambda ob=ob: e.tensor_mul(
                        dytm[:, ob * WBI:(ob + 1) * WBI], dt1[:, :], dt2[:, :]),
                        (sV, 1))
                for ic in range(IC):
                    bank = 4 + ((ic // 4) % 2)
                    q = ic % 4
                    if q == 0:
                        W("tensor", sV)
                        bankw(bank)
                    E("tensor", lambda ic=ic, bank=bank, q=q: e.transpose(
                        ps[:, bank, q * 128:(q + 1) * 128],
                        dytm[:, ic * 128:(ic + 1) * 128], g_["id32"][:, :]),
                        *([(sT, 1)] if (q == 3 or ic == IC - 1) else []))
                    if q == 3 or ic == IC - 1:
                        if ic == IC - 1:
                            y_tick["ytm"] = sT.c
                        W("scalar", sT)
                        E("scalar", lambda ic=ic, bank=bank, q=q: e.activation(
                            dyfm[:, (ic - q) * FMT:(ic + 1) * FMT],
                            ps[:, bank, 0:(q + 1) * 128], AF.Copy, bias=0.0,
                            scale=1.0), (sA, 1))
                        y_tick[bank] = sA.c
                for c0 in range(0, IC, 2):
                    ng = min(2, IC - c0)
                    W("vector", sA)
                    E("vector", lambda c0=c0, ng=ng: e.tensor_copy(
                        dysh[:, 0:ng * FMT], dyfm[:, c0 * FMT:(c0 + ng) * FMT]),
                        (sV, 1))
                    E("vector", lambda c0=c0, ng=ng: e.tensor_sub(
                        dysl[:, 0:ng * FMT], dyfm[:, c0 * FMT:(c0 + ng) * FMT],
                        dysh[:, 0:ng * FMT]), (sV, 1))
                    for ci in range(ng):
                        ic = c0 + ci
                        bank = ic % 2
                        W("tensor", sV)
                        bankw(bank)
                        E("tensor", lambda ci=ci, bank=bank: e.matmul(
                            ps[:, bank, 0:FMT], g_["hmat"][:, :],
                            dysh[:, ci * FMT:(ci + 1) * FMT], start=True,
                            stop=False))
                        E("tensor", lambda ci=ci, bank=bank: e.matmul(
                            ps[:, bank, 0:FMT], g_["hmat"][:, :],
                            dysl[:, ci * FMT:(ci + 1) * FMT], start=False,
                            stop=True), (sT, 1))
                        W("scalar", sT)
                        E("scalar", lambda ic=ic, bank=bank: e.activation(
                            dyfm[:, ic * FMT:(ic + 1) * FMT], ps[:, bank, 0:FMT],
                            AF.Copy, bias=0.0, scale=1.0), (sA, 1))
                        y_tick[bank] = sA.c
                ap2 = dyfm[:, :].rearrange("p (c j) -> p c j", c=IC)
                np2 = IC // 2
                nh = (np2 * FMT + HQ - 1) // HQ
                per = np2 // nh
                for lv in range(L2):
                    stp = 1 << lv
                    ng2 = IC // (2 * stp)
                    if ng2 >= nh:
                        tg, tc_ = ng2 // nh, stp
                    else:
                        tg, tc_ = ng2, stp // nh

                    def tview():
                        return dsc2[:, 0:tg * tc_ * FMT].rearrange(
                            "p (g c j) -> p g c j", g=tg, c=tc_)
                    W("vector", sA if lv == 0 else sV)
                    for hb in range(nh):
                        E("vector", lambda stp=stp, hb=hb, tview=tview:
                          e.tensor_copy(tview(), _bsel2(ap2, stp, 1, hb, nh)),
                            (sV, 1))
                        E("vector", lambda stp=stp, hb=hb: e.tensor_sub(
                            _bsel2(ap2, stp, 1, hb, nh),
                            _bsel2(ap2, stp, 0, hb, nh),
                            _bsel2(ap2, stp, 1, hb, nh)), (sV, 1))
                        E("vector", lambda stp=stp, hb=hb, tview=tview:
                          e.tensor_add(_bsel2(ap2, stp, 0, hb, nh),
                                       _bsel2(ap2, stp, 0, hb, nh), tview()),
                            (sV, 1))
                for ic in range(IC):
                    bank = 4 + ((ic // 4) % 2)
                    q = ic % 4
                    if q == 0:
                        W("tensor", sV)
                        bankw(bank)
                    E("tensor", lambda ic=ic, bank=bank, q=q: e.transpose(
                        ps[:, bank, q * 128:(q + 1) * 128],
                        dyfm[:, ic * FMT:(ic + 1) * FMT], g_["id32"][:, :]),
                        *([(sT, 1)] if (q == 3 or ic == IC - 1) else []))
                    if q == 3 or ic == IC - 1:
                        W("scalar", sT)
                        W("scalar", sV)
                        E("scalar", lambda ic=ic, bank=bank, q=q: e.activation(
                            dytm[:, (ic - q) * 128:(ic + 1) * 128],
                            ps[:, bank, 0:(q + 1) * 128], AF.Copy, bias=0.0,
                            scale=1.0), (sA, 1))
                        y_tick[bank] = sA.c
                W("vector", sA)
                E("vector", lambda: e.tensor_reduce(
                    dsc[:, 12:13], dytm[:, :], axis=AX.X, op=Op.max,
                    apply_absolute_value=True), (sV, 1))
                E("vector", lambda: e.tensor_scalar(
                    dsc[:, 13:14], dsc[:, 12:13], C2, EPS, op0=Op.mult, op1=Op.max),
                    (sV, 1))
                E("vector", lambda: e.tensor_scalar(
                    dsc[:, 13:14], dsc[:, 13:14], 1.0 / 127.0, None, op0=Op.mult),
                    (sV, 1))
                E("vector", lambda: e.reciprocal(dsc[:, 12:13], dsc[:, 13:14]),
                  (sV, 1))
                E("vector", lambda: e.tensor_scalar(
                    dsc[:, 12:13], dsc[:, 12:13], C2, None, op0=Op.mult), (sV, 1))
                E("vector", lambda: e.tensor_scalar(
                    dsc[:, 13:14], dsc[:, 13:14], g_["wsd"][:, 0:1], None,
                    op0=Op.mult), (sV, 1))
                for hf in range(2):
                    nq = max(I // 2 // HQ, 1)
                    specs = []
                    for qd in range(nq):
                        o0 = hf * (I // 2) + qd * HQ
                        bslot = dq2t[:, ((qd + 1) % max(nq, 2)) * HQ:
                                     ((qd + 1) % max(nq, 2) + 1) * HQ] \
                            if nq > 1 else dq2t[:, 0:HQ]
                        specs.append((dytm[:, o0:o0 + HQ], HQ, bslot,
                                      dsc2[:, 0:HQ] if qd % 2 == 0
                                      else dyfm[:, 0:HQ],
                                      dq2t[:, qd * HQ:(qd + 1) * HQ]))
                    for p0 in range(0, nq, 2):
                        ladder_multi(specs[p0:p0 + 2], dsc[:, 12:13])
                    for ici in range(IC // 2):
                        ic = hf * (IC // 2) + ici
                        half = (ici // 4) % 2
                        q = ici % 4
                        if q == 0:
                            W("tensor", sV)
                            W("tensor", sA, gu_bank_tick.get(6, 0))
                        W("tensor", sA, gu_bank_tick.get(7, 0))
                        E("tensor", lambda ici=ici, half=half, q=q: e.transpose(
                            psb[:, half * 512 + q * 128: half * 512 + (q + 1) * 128],
                            dq2t[:, ici * 128:(ici + 1) * 128], g_["idbf"][:, :]),
                            *([(sT, 1)] if (q == 3 or ici == IC // 2 - 1) else []))
                        if q == 3 or ici == IC // 2 - 1:
                            W("scalar", sT)
                            W("scalar", sT, dq2f_tick)
                            for qq in range(q + 1):
                                icq = ic - q + qq
                                E("scalar", lambda icq=icq, half=half, qq=qq:
                                  e.activation(
                                      dq2f[:, icq * FMT:(icq + 1) * FMT],
                                      psb[:, half * 512 + qq * 128:
                                          half * 512 + (qq + 1) * 128], AF.Copy,
                                      bias=0.0, scale=1.0),
                                    *([(sA, 1)] if qq == q else []))
                            gu_bank_tick[6 + half] = sA.c
                for dob in range(DBLK):
                    W("sync", sV, dos_tick)
                    E("sync", lambda mi=mi, dob=dob: e.dma_start(
                        out=dos[:, :],
                        in_=ap["oscr"][mi * 128:(mi + 1) * 128,
                                       dob * WBLK:(dob + 1) * WBLK]), (dX[1], 16))
                    bank = dob % 2
                    for hw_ in range(2):
                        W("sync", sT, wd_tick[hw_])
                        E("sync", lambda dob=dob, hw_=hw_: e.dma_start(
                            out=dwd[hw_][:, :].rearrange(
                                "p (c j) -> p c j", c=IC // 2),
                            in_=ap["wscr_d"][hw_ * (IC // 2):(hw_ + 1) * (IC // 2),
                                             :, dob * WBLK:(dob + 1) * WBLK]
                            .rearrange("c p j -> p c j")), (dK[hw_], 16))
                    for hw_ in range(2):
                        W("tensor", dK[hw_])
                        if hw_ == 0:
                            W("tensor", sA)
                            bankw(bank)
                        k = 0
                        for ici in range(IC // 2):
                            ic = hw_ * (IC // 2) + ici
                            k += 1
                            E("tensor", lambda ic=ic, ici=ici, bank=bank, k=k,
                              hw_=hw_, xsl=xsl: e.matmul(
                                  ps[:, bank, 0:WBLK],
                                  dq2f[:, ic * FMT:(ic + 1) * FMT],
                                  dwd[hw_][:, ici * WBLK:(ici + 1) * WBLK],
                                  start=(hw_ == 0 and k == 1),
                                  stop=(hw_ == 1 and k == IC // 2)),
                                *([(sT, 1)] if (hw_ == 1 and k == IC // 2)
                                  else []))
                        wd_tick[hw_] = sT.c
                    if dob == DBLK - 1:
                        dq2f_tick = sT.c
                    W("vector", sT)
                    W("vector", dX[1])
                    W("vector", gU[0])
                    E("vector", lambda bank=bank: e.scalar_tensor_tensor(
                        dou[0][:, :], ps[:, bank, 0:WBLK], dsc[:, 13:14],
                        dos[:, :], op0=Op.mult, op1=Op.add), (sV, 1))
                    dos_tick = sV.c
                    bank_sV[bank] = sV.c
                    W("gpsimd", sV)
                    E("gpsimd", lambda mi=mi, dob=dob: e.dma_start(
                        out=ap["out"][mi * 128:(mi + 1) * 128,
                                      dob * WBLK:(dob + 1) * WBLK],
                        in_=dou[0][:, :]), (gU[0], 16))

            for step in range(NM + 1):
                if step < NM:
                    EARLY(step)
                if step > 0:
                    LATE(step - 1)
            W("gpsimd", gU[0])

    if 'D' in stages:
        run_stage(ctx, sD)

    stk.close()
    return nc



_NC_CACHE = {}


def kernel(**inputs):
    cfg = default_cfg()
    cd = derived(cfg)
    if "nc" not in _NC_CACHE:
        _NC_CACHE["nc"] = build_nc(cfg)
    nc = _NC_CACHE["nc"]
    in_maps = [host_core_inputs(cfg, inputs, c) for c in range(cfg["n_cores"])]
    res = run_bass_kernel_spmd(nc, in_maps, core_ids=list(range(cfg["n_cores"])))
    B, T, D = cfg["B"], cfg["T"], cfg["D"]
    out = np.empty((B, T, D), dtype=np.float32)
    for c in range(cfg["n_cores"]):
        b, p = c // cd["CPB"], c % cd["CPB"]
        o = np.asarray(res.results[c]["out"])
        for jt in range(cd["NT"]):
            out[b, (2 * jt + p) * 128:(2 * jt + p + 1) * 128, :] = \
                o[jt * 128:(jt + 1) * 128, :]
    return out
